# version 09
# baseline (speedup 1.0000x reference)
import numpy as np
import ml_dtypes
import concourse.bass as bass
import concourse.mybir as mybir
from concourse.bass_utils import run_bass_kernel_spmd

F32 = mybir.dt.float32
BF16 = mybir.dt.bfloat16
I32 = mybir.dt.int32
U8 = mybir.dt.uint8
AF = mybir.ActivationFunctionType
ALU = mybir.AluOpType
AX = mybir.AxisListType

D = 2048
NTOK = 2048
NH = 16
DH = 128
NI = 8192
NKT = D // 128
NIT = NI // 128
MC = 512
NMC = NTOK // MC
BLK = 256
NBLK = NTOK // BLK
ENGS = ("sync", "scalar", "vector", "tensor", "gpsimd")
ENFORCE_SAME_ENGINE = True


class Sem:
    __slots__ = ("h", "name", "val", "_cm")

    def __init__(self, nc, es, name):
        self.h = es.enter_context(nc.semaphore(name))
        self.name = name
        self.val = 0


class Emitter:
    def __init__(self, nc, es):
        self.nc = nc
        self.q = {e: [] for e in ENGS}
        self.lastw = {e: {} for e in ENGS}
        self.R = {}
        self.W = {}
        self.sems = {}
        self._dsems = {}
        self._es = es

    def sem(self, name):
        s = Sem(self.nc, self._es, name)
        self.sems[name] = s
        return s

    def _wait(self, eng, sem, val):
        if val <= 0:
            return
        if self.lastw[eng].get(sem.name, 0) >= val:
            return
        self.lastw[eng][sem.name] = val
        self.q[eng].append(lambda e, s=sem.h, v=val: e.wait_ge(s, v))

    def op(self, eng, build, sem, reads=(), writes=(), amt=1, extra_waits=()):
        for b in reads:
            m = self.R.get(b)
            if m is not None:
                s, v, peng = m
                if peng != eng or ENFORCE_SAME_ENGINE:
                    self._wait(eng, s, v)
        for b in writes:
            m = self.R.get(b)
            if m is not None:
                s, v, peng = m
                if peng != eng or ENFORCE_SAME_ENGINE:
                    self._wait(eng, s, v)
            for (s, v, peng) in self.W.get(b, {}).values():
                if peng != eng or ENFORCE_SAME_ENGINE:
                    self._wait(eng, s, v)
        for (s, v) in extra_waits:
            self._wait(eng, s, v)
        sem.val += amt
        v = sem.val
        self.q[eng].append(lambda e, b=build, s=sem.h, a=amt: b(e).then_inc(s, a))
        for b in writes:
            self.R[b] = (sem, v, eng)
            self.W[b] = {}
        for b in reads:
            self.W.setdefault(b, {})[sem.name] = (sem, v, eng)
        return v

    def dma(self, eng, out, in_, sem, reads=(), writes=(), tp=False, extra_waits=(),
            skey=None):
        if skey is None:
            skey = writes[0] if writes else (reads[0] if reads else None)
        if skey is not None:
            s = self._dsems.get(skey)
            if s is None:
                s = self.sem("d_" + skey)
                self._dsems[skey] = s
            sem = s
        return self.op(
            eng,
            lambda e, o=out, i=in_, t=tp: e.dma_start(out=o, in_=i, transpose=t),
            sem, reads=reads, writes=writes, amt=16, extra_waits=extra_waits)

    def barrier(self, bsem):
        for e in ENGS:
            for s in self.sems.values():
                if s is not bsem:
                    self._wait(e, s, s.val)
            bsem.val += 1
            self.q[e].append(lambda eng, s=bsem.h: eng.sem_inc(s, 1))
        tot = bsem.val
        for e in ENGS:
            self._wait(e, bsem, tot)

    def replay(self):
        with self.nc.Block() as block:
            @block.sync
            def _(e):
                for f in self.q["sync"]:
                    f(e)

            @block.scalar
            def _(e):
                for f in self.q["scalar"]:
                    f(e)

            @block.vector
            def _(e):
                for f in self.q["vector"]:
                    f(e)

            @block.tensor
            def _(e):
                for f in self.q["tensor"]:
                    f(e)

            @block.gpsimd
            def _(e):
                for f in self.q["gpsimd"]:
                    f(e)


class SBAlloc:

    def __init__(self, nc, total_bytes):
        self.nc = nc
        self.total = total_bytes
        self.items = []
        self.regions = []

    def alloc(self, name, shape, dtype, phases=(0, 3)):
        per_part = int(np.prod(shape[1:])) * mybir.dt.size(dtype)
        per_part = (per_part + 63) // 64 * 64
        p0, p1 = phases
        taken = sorted(
            (s, e) for (s, e, q0, q1) in self.regions if not (q1 < p0 or q0 > p1))
        addr = 0
        for (s, e) in taken:
            if addr + per_part <= s:
                break
            addr = max(addr, e)
        if addr + per_part > self.total:
            raise MemoryError(f"SBUF overflow: {name} needs {per_part}B at {addr}")
        h = self.nc.virtual_sbuf_tensor(name, list(shape), dtype)
        self.items.append((h, addr))
        self.regions.append((addr, addr + per_part, p0, p1))
        return h

    def finalize(self, es):
        backing = es.enter_context(self.nc.sbuf_tensor("arena", [128, self.total], U8))
        base = self.nc.lookup_mloc(backing).addr
        for h, addr in self.items:
            self.nc._tensor(h.name, h.shape, h.dtype, type="SB")
            mloc = self.nc.lookup_mloc(h)
            mloc.allocated = not self.nc.target_bir_lowering
            mloc.addr = base + addr
        return backing


def build_program(debug=False, stop_after=3, no_prep=False):
    from contextlib import ExitStack
    es = ExitStack()
    nc = bass.Bass("TRN2", target_bir_lowering=False)

    x_in = nc.dram_tensor("x", [NTOK, D], F32, kind="ExternalInput").ap()
    slope = nc.dram_tensor("slope", [NH], F32, kind="ExternalInput").ap()
    w_qkv = nc.dram_tensor("w_qkv", [3 * D, D], F32, kind="ExternalInput").ap()
    w_gate = nc.dram_tensor("w_gate", [D, D], F32, kind="ExternalInput").ap()
    w_out = nc.dram_tensor("w_out", [D, D], F32, kind="ExternalInput").ap()
    norm_w = nc.dram_tensor("norm_w", [D], F32, kind="ExternalInput").ap()
    gate_w = nc.dram_tensor("gate_w", [NI, D], I32, kind="ExternalInput").ap()
    gate_sc = nc.dram_tensor("gate_sc", [NI], F32, kind="ExternalInput").ap()
    up_w = nc.dram_tensor("up_w", [NI, D], I32, kind="ExternalInput").ap()
    up_sc = nc.dram_tensor("up_sc", [NI], F32, kind="ExternalInput").ap()
    down_w = nc.dram_tensor("down_w", [D, NI], I32, kind="ExternalInput").ap()
    down_sc = nc.dram_tensor("down_sc", [D], F32, kind="ExternalInput").ap()
    c_ident = nc.dram_tensor("c_ident", [128, 128], F32, kind="ExternalInput").ap()
    c_ones_bf = nc.dram_tensor("c_ones_bf", [128, 1], F32, kind="ExternalInput").ap()
    c_ones1 = nc.dram_tensor("c_ones1", [1, 128], F32, kind="ExternalInput").ap()
    c_qdidx = nc.dram_tensor("c_qdidx", [128, 256], F32, kind="ExternalInput").ap()
    c_diffp = nc.dram_tensor("c_diffp", [256, 256], F32, kind="ExternalInput").ap()
    c_maskp = nc.dram_tensor("c_maskp", [256, 256], F32, kind="ExternalInput").ap()
    c_kdidx = nc.dram_tensor("c_kdidx", [256, 1], F32, kind="ExternalInput").ap()
    c_misc = nc.dram_tensor("c_misc", [128, 4], F32, kind="ExternalInput").ap()
    out_t = nc.dram_tensor("out", [NTOK, D], F32, kind="ExternalOutput").ap()
    spk = "ExternalOutput" if debug else "Internal"
    sp_xn = nc.dram_tensor("sp_xn", [NKT, 128, NTOK], BF16, kind=spk).ap()
    if debug:
        dbg_qkv = nc.dram_tensor("dbg_qkv", [128, 6144], BF16, kind="ExternalOutput").ap()
        dbg_G = nc.dram_tensor("dbg_G", [128, 2048], BF16, kind="ExternalOutput").ap()
        dbg_ssq = nc.dram_tensor("dbg_ssq", [1, 2048], F32, kind="ExternalOutput").ap()
    sp_ga = nc.dram_tensor("sp_ga", [NKT, 128, NTOK], BF16, kind=spk).ap()
    sp_wg = nc.dram_tensor("sp_wg", [NIT, 128, NKT * 128], BF16, kind=spk).ap()
    sp_wu = nc.dram_tensor("sp_wu", [NIT, 128, NKT * 128], BF16, kind=spk).ap()
    sp_wd = nc.dram_tensor("sp_wd", [NKT, 128, NIT * 128], BF16, kind=spk).ap()
    sp_wo = nc.dram_tensor("sp_wo", [NKT, 128, NKT * 128], BF16, kind=spk).ap()

    sb = SBAlloc(nc, 207 * 1024)
    A = sb.alloc
    g_ident = A("g_ident", [128, 128], F32, (0, 3))
    g_identb = A("g_identb", [128, 128], BF16, (0, 3))
    g_ones_f = A("g_ones_f", [128, 1], F32, (0, 3))
    g_ones_bf = A("g_ones_bf", [128, 1], BF16, (0, 3))
    g_ones1 = A("g_ones1", [1, 128], F32, (0, 3))
    g_qdidx = A("g_qdidx", [128, 256], F32, (0, 1))
    g_diffp = A("g_diffp", [128, 512], F32, (0, 1))
    g_maskp = A("g_maskp", [128, 512], F32, (0, 1))
    g_kdidx = A("g_kdidx", [128, 2], F32, (0, 1))
    g_srow = A("g_srow", [1, 16], F32, (0, 1))
    g_negs = A("g_negs", [128, 16], F32, (0, 1))
    g_nw = A("g_nw", [128, 16], F32, (0, 1))
    g_ssq = A("g_ssq", [1, 2048], F32, (0, 3))
    g_row = A("g_row", [1, 2048], F32, (2, 3))
    g_rstd = A("g_rstd", [128, 2048], F32, (2, 3))
    g_misc = A("g_misc", [128, 4], F32, (0, 3))
    g_gsc = A("g_gsc", [128, 64], F32, (0, 3))
    g_usc = A("g_usc", [128, 64], F32, (0, 3))
    g_dsc = A("g_dsc", [128, 16], F32, (0, 3))
    big = A("big", [128, 32768], BF16, (0, 3))
    p_x = A("p_x", [128, 4096], F32, (0, 0))
    p_sq = A("p_sq", [128, 2048], F32, (0, 0))
    p_xnt = A("p_xnt", [128, 4096], BF16, (0, 0))
    p_st = A("p_st", [128, 128], F32, (0, 0))
    p_scl = A("p_scl", [128, 512], F32, (0, 0))
    p_wland = A("p_wland", [128, 4096], F32, (1, 1))
    p_wT = A("p_wT", [128, 16384], BF16, (1, 1))
    p_qkv = A("p_qkv", [128, 8192], BF16, (1, 1))
    p_G = A("p_G", [128, 4096], BF16, (1, 1))
    p_GA = A("p_GA", [128, 2048], BF16, (1, 1))
    p_vtm = A("p_vtm", [128, 2048], BF16, (1, 1))
    p_kd = A("p_kd", [128, 2048], BF16, (1, 1))
    p_sb = A("p_sb", [128, 512], BF16, (1, 1))
    p_qd = A("p_qd", [128, 256], BF16, (1, 1))
    p_sqb = A("p_sqb", [128, 256], BF16, (1, 1))
    p_t1 = A("p_t1", [128, 256], F32, (1, 1))
    p_kv = A("p_kv", [128, 128], F32, (1, 1))
    p_kvbf = A("p_kvbf", [128, 256], BF16, (1, 1))
    p_ddt = A("p_ddt", [128, 512], F32, (1, 1))
    p_qdbc = A("p_qdbc", [128, 256], F32, (1, 1))
    p_kdt = A("p_kdt", [128, 2], F32, (1, 1))
    p_bd = A("p_bd", [128, 1], F32, (1, 1))
    p_iland = A("p_iland", [128, 2048], I32, (1, 1))
    p_pnat = A("p_pnat", [128, 2048], F32, (1, 1))
    p_ptr = A("p_ptr", [128, 2048], BF16, (1, 1))
    p3_xnc = A("p3_xnc", [128, 16384], BF16, (3, 3))
    p3_gw = A("p3_gw", [128, 8192], BF16, (3, 3))
    p3_gal = A("p3_gal", [128, 1024], BF16, (3, 3))
    p3_wg = A("p3_wg", [128, 4096], BF16, (3, 3))
    p3_wu = A("p3_wu", [128, 4096], BF16, (3, 3))
    p3_wd = A("p3_wd", [128, 8192], BF16, (3, 3))
    p3_wo = A("p3_wo", [128, 4096], BF16, (3, 3))
    p3_sg = A("p3_sg", [128, 1024], BF16, (3, 3))
    p3_s0 = A("p3_s0", [128, 1024], F32, (3, 3))
    p3_s1 = A("p3_s1", [128, 1024], F32, (3, 3))
    p3_xp = A("p3_xp", [128, 256], F32, (3, 3))
    p3_ob = A("p3_ob", [128, 256], F32, (3, 3))

    pb = [es.enter_context(nc.psum_tensor(f"pb{i}", [128, 512], F32)) for i in range(8)]

    em = Emitter(nc, es)
    S = em.sem
    s_bar = S("bar")
    s_ld = S("ld")
    s_sq = S("s_sq"); s_ln = S("s_ln"); s_xnt = S("s_xnt")
    s_tbl = S("s_tbl"); s_mmt = S("s_mmt")
    s_proj = S("s_proj"); s_pev = S("s_pev")
    s_kdd = S("s_kdd")
    s_S = S("s_S"); s_Sb = S("s_Sb"); s_O = S("s_O"); s_KV = S("s_KV")
    s_kvu = S("s_kvu"); s_kvb = S("s_kvb"); s_ssm = S("s_ssm"); s_ssa = S("s_ssa")
    s_act = S("s_act"); s_ga = S("s_ga")
    s_pcv = S("s_pcv"); s_ptp = S("s_ptp")
    s_rst = S("s_rst"); s_gw = S("s_gw")
    s_gps = S("s_gps"); s_ups = S("s_ups"); s_sg = S("s_sg"); s_h = S("s_h")
    s_at = S("s_at"); s_dn = S("s_dn"); s_s1 = S("s_s1")
    s_tp3 = S("s_tp3"); s_ob = S("s_ob")
    s_x = s_wld = s_pin = s_xnc = s_gal = s_wgld = s_wuld = s_wdld = None
    s_wold = s_xp = s_ow = s_xsp = s_gasp = s_xtp = s_psp = s_ktp = None
    s_wtp = s_wcst = None

    dma = em.dma
    op = em.op
    tgc = [0]
    TP_ROT = (6, 7, 2, 3)

    def emit_tp_group(srcs, dst_ap, srcbufs, dstbuf, bf=False, evac=None):
        bi = TP_ROT[tgc[0] % len(TP_ROT)]
        p = tgc[0] % 2
        tgc[0] += 1
        bank = pb[bi]
        bname = f"pb{bi}"
        bview = bank.ap().bitcast(BF16) if bf else bank.ap()
        ident = g_identb if bf else g_ident
        idb = "identb" if bf else "ident"
        n = len(srcs)

        def tps(e, srcs=srcs, bview=bview, ident=ident, n=n):
            for k in range(n):
                ins = e.transpose(out=bview[:, k * 128:(k + 1) * 128],
                                  in_=srcs[k], identity=ident[:, :])
            return ins
        op("tensor", tps, s_mmt, reads=list(srcbufs) + [idb], writes=[bname])
        if evac is None:
            evac = "vector" if p == 0 else "scalar"
        iv = bview[:, 0:n * 128]
        if evac == "vector":
            op("vector", lambda e, dst_ap=dst_ap, iv=iv: e.tensor_copy(
                out=dst_ap, in_=iv), s_ptp, reads=[bname], writes=[dstbuf])
        else:
            op("scalar", lambda e, dst_ap=dst_ap, iv=iv: e.copy(
                out=dst_ap, in_=iv), s_ptp, reads=[bname], writes=[dstbuf])

    dma("sync", g_ident[:, :], c_ident, s_ld, writes=["ident"], skey="_c")
    dma("sync", g_ones_f[:, :], c_ones_bf, s_ld, writes=["onesf"], skey="_c")
    op("vector", lambda e: e.tensor_copy(out=g_ones_bf[:, :], in_=g_ones_f[:, :]),
       s_tbl, reads=["onesf"], writes=["onesbf"])
    dma("sync", g_ones1[0:1, :], c_ones1, s_ld, writes=["ones1"], skey="_c")
    dma("sync", g_qdidx[:, :], c_qdidx, s_ld, writes=["qdidx"], skey="_c")
    dma("sync", g_diffp[:, 0:256], c_diffp[0:128, :], s_ld, writes=["diffp"], skey="_c")
    dma("sync", g_diffp[:, 256:512], c_diffp[128:256, :], s_ld, writes=["diffp"], skey="_c")
    dma("sync", g_maskp[:, 0:256], c_maskp[0:128, :], s_ld, writes=["maskp"], skey="_c")
    dma("sync", g_maskp[:, 256:512], c_maskp[128:256, :], s_ld, writes=["maskp"], skey="_c")
    dma("sync", g_kdidx[:, 0:1], c_kdidx[0:128, :], s_ld, writes=["kdidx"], skey="_c")
    dma("sync", g_kdidx[:, 1:2], c_kdidx[128:256, :], s_ld, writes=["kdidx"], skey="_c")
    dma("sync", g_srow[0:1, :], slope[None, :], s_ld, writes=["srow"], skey="_c")
    dma("sync", g_misc[:, :], c_misc, s_ld, writes=["misc"], skey="_c")
    _cs = em._dsems["_c"]
    for _b in ("ident", "onesf", "ones1", "qdidx", "diffp", "maskp", "kdidx",
               "srow", "misc"):
        em.R[_b] = (_cs, _cs.val, "dma")

    op("vector", lambda e: e.memset(g_ssq[0:1, :], 0.0), s_tbl, writes=["ssq"])
    op("vector", lambda e: e.tensor_copy(out=g_identb[:, :], in_=g_ident[:, :]),
       s_tbl, reads=["ident"], writes=["identb"])

    for tt in range(16):
        sl = (tt % 2) * 2048
        xb = f"xb{tt % 2}"
        st = p_st[:, tt * 8: tt * 8 + 8]
        stn = f"st{tt}"
        dma("sync", p_x[:, sl:sl + 2048], x_in[tt * 128:(tt + 1) * 128, :], s_x,
            writes=[xb])
        op("vector", lambda e, sl=sl, st=st: e.reduce_sum(
            out=st[:, 0:1], in_=p_x[:, sl:sl + 2048], axis=AX.X),
            s_ln, reads=[xb], writes=[stn])
        op("scalar", lambda e, sl=sl, st=st: e.activation(
            out=p_sq[:, :], in_=p_x[:, sl:sl + 2048], func=AF.Square,
            bias=0.0, scale=1.0),
            s_sq, reads=[xb], writes=["sqbuf"])
        op("vector", lambda e, st=st: e.reduce_sum(
            out=st[:, 1:2], in_=p_sq[:, :], axis=AX.X),
            s_ln, reads=["sqbuf"], writes=[stn + "q"])

        op("vector", lambda e, st=st: e.tensor_scalar_mul(
            out=st[:, 2:3], in0=st[:, 0:1], scalar1=1.0 / D),
            s_ln, reads=[stn], writes=[stn + "m"])
        op("vector", lambda e, st=st: e.tensor_scalar_mul(
            out=st[:, 3:4], in0=st[:, 1:2], scalar1=1.0 / D),
            s_ln, reads=[stn + "q"], writes=[stn + "m2"])
        op("vector", lambda e, st=st: e.scalar_tensor_tensor(
            out=st[:, 4:5], in0=st[:, 2:3], scalar=st[:, 2:3], in1=st[:, 3:4],
            op0=ALU.mult, op1=ALU.subtract),
            s_ln, reads=[stn + "m", stn + "m2"], writes=[stn + "v"])
        op("scalar", lambda e, st=st: e.activation(
            out=st[:, 5:6], in_=st[:, 4:5], func=AF.Sqrt,
            bias=g_misc[:, 0:1], scale=-1.0),
            s_sq, reads=[stn + "v", "misc"], writes=[stn + "s"])
        op("vector", lambda e, st=st: e.reciprocal(
            out=st[:, 6:7], in_=st[:, 5:6]),
            s_ln, reads=[stn + "s"], writes=[stn + "r"])
        op("vector", lambda e, st=st: e.tensor_scalar(
            out=st[:, 7:8], in0=st[:, 2:3], scalar1=st[:, 6:7], scalar2=-1.0,
            op0=ALU.mult, op1=ALU.mult),
            s_ln, reads=[stn + "m", stn + "r"], writes=[stn + "f"])
        xnb = f"xnt{tt % 2}"
        op("scalar", lambda e, sl=sl, st=st: e.activation(
            out=p_xnt[:, sl:sl + 2048], in_=p_x[:, sl:sl + 2048], func=AF.Identity,
            bias=st[:, 7:8], scale=st[:, 6:7]),
            s_xnt, reads=[xb, stn + "f", stn + "r"], writes=[xnb])
        big_r = big.ap().rearrange("p (dk c) -> p dk c", c=2048)
        for g in range(4):
            srcs = [p_xnt[:, sl + (4 * g + k) * 128: sl + (4 * g + k) * 128 + 128]
                    for k in range(4)]
            dst = big_r[:, 4 * g:4 * g + 4, tt * 128: tt * 128 + 128]
            emit_tp_group(srcs, dst, [xnb, "identb"],
                          f"xng{4 * g}", bf=True)
    for dk in range(16):
        dma("sync", sp_xn[dk], big[:, dk * 2048:(dk + 1) * 2048], s_xsp,
            reads=[f"xng{(dk // 4) * 4}"], writes=[f"spxn{dk}"], skey="spxn")

    op("vector", lambda e: e.tensor_scalar_mul(
        out=g_srow[0:1, :], in0=g_srow[0:1, :], scalar1=-1.0),
        s_tbl, reads=["srow"], writes=["srow"])
    op("tensor", lambda e: e.matmul(pb[6][:, 0:16], g_ones1[0:1, :],
                                    g_srow[0:1, 0:16], start=True, stop=True),
       s_mmt, reads=["ones1", "srow"], writes=["pb6"])
    op("vector", lambda e: e.tensor_copy(out=g_negs[:, :], in_=pb[6][:, 0:16]),
       s_tbl, reads=["pb6"], writes=["negs"])
    dma("sync", p_scl[0:64, 0:128], gate_sc.rearrange("(a b) -> a b", b=128), s_wld,
        writes=["scl"])
    dma("sync", p_scl[0:64, 128:256], up_sc.rearrange("(a b) -> a b", b=128), s_wld,
        writes=["scl"])
    dma("sync", p_scl[0:16, 256:384], down_sc.rearrange("(a b) -> a b", b=128), s_wld,
        writes=["scl"])
    dma("sync", p_scl[0:16, 384:512], norm_w.rearrange("(a b) -> a b", b=128), s_wld,
        writes=["scl"])

    def sc_tp(e):
        e.transpose(out=pb[6][:, 0:64], in_=p_scl[0:64, 0:128],
                    identity=g_ident[0:64, 0:64])
        e.transpose(out=pb[7][:, 0:64], in_=p_scl[0:64, 128:256],
                    identity=g_ident[0:64, 0:64])
        e.transpose(out=pb[6][:, 64:80], in_=p_scl[0:16, 256:384],
                    identity=g_ident[0:16, 0:16])
        return e.transpose(out=pb[7][:, 64:80], in_=p_scl[0:16, 384:512],
                           identity=g_ident[0:16, 0:16])
    op("tensor", sc_tp, s_mmt, reads=["scl", "ident"], writes=["pb6", "pb7"])

    def sc_ev(e):
        e.tensor_copy(out=g_gsc[:, :], in_=pb[6][:, 0:64])
        e.tensor_copy(out=g_usc[:, :], in_=pb[7][:, 0:64])
        e.tensor_copy(out=g_dsc[:, :], in_=pb[6][:, 64:80])
        return e.tensor_copy(out=g_nw[:, :], in_=pb[7][:, 64:80])
    op("vector", sc_ev, s_tbl, reads=["pb6", "pb7"], writes=["scales"])

    em.barrier(s_bar)

    if stop_after == 0:
        sb.finalize(es)
        em.replay()
        return nc, es

    prep_chunks = []
    for r in range(NIT):
        prep_chunks += [("wg", r, c) for c in range(4)]
    for r in range(NIT):
        prep_chunks += [("wu", r, c) for c in range(4)]
    for r in range(NKT):
        prep_chunks += [("wd", r, c) for c in range(16)]
    for r in range(NKT):
        prep_chunks += [("wo", r, c) for c in range(4)]
    pj = [0]

    def emit_prep_chunk():
        j = pj[0]
        if no_prep or j >= len(prep_chunks):
            pj[0] = len(prep_chunks)
            return
        pj[0] += 1
        kind, r, c = prep_chunks[j]
        par = j % 4
        src = {"wg": gate_w, "wu": up_w, "wd": down_w, "wo": w_out}[kind]
        dst = {"wg": sp_wg, "wu": sp_wu, "wd": sp_wd, "wo": sp_wo}[kind]
        ilb, pnb, ptb = f"il{par}", f"pn{par}", f"pt{par}"
        if kind == "wo":
            dma("gpsimd", p_pnat[:, par * 512:(par + 1) * 512],
                src[r * 128:(r + 1) * 128, c * 512:(c + 1) * 512], s_pin,
                writes=[pnb])
        else:
            conv = {"wg": "vector", "wu": "scalar", "wd": ("vector", "scalar")[j % 2]}[kind]
            dma("gpsimd", p_iland[:, par * 512:(par + 1) * 512],
                src[r * 128:(r + 1) * 128, c * 512:(c + 1) * 512], s_pin,
                writes=[ilb])
            if conv == "scalar":
                op(conv, lambda e, par=par: e.activation(
                    out=p_pnat[:, par * 512:(par + 1) * 512],
                    in_=p_iland[:, par * 512:(par + 1) * 512],
                    func=AF.Identity, bias=g_misc[:, 2:3], scale=1.0),
                    s_pcv, reads=[ilb, "misc"], writes=[pnb])
            else:
                op(conv, lambda e, par=par: e.tensor_single_scalar(
                    out=p_pnat[:, par * 512:(par + 1) * 512],
                    in_=p_iland[:, par * 512:(par + 1) * 512],
                    scalar=-128.0, op=ALU.add),
                    s_pcv, reads=[ilb], writes=[pnb])
        srcs = [p_pnat[:, par * 512 + k * 128: par * 512 + (k + 1) * 128]
                for k in range(4)]
        emit_tp_group(srcs, p_ptr[:, par * 512:(par + 1) * 512], [pnb], ptb)
        dma("sync", dst[r][:, c * 512:(c + 1) * 512],
            p_ptr[:, par * 512:(par + 1) * 512],
            s_psp, reads=[ptb], writes=[f"sp_{kind}"], skey=ptb)

    def emit_head_weight_row(h, ri):
        hp = h % 2
        rows = [(w_qkv, h), (w_qkv, NH + h), (w_qkv, 2 * NH + h), (w_gate, h)]
        srcw, r = rows[ri]
        gi = h * 4 + ri
        par = gi % 2
        lb = f"wl{par}"
        dma("sync", p_wland[:, par * 2048:(par + 1) * 2048],
            srcw[r * 128:(r + 1) * 128, :], s_wld, writes=[lb])
        base = (hp * 4 + ri) * 2048
        for g in range(4):
            srcs = [p_wland[:, par * 2048 + g * 512 + k * 128:
                            par * 2048 + g * 512 + (k + 1) * 128]
                    for k in range(4)]
            emit_tp_group(srcs, p_wT[:, base + g * 512: base + (g + 1) * 512],
                          [lb], f"wT{hp}")

    def emit_head_weights(h):
        for ri in range(4):
            emit_head_weight_row(h, ri)

    def emit_tables(h):
        ng = g_negs[:, h:h + 1]
        op("scalar", lambda e: e.activation(
            out=p_qdbc[:, :], in_=g_qdidx[:, :], func=AF.Exp, bias=0.0, scale=ng),
            s_act, reads=["negs"], writes=["qdbc"])
        op("scalar", lambda e: e.activation(
            out=p_ddt[:, :], in_=g_diffp[:, :], func=AF.Exp, bias=0.0, scale=ng),
            s_act, reads=["negs"], writes=["ddt"])
        op("scalar", lambda e: e.activation(
            out=p_kdt[:, :], in_=g_kdidx[:, :], func=AF.Exp, bias=0.0, scale=ng),
            s_act, reads=["negs"], writes=["kdt"])
        op("scalar", lambda e: e.activation(
            out=p_bd[:, :], in_=ng, func=AF.Exp, bias=0.0, scale=256.0),
            s_act, reads=["negs"], writes=["bd"])
        op("vector", lambda e: e.tensor_tensor(
            out=p_ddt[:, :], in0=p_ddt[:, :], in1=g_maskp[:, :], op=ALU.mult),
            s_tbl, reads=["ddt"], writes=["ddt"])

    def emit_proj_group(h, gidx):
        hp = h % 2
        r, mc = divmod(gidx, 4)
        bank = gidx % 2
        pbuf = pb[bank]
        wtb = f"wT{hp}"
        base = (hp * 4 + r) * 2048

        if r == 2:
            def mmv(e, base=base, mc=mc, pbuf=pbuf):
                for smt in range(4):
                    mt = mc * 4 + smt
                    for dk in range(16):
                        ins = e.matmul(
                            pbuf[:, smt * 128:(smt + 1) * 128],
                            big[:, dk * 2048 + mt * 128: dk * 2048 + mt * 128 + 128],
                            p_wT[:, base + dk * 128: base + dk * 128 + 128],
                            start=(dk == 0), stop=(dk == 15))
                return ins
            op("tensor", mmv, s_proj, reads=[wtb, "xnres"], writes=[f"pb{bank}"])
            dst = p_vtm[:, mc * 512:(mc + 1) * 512]
            op("scalar", lambda e, dst=dst, pbuf=pbuf: e.activation(
                out=dst, in_=pbuf[:, :], func=AF.Silu, bias=0.0, scale=1.0),
                s_pev, reads=[f"pb{bank}"], writes=["vtm"])
            return

        def mm(e, base=base, mc=mc, pbuf=pbuf):
            for dk in range(16):
                ins = e.matmul(pbuf[:, :],
                               p_wT[:, base + dk * 128: base + dk * 128 + 128],
                               big[:, dk * 2048 + mc * 512: dk * 2048 + mc * 512 + 512],
                               start=(dk == 0), stop=(dk == 15))
            return ins
        op("tensor", mm, s_proj, reads=[wtb, "xnres"], writes=[f"pb{bank}"])
        if r < 2:
            dst = p_qkv[:, (hp * 2 + r) * 2048 + mc * 512:
                        (hp * 2 + r) * 2048 + mc * 512 + 512]
            op("scalar", lambda e, dst=dst, pbuf=pbuf: e.activation(
                out=dst, in_=pbuf[:, :], func=AF.Silu, bias=0.0, scale=1.0),
                s_pev, reads=[f"pb{bank}"], writes=[f"qkv{hp}"])
        else:
            dst = p_G[:, hp * 2048 + mc * 512: hp * 2048 + mc * 512 + 512]
            op("scalar", lambda e, dst=dst, pbuf=pbuf: e.activation(
                out=dst, in_=pbuf[:, :], func=AF.Sigmoid, bias=0.0, scale=1.0),
                s_pev, reads=[f"pb{bank}"], writes=[f"G{hp}"])

    def emit_ktm(h):
        hp = h % 2
        kbase = (hp * 2 + 1) * 2048
        for g in range(4):
            bi = TP_ROT[tgc[0] % len(TP_ROT)]
            tgc[0] += 1
            bank = pb[bi]
            bname = f"pb{bi}"
            bview = bank.ap().bitcast(BF16)

            def tps(e, g=g, bview=bview, kbase=kbase):
                for k in range(4):
                    tt = 4 * g + k
                    ins = e.transpose(
                        out=bview[:, k * 128:(k + 1) * 128],
                        in_=p_qkv[:, kbase + tt * 128: kbase + tt * 128 + 128],
                        identity=g_identb[:, :])
                return ins
            op("tensor", tps, s_mmt, reads=[f"qkv{hp}", "identb"], writes=[bname])
            for k in range(4):
                tt = 4 * g + k
                op("vector", lambda e, tt=tt, k=k, bview=bview: e.tensor_scalar_mul(
                    out=p_kd[:, tt * 128:(tt + 1) * 128],
                    in0=bview[:, k * 128:(k + 1) * 128],
                    scalar1=p_kdt[:, (tt % 2):(tt % 2) + 1]),
                    s_kdd, reads=[bname, "kdt"], writes=["kdbuf"])

    def emit_scan_init(h):
        def z(e):
            e.memset(p_kv[:, :], 0.0)
            return e.memset(p_kvbf[:, :], 0.0)
        op("vector", z, s_kvu, writes=["kv", "kvbf0", "kvbf1"])

    def emit_scan_block(h, b):
        hp = h % 2
        qbase = hp * 2 * 2048
        kbase = (hp * 2 + 1) * 2048
        m0 = b * BLK

        def mmS(e):
            e.matmul(pb[2][:, 0:256],
                     p_qkv[:, kbase + m0: kbase + m0 + 128],
                     p_qkv[:, qbase + m0: qbase + m0 + 256], start=True, stop=True)
            return e.matmul(pb[2][:, 256:512],
                            p_qkv[:, kbase + m0 + 128: kbase + m0 + 256],
                            p_qkv[:, qbase + m0: qbase + m0 + 256],
                            start=True, stop=True)
        op("tensor", mmS, s_S, reads=[f"qkv{hp}"], writes=["pb2"])

        def mmKV(e, b=b):
            e.matmul(pb[4][:, 0:128], p_kd[:, 2 * b * 128: 2 * b * 128 + 128],
                     p_vtm[:, 2 * b * 128: 2 * b * 128 + 128], start=True, stop=False)
            return e.matmul(pb[4][:, 0:128],
                            p_kd[:, (2 * b + 1) * 128: (2 * b + 1) * 128 + 128],
                            p_vtm[:, (2 * b + 1) * 128: (2 * b + 1) * 128 + 128],
                            start=False, stop=True)
        op("tensor", mmKV, s_KV, reads=["kdbuf", "vtm"], writes=["pb4"])

        for kc in range(2):
            op("vector", lambda e, kc=kc: e.tensor_tensor(
                out=p_sb[:, kc * 256:(kc + 1) * 256],
                in0=pb[2][:, kc * 256:(kc + 1) * 256],
                in1=p_ddt[:, kc * 256:(kc + 1) * 256], op=ALU.mult),
                s_Sb, reads=["pb2", "ddt"], writes=["sb"])
        op("vector", lambda e: e.tensor_tensor(
            out=p_qd[:, :], in0=p_qkv[:, qbase + m0: qbase + m0 + 256],
            in1=p_qdbc[:, :], op=ALU.mult),
            s_Sb, reads=[f"qkv{hp}", "qdbc"], writes=["qd"])

        kvslot = (b + 1) % 2

        def mmO(e, b=b, kvslot=kvslot):
            e.matmul(pb[3][:, 0:256], p_vtm[:, 2 * b * 128: 2 * b * 128 + 128],
                     p_sb[:, 0:256], start=True, stop=False)
            e.matmul(pb[3][:, 0:256],
                     p_vtm[:, (2 * b + 1) * 128: (2 * b + 1) * 128 + 128],
                     p_sb[:, 256:512], start=False, stop=False)
            return e.matmul(pb[3][:, 0:256],
                            p_kvbf[:, kvslot * 128: kvslot * 128 + 128],
                            p_qd[:, :], start=False, stop=True)
        op("tensor", mmO, s_O, reads=["vtm", "sb", "qd", f"kvbf{kvslot}"],
           writes=["pb3"])

        op("vector", lambda e: e.scalar_tensor_tensor(
            out=p_kv[:, :], in0=p_kv[:, :], scalar=p_bd[:, 0:1],
            in1=pb[4][:, 0:128], op0=ALU.mult, op1=ALU.add),
            s_kvu, reads=["pb4", "bd", "kv"], writes=["kv"])
        kvs = b % 2
        op("scalar", lambda e, kvs=kvs: e.copy(
            out=p_kvbf[:, kvs * 128: kvs * 128 + 128], in_=p_kv[:, :]),
            s_kvb, reads=["kv"], writes=[f"kvbf{kvs}"])

        op("scalar", lambda e: e.activation(
            out=p_sqb[:, :], in_=pb[3][:, 0:256], func=AF.Square, bias=0.0, scale=1.0),
            s_act, reads=["pb3"], writes=["sqb"])
        op("scalar", lambda e, h=h: e.activation(
            out=p_t1[:, :], in_=pb[3][:, 0:256], func=AF.Identity, bias=0.0,
            scale=g_nw[:, h:h + 1]),
            s_act, reads=["pb3", "scales"], writes=["t1"])
        op("tensor", lambda e: e.matmul(
            pb[5][0:1, 0:256], g_ones_bf[:, 0:1], p_sqb[:, :], start=True, stop=True),
            s_ssm, reads=["sqb", "onesbf"], writes=["pb5"])
        op("vector", lambda e, m0=m0: e.tensor_tensor(
            out=g_ssq[0:1, m0:m0 + 256], in0=g_ssq[0:1, m0:m0 + 256],
            in1=pb[5][0:1, 0:256], op=ALU.add),
            s_ssa, reads=["pb5", "ssq"], writes=["ssq"])
        op("vector", lambda e, m0=m0, hp=hp: e.tensor_tensor(
            out=p_GA[:, m0:m0 + 256], in0=p_t1[:, :],
            in1=p_G[:, hp * 2048 + m0: hp * 2048 + m0 + 256], op=ALU.mult),
            s_ga, reads=["t1", f"G{hp}"], writes=["GA"])

    def emit_ga_spill(h):
        dma("gpsimd", sp_ga[h], p_GA[:, :], s_gasp, reads=["GA"],
            writes=["spga"])

    em.R["xnres"] = (s_bar, s_bar.val, "none")
    emit_head_weights(0)
    for _ in range(8):
        emit_prep_chunk()
    for h in range(NH):
        scan_h = h - 1
        for sub in range(8):
            emit_proj_group(h, 2 * sub)
            emit_prep_chunk()
            emit_proj_group(h, 2 * sub + 1)
            emit_prep_chunk()
            if scan_h >= 0:
                emit_scan_block(scan_h, sub)
            if sub < 4 and h + 1 < NH:
                emit_head_weight_row(h + 1, sub)
            emit_prep_chunk()
        if scan_h >= 0:
            emit_ga_spill(scan_h)
        if debug and scan_h == 0:
            dma("gpsimd", dbg_qkv, p_qkv[:, 0:6144], s_gasp, reads=["qkv0"],
                writes=["dbgq"])
            dma("gpsimd", dbg_G, p_G[:, 0:2048], s_gasp, reads=["G0"],
                writes=["dbgg"])
        emit_tables(h)
        emit_ktm(h)
        emit_scan_init(h)
    for b in range(NBLK):
        emit_scan_block(NH - 1, b)
        emit_prep_chunk()
    emit_ga_spill(NH - 1)
    while pj[0] < len(prep_chunks):
        emit_prep_chunk()

    em.barrier(s_bar)

    if stop_after == 1:
        if debug:
            dma("gpsimd", dbg_ssq, g_ssq[0:1, :], s_gasp, reads=["ssq"],
                writes=["dbgs"])
            em.barrier(s_bar)
        sb.finalize(es)
        em.replay()
        return nc, es

    op("scalar", lambda e: e.activation(
        out=g_row[0:1, :], in_=g_ssq[0:1, :], func=AF.Sqrt,
        bias=g_misc[0:1, 1:2], scale=1.0 / D),
        s_rst, reads=["ssq", "misc"], writes=["row"])
    op("vector", lambda e: e.reciprocal(out=g_row[0:1, :], in_=g_row[0:1, :]),
       s_rst, reads=["row"], writes=["row"])
    for mc in range(NMC):
        op("tensor", lambda e, mc=mc: e.matmul(
            pb[6][:, :], g_ones1[0:1, :], g_row[0:1, mc * 512:(mc + 1) * 512],
            start=True, stop=True),
            s_mmt, reads=["row", "ones1"], writes=["pb6"])
        op("vector", lambda e, mc=mc: e.tensor_copy(
            out=g_rstd[:, mc * 512:(mc + 1) * 512], in_=pb[6][:, :]),
            s_rst, reads=["rstd", "pb6"], writes=["rstd"])


    def emit_xnc(mc):
        par = mc % 2
        for dk in range(16):
            dma("sync", p3_xnc[:, par * 8192 + dk * 512: par * 8192 + dk * 512 + 512],
                sp_xn[dk][:, mc * 512:(mc + 1) * 512], s_xnc,
                reads=[f"spxn{dk}"], writes=[f"xnc{par}"])

    def emit_p3_tail(mc, hb):
        hp2 = hb % 2
        op("vector", lambda e, hb=hb, hp2=hp2: e.tensor_scalar_mul(
            out=p3_s0[:, hp2 * 512:(hp2 + 1) * 512], in0=pb[4 + hp2][:, :],
            scalar1=g_dsc[:, hb:hb + 1]),
            s_s1, reads=[f"pb{4 + hp2}", "scales"], writes=[f"s0_{hp2}"])
        op("vector", lambda e, hp2=hp2: e.tensor_tensor(
            out=p3_s1[:, hp2 * 512:(hp2 + 1) * 512],
            in0=p3_s0[:, hp2 * 512:(hp2 + 1) * 512],
            in1=pb[2 + hp2][:, :], op=ALU.add),
            s_s1, reads=[f"s0_{hp2}", f"pb{2 + hp2}"], writes=[f"s1_{hp2}"])
        for j in range(4):
            jp = j % 2
            mt = mc * 4 + j
            dma("sync", p3_xp[:, jp * 128:(jp + 1) * 128],
                x_in[mt * 128:(mt + 1) * 128, hb * 128:(hb + 1) * 128], s_xp,
                writes=[f"xp{jp}"])
            op("tensor", lambda e, hp2=hp2, j=j, jp=jp: e.transpose(
                out=pb[6 + jp][:, 0:128],
                in_=p3_s1[:, hp2 * 512 + j * 128: hp2 * 512 + j * 128 + 128],
                identity=g_ident[:, :]),
                s_tp3, reads=[f"s1_{hp2}", "ident"], writes=[f"pb{6 + jp}"])
            op("vector", lambda e, jp=jp: e.tensor_tensor(
                out=p3_ob[:, jp * 128:(jp + 1) * 128], in0=pb[6 + jp][:, 0:128],
                in1=p3_xp[:, jp * 128:(jp + 1) * 128], op=ALU.add),
                s_ob, reads=[f"pb{6 + jp}", f"xp{jp}"], writes=[f"ob{jp}"])
            dma("sync", out_t[mt * 128:(mt + 1) * 128, hb * 128:(hb + 1) * 128],
                p3_ob[:, jp * 128:(jp + 1) * 128], s_ow,
                reads=[f"ob{jp}"], writes=["hbm_out"], skey=f"ob{jp}")

    emit_xnc(0)
    for mc in range(NMC):
        par = mc % 2
        xncb = f"xnc{par}"
        for ft in range(16):
            gp = ft % 2
            dma("scalar", p3_gal[:, gp * 512:(gp + 1) * 512],
                sp_ga[ft][:, mc * 512:(mc + 1) * 512], s_gal,
                reads=["spga"], writes=[f"gal{gp}"])
            op("vector", lambda e, ft=ft, gp=gp, mc=mc: e.tensor_tensor(
                out=p3_gw[:, ft * 512:(ft + 1) * 512],
                in0=p3_gal[:, gp * 512:(gp + 1) * 512],
                in1=g_rstd[:, mc * 512:(mc + 1) * 512], op=ALU.mult),
                s_gw, reads=[f"gal{gp}", "rstd"], writes=["gw"])
        for it in range(NIT):
            if it == 40 and mc + 1 < NMC:
                emit_xnc(mc + 1)
            wp = it % 2
            wgb, wub = f"wg{wp}", f"wu{wp}"
            dma("gpsimd", p3_wg[:, wp * 2048:(wp + 1) * 2048],
                sp_wg[it], s_wgld, writes=[wgb])
            dma("gpsimd", p3_wu[:, wp * 2048:(wp + 1) * 2048],
                sp_wu[it], s_wuld, writes=[wub])

            def mmg(e, wp=wp, par=par):
                for dk in range(16):
                    ins = e.matmul(pb[0][:, :],
                                   p3_wg[:, wp * 2048 + dk * 128: wp * 2048 + dk * 128 + 128],
                                   p3_xnc[:, par * 8192 + dk * 512: par * 8192 + dk * 512 + 512],
                                   start=(dk == 0), stop=(dk == 15))
                return ins
            op("tensor", mmg, s_gps, reads=[wgb, xncb], writes=["pb0"])

            def mmu(e, wp=wp, par=par):
                for dk in range(16):
                    ins = e.matmul(pb[1][:, :],
                                   p3_wu[:, wp * 2048 + dk * 128: wp * 2048 + dk * 128 + 128],
                                   p3_xnc[:, par * 8192 + dk * 512: par * 8192 + dk * 512 + 512],
                                   start=(dk == 0), stop=(dk == 15))
                return ins
            op("tensor", mmu, s_ups, reads=[wub, xncb], writes=["pb1"])
            sgp = it % 2
            op("scalar", lambda e, it=it, sgp=sgp: e.activation(
                out=p3_sg[:, sgp * 512:(sgp + 1) * 512], in_=pb[0][:, :],
                func=AF.Silu, bias=0.0, scale=g_gsc[:, it:it + 1]),
                s_sg, reads=["pb0", "scales"], writes=[f"sg{sgp}"])
            op("vector", lambda e, it=it, sgp=sgp: e.scalar_tensor_tensor(
                out=big[:, it * 512:(it + 1) * 512], in0=pb[1][:, :],
                scalar=g_usc[:, it:it + 1],
                in1=p3_sg[:, sgp * 512:(sgp + 1) * 512],
                op0=ALU.mult, op1=ALU.mult),
                s_h, reads=["pb1", f"sg{sgp}", "scales"], writes=["hres"])
        for hb in range(NKT):
            hp2 = hb % 2
            wob = f"wo{hp2}"
            dma("sync", p3_wo[:, hp2 * 2048:(hp2 + 1) * 2048],
                sp_wo[hb], s_wold, writes=[wob])

            def mmat(e, hp2=hp2):
                for ft in range(16):
                    ins = e.matmul(pb[2 + hp2][:, :],
                                   p3_wo[:, hp2 * 2048 + ft * 128: hp2 * 2048 + ft * 128 + 128],
                                   p3_gw[:, ft * 512:(ft + 1) * 512],
                                   start=(ft == 0), stop=(ft == 15))
                return ins
            op("tensor", mmat, s_at, reads=[wob, "gw"], writes=[f"pb{2 + hp2}"])
            for q in range(4):
                qp = (hb * 4 + q) % 4
                wdb = f"wd{qp}"
                dma("scalar", p3_wd[:, qp * 2048:(qp + 1) * 2048],
                    sp_wd[hb][:, q * 2048:(q + 1) * 2048],
                    s_wdld, writes=[wdb])

                def mmdn(e, hp2=hp2, q=q, qp=qp):
                    for j in range(16):
                        it = q * 16 + j
                        ins = e.matmul(pb[4 + hp2][:, :],
                                       p3_wd[:, qp * 2048 + j * 128: qp * 2048 + j * 128 + 128],
                                       big[:, it * 512:(it + 1) * 512],
                                       start=(it == 0), stop=(it == 63))
                    return ins
                op("tensor", mmdn, s_dn, reads=[wdb, "hres"], writes=[f"pb{4 + hp2}"])
            if hb > 0:
                emit_p3_tail(mc, hb - 1)
        emit_p3_tail(mc, NKT - 1)

    em.barrier(s_bar)

    sb.finalize(es)
    em.replay()
    return nc, es


_CONSTS = None


def _consts():
    global _CONSTS
    if _CONSTS is None:
        m = np.arange(256)
        k = np.arange(256)
        diffp = np.maximum(m[None, :] - k[:, None], 0).astype(np.float32)
        maskp = (m[None, :] >= k[:, None]).astype(np.float32)
        _CONSTS = {
            "c_ident": np.eye(128, dtype=np.float32),
            "c_ones_bf": np.ones((128, 1), dtype=np.float32),
            "c_ones1": np.ones((1, 128), dtype=np.float32),
            "c_qdidx": np.broadcast_to((m + 1).astype(np.float32), (128, 256)).copy(),
            "c_diffp": diffp,
            "c_maskp": maskp,
            "c_kdidx": (255 - k).astype(np.float32).reshape(256, 1),
            "c_misc": np.broadcast_to(
                np.array([1e-5, 1e-6, -128.0, 0.0], dtype=np.float32), (128, 4)
            ).copy(),
        }
    return _CONSTS


def kernel(L0_attn_hidden_states, L0_attn_slope_rate, L0_attn_w_qkv,
           L0_attn_w_gate, L0_attn_w_out, L0_attn_norm_w,
           L0_ffn_gate_w, L0_ffn_gate_scale, L0_ffn_up_w, L0_ffn_up_scale,
           L0_ffn_down_w, L0_ffn_down_scale):
    x = np.ascontiguousarray(np.asarray(L0_attn_hidden_states, dtype=np.float32))
    shared = {
        "slope": np.asarray(L0_attn_slope_rate, dtype=np.float32),
        "w_qkv": np.asarray(L0_attn_w_qkv, dtype=np.float32),
        "w_gate": np.asarray(L0_attn_w_gate, dtype=np.float32),
        "w_out": np.asarray(L0_attn_w_out, dtype=np.float32),
        "norm_w": np.asarray(L0_attn_norm_w, dtype=np.float32),
        "gate_w": np.asarray(L0_ffn_gate_w, dtype=np.int32),
        "gate_sc": np.asarray(L0_ffn_gate_scale, dtype=np.float32),
        "up_w": np.asarray(L0_ffn_up_w, dtype=np.int32),
        "up_sc": np.asarray(L0_ffn_up_scale, dtype=np.float32),
        "down_w": np.asarray(L0_ffn_down_w, dtype=np.int32),
        "down_sc": np.asarray(L0_ffn_down_scale, dtype=np.float32),
    }
    shared.update(_consts())
    nc, es = build_program()
    in_maps = [dict(shared, x=x[c]) for c in range(8)]
    try:
        res = run_bass_kernel_spmd(nc, in_maps, core_ids=list(range(8)))
    finally:
        es.close()
    return np.stack([np.asarray(r["out"]) for r in res.results], axis=0)


# version 10
# speedup vs baseline: 1.0669x; 1.0669x over previous
import numpy as np
import ml_dtypes
import concourse.bass as bass
import concourse.mybir as mybir
from concourse.bass_utils import run_bass_kernel_spmd

F32 = mybir.dt.float32
BF16 = mybir.dt.bfloat16
I32 = mybir.dt.int32
U8 = mybir.dt.uint8
AF = mybir.ActivationFunctionType
ALU = mybir.AluOpType
AX = mybir.AxisListType

D = 2048
NTOK = 2048
NH = 16
DH = 128
NI = 8192
NKT = D // 128
NIT = NI // 128
MC = 512
NMC = NTOK // MC
BLK = 256
NBLK = NTOK // BLK
ENGS = ("sync", "scalar", "vector", "tensor", "gpsimd")
ENFORCE_SAME_ENGINE = True


class Sem:
    __slots__ = ("h", "name", "val", "_cm")

    def __init__(self, nc, es, name):
        self.h = es.enter_context(nc.semaphore(name))
        self.name = name
        self.val = 0


class Emitter:
    def __init__(self, nc, es):
        self.nc = nc
        self.q = {e: [] for e in ENGS}
        self.lastw = {e: {} for e in ENGS}
        self.R = {}
        self.W = {}
        self.sems = {}
        self._dsems = {}
        self._es = es

    def sem(self, name):
        s = Sem(self.nc, self._es, name)
        self.sems[name] = s
        return s

    def _wait(self, eng, sem, val):
        if val <= 0:
            return
        if self.lastw[eng].get(sem.name, 0) >= val:
            return
        self.lastw[eng][sem.name] = val
        self.q[eng].append(lambda e, s=sem.h, v=val: e.wait_ge(s, v))

    def op(self, eng, build, sem, reads=(), writes=(), amt=1, extra_waits=()):
        for b in reads:
            m = self.R.get(b)
            if m is not None:
                s, v, peng = m
                if peng != eng or ENFORCE_SAME_ENGINE:
                    self._wait(eng, s, v)
        for b in writes:
            m = self.R.get(b)
            if m is not None:
                s, v, peng = m
                if peng != eng or ENFORCE_SAME_ENGINE:
                    self._wait(eng, s, v)
            for (s, v, peng) in self.W.get(b, {}).values():
                if peng != eng or ENFORCE_SAME_ENGINE:
                    self._wait(eng, s, v)
        for (s, v) in extra_waits:
            self._wait(eng, s, v)
        sem.val += amt
        v = sem.val
        self.q[eng].append(lambda e, b=build, s=sem.h, a=amt: b(e).then_inc(s, a))
        for b in writes:
            self.R[b] = (sem, v, eng)
            self.W[b] = {}
        for b in reads:
            self.W.setdefault(b, {})[sem.name] = (sem, v, eng)
        return v

    def dma(self, eng, out, in_, sem, reads=(), writes=(), tp=False, extra_waits=(),
            skey=None):
        if skey is None:
            skey = writes[0] if writes else (reads[0] if reads else None)
        if skey is not None:
            s = self._dsems.get(skey)
            if s is None:
                s = self.sem("d_" + skey)
                self._dsems[skey] = s
            sem = s
        return self.op(
            eng,
            lambda e, o=out, i=in_, t=tp: e.dma_start(out=o, in_=i, transpose=t),
            sem, reads=reads, writes=writes, amt=16, extra_waits=extra_waits)

    def barrier(self, bsem):
        for e in ENGS:
            for s in self.sems.values():
                if s is not bsem:
                    self._wait(e, s, s.val)
            bsem.val += 1
            self.q[e].append(lambda eng, s=bsem.h: eng.sem_inc(s, 1))
        tot = bsem.val
        for e in ENGS:
            self._wait(e, bsem, tot)

    def replay(self):
        with self.nc.Block() as block:
            @block.sync
            def _(e):
                for f in self.q["sync"]:
                    f(e)

            @block.scalar
            def _(e):
                for f in self.q["scalar"]:
                    f(e)

            @block.vector
            def _(e):
                for f in self.q["vector"]:
                    f(e)

            @block.tensor
            def _(e):
                for f in self.q["tensor"]:
                    f(e)

            @block.gpsimd
            def _(e):
                for f in self.q["gpsimd"]:
                    f(e)


class SBAlloc:

    def __init__(self, nc, total_bytes):
        self.nc = nc
        self.total = total_bytes
        self.items = []
        self.regions = []

    def alloc(self, name, shape, dtype, phases=(0, 3)):
        per_part = int(np.prod(shape[1:])) * mybir.dt.size(dtype)
        per_part = (per_part + 63) // 64 * 64
        p0, p1 = phases
        taken = sorted(
            (s, e) for (s, e, q0, q1) in self.regions if not (q1 < p0 or q0 > p1))
        addr = 0
        for (s, e) in taken:
            if addr + per_part <= s:
                break
            addr = max(addr, e)
        if addr + per_part > self.total:
            raise MemoryError(f"SBUF overflow: {name} needs {per_part}B at {addr}")
        h = self.nc.virtual_sbuf_tensor(name, list(shape), dtype)
        self.items.append((h, addr))
        self.regions.append((addr, addr + per_part, p0, p1))
        return h

    def finalize(self, es):
        backing = es.enter_context(self.nc.sbuf_tensor("arena", [128, self.total], U8))
        base = self.nc.lookup_mloc(backing).addr
        for h, addr in self.items:
            self.nc._tensor(h.name, h.shape, h.dtype, type="SB")
            mloc = self.nc.lookup_mloc(h)
            mloc.allocated = not self.nc.target_bir_lowering
            mloc.addr = base + addr
        return backing


def build_program(debug=False, stop_after=3, no_prep=False):
    from contextlib import ExitStack
    es = ExitStack()
    nc = bass.Bass("TRN2", target_bir_lowering=False)

    x_in = nc.dram_tensor("x", [NTOK, D], F32, kind="ExternalInput").ap()
    slope = nc.dram_tensor("slope", [NH], F32, kind="ExternalInput").ap()
    w_qkv = nc.dram_tensor("w_qkv", [3 * D, D], F32, kind="ExternalInput").ap()
    w_gate = nc.dram_tensor("w_gate", [D, D], F32, kind="ExternalInput").ap()
    w_out = nc.dram_tensor("w_out", [D, D], F32, kind="ExternalInput").ap()
    norm_w = nc.dram_tensor("norm_w", [D], F32, kind="ExternalInput").ap()
    gate_w = nc.dram_tensor("gate_w", [NI, D], I32, kind="ExternalInput").ap()
    gate_sc = nc.dram_tensor("gate_sc", [NI], F32, kind="ExternalInput").ap()
    up_w = nc.dram_tensor("up_w", [NI, D], I32, kind="ExternalInput").ap()
    up_sc = nc.dram_tensor("up_sc", [NI], F32, kind="ExternalInput").ap()
    down_w = nc.dram_tensor("down_w", [D, NI], I32, kind="ExternalInput").ap()
    down_sc = nc.dram_tensor("down_sc", [D], F32, kind="ExternalInput").ap()
    c_ident = nc.dram_tensor("c_ident", [128, 128], F32, kind="ExternalInput").ap()
    c_ones_bf = nc.dram_tensor("c_ones_bf", [128, 1], F32, kind="ExternalInput").ap()
    c_ones1 = nc.dram_tensor("c_ones1", [1, 128], F32, kind="ExternalInput").ap()
    c_qdidx = nc.dram_tensor("c_qdidx", [128, 256], F32, kind="ExternalInput").ap()
    c_diffp = nc.dram_tensor("c_diffp", [256, 256], F32, kind="ExternalInput").ap()
    c_maskp = nc.dram_tensor("c_maskp", [256, 256], F32, kind="ExternalInput").ap()
    c_kdidx = nc.dram_tensor("c_kdidx", [256, 1], F32, kind="ExternalInput").ap()
    c_misc = nc.dram_tensor("c_misc", [128, 4], F32, kind="ExternalInput").ap()
    out_t = nc.dram_tensor("out", [NTOK, D], F32, kind="ExternalOutput").ap()
    spk = "ExternalOutput" if debug else "Internal"
    sp_xn = nc.dram_tensor("sp_xn", [NKT, 128, NTOK], BF16, kind=spk).ap()
    if debug:
        dbg_qkv = nc.dram_tensor("dbg_qkv", [128, 6144], BF16, kind="ExternalOutput").ap()
        dbg_G = nc.dram_tensor("dbg_G", [128, 2048], BF16, kind="ExternalOutput").ap()
        dbg_ssq = nc.dram_tensor("dbg_ssq", [1, 2048], F32, kind="ExternalOutput").ap()
    sp_ga = nc.dram_tensor("sp_ga", [NKT, 128, NTOK], BF16, kind=spk).ap()
    sp_wg = nc.dram_tensor("sp_wg", [NIT, 128, NKT * 128], BF16, kind=spk).ap()
    sp_wu = nc.dram_tensor("sp_wu", [NIT, 128, NKT * 128], BF16, kind=spk).ap()
    sp_wd = nc.dram_tensor("sp_wd", [NKT, 128, NIT * 128], BF16, kind=spk).ap()
    sp_wo = nc.dram_tensor("sp_wo", [NKT, 128, NKT * 128], BF16, kind=spk).ap()

    sb = SBAlloc(nc, 207 * 1024)
    A = sb.alloc
    g_ident = A("g_ident", [128, 128], F32, (0, 3))
    g_identb = A("g_identb", [128, 128], BF16, (0, 3))
    g_ones_f = A("g_ones_f", [128, 1], F32, (0, 3))
    g_ones_bf = A("g_ones_bf", [128, 1], BF16, (0, 3))
    g_ones1 = A("g_ones1", [1, 128], F32, (0, 3))
    g_qdidx = A("g_qdidx", [128, 256], F32, (0, 1))
    g_diffp = A("g_diffp", [128, 512], F32, (0, 1))
    g_maskp = A("g_maskp", [128, 512], F32, (0, 1))
    g_kdidx = A("g_kdidx", [128, 2], F32, (0, 1))
    g_srow = A("g_srow", [1, 16], F32, (0, 1))
    g_negs = A("g_negs", [128, 16], F32, (0, 1))
    g_nw = A("g_nw", [128, 16], F32, (0, 1))
    g_ssq = A("g_ssq", [1, 2048], F32, (0, 3))
    g_row = A("g_row", [1, 2048], F32, (2, 3))
    g_rstd = A("g_rstd", [128, 2048], F32, (2, 3))
    g_misc = A("g_misc", [128, 4], F32, (0, 3))
    g_gsc = A("g_gsc", [128, 64], F32, (0, 3))
    g_usc = A("g_usc", [128, 64], F32, (0, 3))
    g_dsc = A("g_dsc", [128, 16], F32, (0, 3))
    big = A("big", [128, 32768], BF16, (0, 3))
    p_x = A("p_x", [128, 4096], F32, (0, 0))
    p_sq = A("p_sq", [128, 2048], F32, (0, 0))
    p_xnt = A("p_xnt", [128, 4096], BF16, (0, 0))
    p_st = A("p_st", [128, 128], F32, (0, 0))
    p_scl = A("p_scl", [128, 512], F32, (0, 0))
    p_wland = A("p_wland", [128, 4096], F32, (1, 1))
    p_wT = A("p_wT", [128, 16384], BF16, (1, 1))
    p_qkv = A("p_qkv", [128, 8192], BF16, (1, 1))
    p_G = A("p_G", [128, 4096], BF16, (1, 1))
    p_GA = A("p_GA", [128, 2048], BF16, (1, 1))
    p_vtm = A("p_vtm", [128, 2048], BF16, (1, 1))
    p_kd = A("p_kd", [128, 2048], BF16, (1, 1))
    p_sb = A("p_sb", [128, 512], BF16, (1, 1))
    p_qd = A("p_qd", [128, 256], BF16, (1, 1))
    p_sqb = A("p_sqb", [128, 256], BF16, (1, 1))
    p_t1 = A("p_t1", [128, 256], F32, (1, 1))
    p_kv = A("p_kv", [128, 128], F32, (1, 1))
    p_kvbf = A("p_kvbf", [128, 256], BF16, (1, 1))
    p_ddt = A("p_ddt", [128, 512], F32, (1, 1))
    p_qdbc = A("p_qdbc", [128, 256], F32, (1, 1))
    p_kdt = A("p_kdt", [128, 2], F32, (1, 1))
    p_bd = A("p_bd", [128, 1], F32, (1, 1))
    p_iland = A("p_iland", [128, 2048], I32, (1, 1))
    p_pnat = A("p_pnat", [128, 2048], F32, (1, 1))
    p_ptr = A("p_ptr", [128, 2048], BF16, (1, 1))
    p3_xnc = A("p3_xnc", [128, 16384], BF16, (3, 3))
    p3_gw = A("p3_gw", [128, 8192], BF16, (3, 3))
    p3_gal = A("p3_gal", [128, 1024], BF16, (3, 3))
    p3_wg = A("p3_wg", [128, 4096], BF16, (3, 3))
    p3_wu = A("p3_wu", [128, 4096], BF16, (3, 3))
    p3_wd = A("p3_wd", [128, 8192], BF16, (3, 3))
    p3_wo = A("p3_wo", [128, 4096], BF16, (3, 3))
    p3_sg = A("p3_sg", [128, 1024], BF16, (3, 3))
    p3_s0 = A("p3_s0", [128, 1024], F32, (3, 3))
    p3_s1 = A("p3_s1", [128, 1024], F32, (3, 3))
    p3_xp = A("p3_xp", [128, 256], F32, (3, 3))
    p3_ob = A("p3_ob", [128, 256], F32, (3, 3))

    pb = [es.enter_context(nc.psum_tensor(f"pb{i}", [128, 512], F32)) for i in range(8)]

    em = Emitter(nc, es)
    S = em.sem
    s_bar = S("bar")
    s_ld = S("ld")
    s_sq = S("s_sq"); s_ln = S("s_ln"); s_xnt = S("s_xnt")
    s_tbl = S("s_tbl"); s_mmt = S("s_mmt")
    s_proj = S("s_proj"); s_pev = S("s_pev")
    s_kdd = S("s_kdd")
    s_S = S("s_S"); s_Sb = S("s_Sb"); s_O = S("s_O"); s_KV = S("s_KV")
    s_kvu = S("s_kvu"); s_kvb = S("s_kvb"); s_ssm = S("s_ssm"); s_ssa = S("s_ssa")
    s_act = S("s_act"); s_ga = S("s_ga")
    s_pcv = S("s_pcv"); s_ptp = S("s_ptp")
    s_rst = S("s_rst"); s_gw = S("s_gw")
    s_gps = S("s_gps"); s_ups = S("s_ups"); s_sg = S("s_sg"); s_h = S("s_h")
    s_at = S("s_at"); s_dn = S("s_dn"); s_s1 = S("s_s1")
    s_tp3 = S("s_tp3"); s_ob = S("s_ob")
    s_x = s_wld = s_pin = s_xnc = s_gal = s_wgld = s_wuld = s_wdld = None
    s_wold = s_xp = s_ow = s_xsp = s_gasp = s_xtp = s_psp = s_ktp = None
    s_wtp = s_wcst = None

    dma = em.dma
    op = em.op
    tgc = [0]
    TP_ROT = (6, 7, 2, 3)

    def emit_tp_group(srcs, dst_ap, srcbufs, dstbuf, bf=False, evac=None):
        bi = TP_ROT[tgc[0] % len(TP_ROT)]
        p = tgc[0] % 2
        tgc[0] += 1
        bank = pb[bi]
        bname = f"pb{bi}"
        bview = bank.ap().bitcast(BF16) if bf else bank.ap()
        ident = g_identb if bf else g_ident
        idb = "identb" if bf else "ident"
        n = len(srcs)

        def tps(e, srcs=srcs, bview=bview, ident=ident, n=n):
            for k in range(n):
                ins = e.transpose(out=bview[:, k * 128:(k + 1) * 128],
                                  in_=srcs[k], identity=ident[:, :])
            return ins
        op("tensor", tps, s_mmt, reads=list(srcbufs) + [idb], writes=[bname])
        if evac is None:
            evac = "vector" if p == 0 else "scalar"
        iv = bview[:, 0:n * 128]
        if evac == "vector":
            op("vector", lambda e, dst_ap=dst_ap, iv=iv: e.tensor_copy(
                out=dst_ap, in_=iv), s_ptp, reads=[bname], writes=[dstbuf])
        else:
            op("scalar", lambda e, dst_ap=dst_ap, iv=iv: e.copy(
                out=dst_ap, in_=iv), s_ptp, reads=[bname], writes=[dstbuf])

    dma("sync", g_ident[:, :], c_ident, s_ld, writes=["ident"], skey="_c")
    dma("sync", g_ones_f[:, :], c_ones_bf, s_ld, writes=["onesf"], skey="_c")
    op("vector", lambda e: e.tensor_copy(out=g_ones_bf[:, :], in_=g_ones_f[:, :]),
       s_tbl, reads=["onesf"], writes=["onesbf"])
    dma("sync", g_ones1[0:1, :], c_ones1, s_ld, writes=["ones1"], skey="_c")
    dma("sync", g_qdidx[:, :], c_qdidx, s_ld, writes=["qdidx"], skey="_c")
    dma("sync", g_diffp[:, 0:256], c_diffp[0:128, :], s_ld, writes=["diffp"], skey="_c")
    dma("sync", g_diffp[:, 256:512], c_diffp[128:256, :], s_ld, writes=["diffp"], skey="_c")
    dma("sync", g_maskp[:, 0:256], c_maskp[0:128, :], s_ld, writes=["maskp"], skey="_c")
    dma("sync", g_maskp[:, 256:512], c_maskp[128:256, :], s_ld, writes=["maskp"], skey="_c")
    dma("sync", g_kdidx[:, 0:1], c_kdidx[0:128, :], s_ld, writes=["kdidx"], skey="_c")
    dma("sync", g_kdidx[:, 1:2], c_kdidx[128:256, :], s_ld, writes=["kdidx"], skey="_c")
    dma("sync", g_srow[0:1, :], slope[None, :], s_ld, writes=["srow"], skey="_c")
    dma("sync", g_misc[:, :], c_misc, s_ld, writes=["misc"], skey="_c")
    _cs = em._dsems["_c"]
    for _b in ("ident", "onesf", "ones1", "qdidx", "diffp", "maskp", "kdidx",
               "srow", "misc"):
        em.R[_b] = (_cs, _cs.val, "dma")

    op("vector", lambda e: e.memset(g_ssq[0:1, :], 0.0), s_tbl, writes=["ssq"])
    op("vector", lambda e: e.tensor_copy(out=g_identb[:, :], in_=g_ident[:, :]),
       s_tbl, reads=["ident"], writes=["identb"])

    for tt in range(16):
        sl = (tt % 2) * 2048
        xb = f"xb{tt % 2}"
        st = p_st[:, tt * 8: tt * 8 + 8]
        stn = f"st{tt}"
        dma("sync", p_x[:, sl:sl + 2048], x_in[tt * 128:(tt + 1) * 128, :], s_x,
            writes=[xb])
        op("vector", lambda e, sl=sl, st=st: e.reduce_sum(
            out=st[:, 0:1], in_=p_x[:, sl:sl + 2048], axis=AX.X),
            s_ln, reads=[xb], writes=[stn])
        op("scalar", lambda e, sl=sl, st=st: e.activation(
            out=p_sq[:, :], in_=p_x[:, sl:sl + 2048], func=AF.Square,
            bias=0.0, scale=1.0),
            s_sq, reads=[xb], writes=["sqbuf"])
        op("vector", lambda e, st=st: e.reduce_sum(
            out=st[:, 1:2], in_=p_sq[:, :], axis=AX.X),
            s_ln, reads=["sqbuf"], writes=[stn + "q"])

        op("vector", lambda e, st=st: e.tensor_scalar_mul(
            out=st[:, 2:3], in0=st[:, 0:1], scalar1=1.0 / D),
            s_ln, reads=[stn], writes=[stn + "m"])
        op("vector", lambda e, st=st: e.tensor_scalar_mul(
            out=st[:, 3:4], in0=st[:, 1:2], scalar1=1.0 / D),
            s_ln, reads=[stn + "q"], writes=[stn + "m2"])
        op("vector", lambda e, st=st: e.scalar_tensor_tensor(
            out=st[:, 4:5], in0=st[:, 2:3], scalar=st[:, 2:3], in1=st[:, 3:4],
            op0=ALU.mult, op1=ALU.subtract),
            s_ln, reads=[stn + "m", stn + "m2"], writes=[stn + "v"])
        op("scalar", lambda e, st=st: e.activation(
            out=st[:, 5:6], in_=st[:, 4:5], func=AF.Sqrt,
            bias=g_misc[:, 0:1], scale=-1.0),
            s_sq, reads=[stn + "v", "misc"], writes=[stn + "s"])
        op("vector", lambda e, st=st: e.reciprocal(
            out=st[:, 6:7], in_=st[:, 5:6]),
            s_ln, reads=[stn + "s"], writes=[stn + "r"])
        op("vector", lambda e, st=st: e.tensor_scalar(
            out=st[:, 7:8], in0=st[:, 2:3], scalar1=st[:, 6:7], scalar2=-1.0,
            op0=ALU.mult, op1=ALU.mult),
            s_ln, reads=[stn + "m", stn + "r"], writes=[stn + "f"])
        xnb = f"xnt{tt % 2}"
        op("scalar", lambda e, sl=sl, st=st: e.activation(
            out=p_xnt[:, sl:sl + 2048], in_=p_x[:, sl:sl + 2048], func=AF.Identity,
            bias=st[:, 7:8], scale=st[:, 6:7]),
            s_xnt, reads=[xb, stn + "f", stn + "r"], writes=[xnb])
        big_r = big.ap().rearrange("p (dk c) -> p dk c", c=2048)
        for g in range(4):
            srcs = [p_xnt[:, sl + (4 * g + k) * 128: sl + (4 * g + k) * 128 + 128]
                    for k in range(4)]
            dst = big_r[:, 4 * g:4 * g + 4, tt * 128: tt * 128 + 128]
            emit_tp_group(srcs, dst, [xnb, "identb"],
                          f"xng{4 * g}", bf=True)
    for dk in range(16):
        dma("sync", sp_xn[dk], big[:, dk * 2048:(dk + 1) * 2048], s_xsp,
            reads=[f"xng{(dk // 4) * 4}"], writes=[f"spxn{dk}"], skey="spxn")

    op("vector", lambda e: e.tensor_scalar_mul(
        out=g_srow[0:1, :], in0=g_srow[0:1, :], scalar1=-1.0),
        s_tbl, reads=["srow"], writes=["srow"])
    op("tensor", lambda e: e.matmul(pb[6][:, 0:16], g_ones1[0:1, :],
                                    g_srow[0:1, 0:16], start=True, stop=True),
       s_mmt, reads=["ones1", "srow"], writes=["pb6"])
    op("vector", lambda e: e.tensor_copy(out=g_negs[:, :], in_=pb[6][:, 0:16]),
       s_tbl, reads=["pb6"], writes=["negs"])
    dma("sync", p_scl[0:64, 0:128], gate_sc.rearrange("(a b) -> a b", b=128), s_wld,
        writes=["scl"])
    dma("sync", p_scl[0:64, 128:256], up_sc.rearrange("(a b) -> a b", b=128), s_wld,
        writes=["scl"])
    dma("sync", p_scl[0:16, 256:384], down_sc.rearrange("(a b) -> a b", b=128), s_wld,
        writes=["scl"])
    dma("sync", p_scl[0:16, 384:512], norm_w.rearrange("(a b) -> a b", b=128), s_wld,
        writes=["scl"])

    def sc_tp(e):
        e.transpose(out=pb[6][:, 0:64], in_=p_scl[0:64, 0:128],
                    identity=g_ident[0:64, 0:64])
        e.transpose(out=pb[7][:, 0:64], in_=p_scl[0:64, 128:256],
                    identity=g_ident[0:64, 0:64])
        e.transpose(out=pb[6][:, 64:80], in_=p_scl[0:16, 256:384],
                    identity=g_ident[0:16, 0:16])
        return e.transpose(out=pb[7][:, 64:80], in_=p_scl[0:16, 384:512],
                           identity=g_ident[0:16, 0:16])
    op("tensor", sc_tp, s_mmt, reads=["scl", "ident"], writes=["pb6", "pb7"])

    def sc_ev(e):
        e.tensor_copy(out=g_gsc[:, :], in_=pb[6][:, 0:64])
        e.tensor_copy(out=g_usc[:, :], in_=pb[7][:, 0:64])
        e.tensor_copy(out=g_dsc[:, :], in_=pb[6][:, 64:80])
        return e.tensor_copy(out=g_nw[:, :], in_=pb[7][:, 64:80])
    op("vector", sc_ev, s_tbl, reads=["pb6", "pb7"], writes=["scales"])

    em.barrier(s_bar)

    if stop_after == 0:
        sb.finalize(es)
        em.replay()
        return nc, es

    prep_chunks = []
    for r in range(NIT):
        prep_chunks += [("wg", r, c) for c in range(2)]
    for r in range(NIT):
        prep_chunks += [("wu", r, c) for c in range(2)]
    for r in range(NKT):
        prep_chunks += [("wd", r, c) for c in range(8)]
    for r in range(NKT):
        prep_chunks += [("wo", r, c) for c in range(2)]
    pj = [0]

    def emit_prep_chunk():
        j = pj[0]
        if no_prep or j >= len(prep_chunks):
            pj[0] = len(prep_chunks)
            return
        pj[0] += 1
        kind, r, c = prep_chunks[j]
        par = j % 2
        src = {"wg": gate_w, "wu": up_w, "wd": down_w, "wo": w_out}[kind]
        dst = {"wg": sp_wg, "wu": sp_wu, "wd": sp_wd, "wo": sp_wo}[kind]
        ilb, pnb, ptb = f"il{par}", f"pn{par}", f"pt{par}"
        if kind == "wo":
            dma("gpsimd", p_pnat[:, par * 1024:(par + 1) * 1024],
                src[r * 128:(r + 1) * 128, c * 1024:(c + 1) * 1024], s_pin,
                writes=[pnb])
        else:
            conv = {"wg": "vector", "wu": "scalar", "wd": ("vector", "scalar")[j % 2]}[kind]
            dma("gpsimd", p_iland[:, par * 1024:(par + 1) * 1024],
                src[r * 128:(r + 1) * 128, c * 1024:(c + 1) * 1024], s_pin,
                writes=[ilb])
            if conv == "scalar":
                op(conv, lambda e, par=par: e.activation(
                    out=p_pnat[:, par * 1024:(par + 1) * 1024],
                    in_=p_iland[:, par * 1024:(par + 1) * 1024],
                    func=AF.Identity, bias=g_misc[:, 2:3], scale=1.0),
                    s_pcv, reads=[ilb, "misc"], writes=[pnb])
            else:
                op(conv, lambda e, par=par: e.tensor_single_scalar(
                    out=p_pnat[:, par * 1024:(par + 1) * 1024],
                    in_=p_iland[:, par * 1024:(par + 1) * 1024],
                    scalar=-128.0, op=ALU.add),
                    s_pcv, reads=[ilb], writes=[pnb])
        for g in range(2):
            srcs = [p_pnat[:, par * 1024 + g * 512 + k * 128:
                           par * 1024 + g * 512 + (k + 1) * 128] for k in range(4)]
            emit_tp_group(srcs, p_ptr[:, par * 1024 + g * 512:
                                      par * 1024 + (g + 1) * 512], [pnb], ptb)
        dma("sync", dst[r][:, c * 1024:(c + 1) * 1024],
            p_ptr[:, par * 1024:(par + 1) * 1024],
            s_psp, reads=[ptb], writes=[f"sp_{kind}"], skey=ptb)

    def emit_head_weight_row(h, ri):
        hp = h % 2
        rows = [(w_qkv, h), (w_qkv, NH + h), (w_qkv, 2 * NH + h), (w_gate, h)]
        srcw, r = rows[ri]
        gi = h * 4 + ri
        par = gi % 2
        lb = f"wl{par}"
        dma("sync", p_wland[:, par * 2048:(par + 1) * 2048],
            srcw[r * 128:(r + 1) * 128, :], s_wld, writes=[lb])
        base = (hp * 4 + ri) * 2048
        for g in range(4):
            srcs = [p_wland[:, par * 2048 + g * 512 + k * 128:
                            par * 2048 + g * 512 + (k + 1) * 128]
                    for k in range(4)]
            emit_tp_group(srcs, p_wT[:, base + g * 512: base + (g + 1) * 512],
                          [lb], f"wT{hp}")

    def emit_head_weights(h):
        for ri in range(4):
            emit_head_weight_row(h, ri)

    def emit_tables(h):
        ng = g_negs[:, h:h + 1]
        op("scalar", lambda e: e.activation(
            out=p_qdbc[:, :], in_=g_qdidx[:, :], func=AF.Exp, bias=0.0, scale=ng),
            s_act, reads=["negs"], writes=["qdbc"])
        op("scalar", lambda e: e.activation(
            out=p_ddt[:, :], in_=g_diffp[:, :], func=AF.Exp, bias=0.0, scale=ng),
            s_act, reads=["negs"], writes=["ddt"])
        op("scalar", lambda e: e.activation(
            out=p_kdt[:, :], in_=g_kdidx[:, :], func=AF.Exp, bias=0.0, scale=ng),
            s_act, reads=["negs"], writes=["kdt"])
        op("scalar", lambda e: e.activation(
            out=p_bd[:, :], in_=ng, func=AF.Exp, bias=0.0, scale=256.0),
            s_act, reads=["negs"], writes=["bd"])
        op("vector", lambda e: e.tensor_tensor(
            out=p_ddt[:, :], in0=p_ddt[:, :], in1=g_maskp[:, :], op=ALU.mult),
            s_tbl, reads=["ddt"], writes=["ddt"])

    def emit_proj_group(h, gidx):
        hp = h % 2
        r, mc = divmod(gidx, 4)
        bank = gidx % 2
        pbuf = pb[bank]
        wtb = f"wT{hp}"
        base = (hp * 4 + r) * 2048

        if r == 2:
            def mmv(e, base=base, mc=mc, pbuf=pbuf):
                for smt in range(4):
                    mt = mc * 4 + smt
                    for dk in range(16):
                        ins = e.matmul(
                            pbuf[:, smt * 128:(smt + 1) * 128],
                            big[:, dk * 2048 + mt * 128: dk * 2048 + mt * 128 + 128],
                            p_wT[:, base + dk * 128: base + dk * 128 + 128],
                            start=(dk == 0), stop=(dk == 15))
                return ins
            op("tensor", mmv, s_proj, reads=[wtb, "xnres"], writes=[f"pb{bank}"])
            dst = p_vtm[:, mc * 512:(mc + 1) * 512]
            op("scalar", lambda e, dst=dst, pbuf=pbuf: e.activation(
                out=dst, in_=pbuf[:, :], func=AF.Silu, bias=0.0, scale=1.0),
                s_pev, reads=[f"pb{bank}"], writes=["vtm"])
            return

        def mm(e, base=base, mc=mc, pbuf=pbuf):
            for dk in range(16):
                ins = e.matmul(pbuf[:, :],
                               p_wT[:, base + dk * 128: base + dk * 128 + 128],
                               big[:, dk * 2048 + mc * 512: dk * 2048 + mc * 512 + 512],
                               start=(dk == 0), stop=(dk == 15))
            return ins
        op("tensor", mm, s_proj, reads=[wtb, "xnres"], writes=[f"pb{bank}"])
        if r < 2:
            dst = p_qkv[:, (hp * 2 + r) * 2048 + mc * 512:
                        (hp * 2 + r) * 2048 + mc * 512 + 512]
            op("scalar", lambda e, dst=dst, pbuf=pbuf: e.activation(
                out=dst, in_=pbuf[:, :], func=AF.Silu, bias=0.0, scale=1.0),
                s_pev, reads=[f"pb{bank}"], writes=[f"qkv{hp}"])
        else:
            dst = p_G[:, hp * 2048 + mc * 512: hp * 2048 + mc * 512 + 512]
            op("scalar", lambda e, dst=dst, pbuf=pbuf: e.activation(
                out=dst, in_=pbuf[:, :], func=AF.Sigmoid, bias=0.0, scale=1.0),
                s_pev, reads=[f"pb{bank}"], writes=[f"G{hp}"])

    def emit_ktm(h):
        hp = h % 2
        kbase = (hp * 2 + 1) * 2048
        for g in range(4):
            bi = TP_ROT[tgc[0] % len(TP_ROT)]
            tgc[0] += 1
            bank = pb[bi]
            bname = f"pb{bi}"
            bview = bank.ap().bitcast(BF16)

            def tps(e, g=g, bview=bview, kbase=kbase):
                for k in range(4):
                    tt = 4 * g + k
                    ins = e.transpose(
                        out=bview[:, k * 128:(k + 1) * 128],
                        in_=p_qkv[:, kbase + tt * 128: kbase + tt * 128 + 128],
                        identity=g_identb[:, :])
                return ins
            op("tensor", tps, s_mmt, reads=[f"qkv{hp}", "identb"], writes=[bname])
            for k in range(4):
                tt = 4 * g + k
                op("vector", lambda e, tt=tt, k=k, bview=bview: e.tensor_scalar_mul(
                    out=p_kd[:, tt * 128:(tt + 1) * 128],
                    in0=bview[:, k * 128:(k + 1) * 128],
                    scalar1=p_kdt[:, (tt % 2):(tt % 2) + 1]),
                    s_kdd, reads=[bname, "kdt"], writes=["kdbuf"])

    def emit_scan_init(h):
        def z(e):
            e.memset(p_kv[:, :], 0.0)
            return e.memset(p_kvbf[:, :], 0.0)
        op("vector", z, s_kvu, writes=["kv", "kvbf0", "kvbf1"])

    def emit_scan_block(h, b):
        hp = h % 2
        qbase = hp * 2 * 2048
        kbase = (hp * 2 + 1) * 2048
        m0 = b * BLK

        def mmS(e):
            e.matmul(pb[2][:, 0:256],
                     p_qkv[:, kbase + m0: kbase + m0 + 128],
                     p_qkv[:, qbase + m0: qbase + m0 + 256], start=True, stop=True)
            return e.matmul(pb[2][:, 256:512],
                            p_qkv[:, kbase + m0 + 128: kbase + m0 + 256],
                            p_qkv[:, qbase + m0: qbase + m0 + 256],
                            start=True, stop=True)
        op("tensor", mmS, s_S, reads=[f"qkv{hp}"], writes=["pb2"])

        def mmKV(e, b=b):
            e.matmul(pb[4][:, 0:128], p_kd[:, 2 * b * 128: 2 * b * 128 + 128],
                     p_vtm[:, 2 * b * 128: 2 * b * 128 + 128], start=True, stop=False)
            return e.matmul(pb[4][:, 0:128],
                            p_kd[:, (2 * b + 1) * 128: (2 * b + 1) * 128 + 128],
                            p_vtm[:, (2 * b + 1) * 128: (2 * b + 1) * 128 + 128],
                            start=False, stop=True)
        op("tensor", mmKV, s_KV, reads=["kdbuf", "vtm"], writes=["pb4"])

        for kc in range(2):
            op("vector", lambda e, kc=kc: e.tensor_tensor(
                out=p_sb[:, kc * 256:(kc + 1) * 256],
                in0=pb[2][:, kc * 256:(kc + 1) * 256],
                in1=p_ddt[:, kc * 256:(kc + 1) * 256], op=ALU.mult),
                s_Sb, reads=["pb2", "ddt"], writes=["sb"])
        op("vector", lambda e: e.tensor_tensor(
            out=p_qd[:, :], in0=p_qkv[:, qbase + m0: qbase + m0 + 256],
            in1=p_qdbc[:, :], op=ALU.mult),
            s_Sb, reads=[f"qkv{hp}", "qdbc"], writes=["qd"])

        kvslot = (b + 1) % 2

        def mmO(e, b=b, kvslot=kvslot):
            e.matmul(pb[3][:, 0:256], p_vtm[:, 2 * b * 128: 2 * b * 128 + 128],
                     p_sb[:, 0:256], start=True, stop=False)
            e.matmul(pb[3][:, 0:256],
                     p_vtm[:, (2 * b + 1) * 128: (2 * b + 1) * 128 + 128],
                     p_sb[:, 256:512], start=False, stop=False)
            return e.matmul(pb[3][:, 0:256],
                            p_kvbf[:, kvslot * 128: kvslot * 128 + 128],
                            p_qd[:, :], start=False, stop=True)
        op("tensor", mmO, s_O, reads=["vtm", "sb", "qd", f"kvbf{kvslot}"],
           writes=["pb3"])

        op("vector", lambda e: e.scalar_tensor_tensor(
            out=p_kv[:, :], in0=p_kv[:, :], scalar=p_bd[:, 0:1],
            in1=pb[4][:, 0:128], op0=ALU.mult, op1=ALU.add),
            s_kvu, reads=["pb4", "bd", "kv"], writes=["kv"])
        kvs = b % 2
        op("scalar", lambda e, kvs=kvs: e.copy(
            out=p_kvbf[:, kvs * 128: kvs * 128 + 128], in_=p_kv[:, :]),
            s_kvb, reads=["kv"], writes=[f"kvbf{kvs}"])

        op("scalar", lambda e: e.activation(
            out=p_sqb[:, :], in_=pb[3][:, 0:256], func=AF.Square, bias=0.0, scale=1.0),
            s_act, reads=["pb3"], writes=["sqb"])
        op("scalar", lambda e, h=h: e.activation(
            out=p_t1[:, :], in_=pb[3][:, 0:256], func=AF.Identity, bias=0.0,
            scale=g_nw[:, h:h + 1]),
            s_act, reads=["pb3", "scales"], writes=["t1"])
        op("tensor", lambda e: e.matmul(
            pb[5][0:1, 0:256], g_ones_bf[:, 0:1], p_sqb[:, :], start=True, stop=True),
            s_ssm, reads=["sqb", "onesbf"], writes=["pb5"])
        op("vector", lambda e, m0=m0: e.tensor_tensor(
            out=g_ssq[0:1, m0:m0 + 256], in0=g_ssq[0:1, m0:m0 + 256],
            in1=pb[5][0:1, 0:256], op=ALU.add),
            s_ssa, reads=["pb5", "ssq"], writes=["ssq"])
        op("vector", lambda e, m0=m0, hp=hp: e.tensor_tensor(
            out=p_GA[:, m0:m0 + 256], in0=p_t1[:, :],
            in1=p_G[:, hp * 2048 + m0: hp * 2048 + m0 + 256], op=ALU.mult),
            s_ga, reads=["t1", f"G{hp}"], writes=["GA"])

    def emit_ga_spill(h):
        dma("gpsimd", sp_ga[h], p_GA[:, :], s_gasp, reads=["GA"],
            writes=["spga"])

    em.R["xnres"] = (s_bar, s_bar.val, "none")
    emit_head_weights(0)
    for _ in range(8):
        emit_prep_chunk()
    for h in range(NH):
        scan_h = h - 1
        for sub in range(8):
            emit_proj_group(h, 2 * sub)
            emit_prep_chunk()
            emit_proj_group(h, 2 * sub + 1)
            emit_prep_chunk()
            if scan_h >= 0:
                emit_scan_block(scan_h, sub)
            if sub < 4 and h + 1 < NH:
                emit_head_weight_row(h + 1, sub)
            emit_prep_chunk()
        if scan_h >= 0:
            emit_ga_spill(scan_h)
        if debug and scan_h == 0:
            dma("gpsimd", dbg_qkv, p_qkv[:, 0:6144], s_gasp, reads=["qkv0"],
                writes=["dbgq"])
            dma("gpsimd", dbg_G, p_G[:, 0:2048], s_gasp, reads=["G0"],
                writes=["dbgg"])
        emit_tables(h)
        emit_ktm(h)
        emit_scan_init(h)
    for b in range(NBLK):
        emit_scan_block(NH - 1, b)
        emit_prep_chunk()
    emit_ga_spill(NH - 1)
    while pj[0] < len(prep_chunks):
        emit_prep_chunk()

    em.barrier(s_bar)

    if stop_after == 1:
        if debug:
            dma("gpsimd", dbg_ssq, g_ssq[0:1, :], s_gasp, reads=["ssq"],
                writes=["dbgs"])
            em.barrier(s_bar)
        sb.finalize(es)
        em.replay()
        return nc, es

    op("scalar", lambda e: e.activation(
        out=g_row[0:1, :], in_=g_ssq[0:1, :], func=AF.Sqrt,
        bias=g_misc[0:1, 1:2], scale=1.0 / D),
        s_rst, reads=["ssq", "misc"], writes=["row"])
    op("vector", lambda e: e.reciprocal(out=g_row[0:1, :], in_=g_row[0:1, :]),
       s_rst, reads=["row"], writes=["row"])
    for mc in range(NMC):
        op("tensor", lambda e, mc=mc: e.matmul(
            pb[6][:, :], g_ones1[0:1, :], g_row[0:1, mc * 512:(mc + 1) * 512],
            start=True, stop=True),
            s_mmt, reads=["row", "ones1"], writes=["pb6"])
        op("vector", lambda e, mc=mc: e.tensor_copy(
            out=g_rstd[:, mc * 512:(mc + 1) * 512], in_=pb[6][:, :]),
            s_rst, reads=["rstd", "pb6"], writes=["rstd"])


    def emit_xnc(mc):
        par = mc % 2
        for dk in range(16):
            dma("sync", p3_xnc[:, par * 8192 + dk * 512: par * 8192 + dk * 512 + 512],
                sp_xn[dk][:, mc * 512:(mc + 1) * 512], s_xnc,
                reads=[f"spxn{dk}"], writes=[f"xnc{par}"])

    def emit_p3_tail(mc, hb):
        hp2 = hb % 2
        op("vector", lambda e, hb=hb, hp2=hp2: e.tensor_scalar_mul(
            out=p3_s0[:, hp2 * 512:(hp2 + 1) * 512], in0=pb[4 + hp2][:, :],
            scalar1=g_dsc[:, hb:hb + 1]),
            s_s1, reads=[f"pb{4 + hp2}", "scales"], writes=[f"s0_{hp2}"])
        op("vector", lambda e, hp2=hp2: e.tensor_tensor(
            out=p3_s1[:, hp2 * 512:(hp2 + 1) * 512],
            in0=p3_s0[:, hp2 * 512:(hp2 + 1) * 512],
            in1=pb[2 + hp2][:, :], op=ALU.add),
            s_s1, reads=[f"s0_{hp2}", f"pb{2 + hp2}"], writes=[f"s1_{hp2}"])
        for j in range(4):
            jp = j % 2
            mt = mc * 4 + j
            dma("sync", p3_xp[:, jp * 128:(jp + 1) * 128],
                x_in[mt * 128:(mt + 1) * 128, hb * 128:(hb + 1) * 128], s_xp,
                writes=[f"xp{jp}"])
            op("tensor", lambda e, hp2=hp2, j=j, jp=jp: e.transpose(
                out=pb[6 + jp][:, 0:128],
                in_=p3_s1[:, hp2 * 512 + j * 128: hp2 * 512 + j * 128 + 128],
                identity=g_ident[:, :]),
                s_tp3, reads=[f"s1_{hp2}", "ident"], writes=[f"pb{6 + jp}"])
            op("vector", lambda e, jp=jp: e.tensor_tensor(
                out=p3_ob[:, jp * 128:(jp + 1) * 128], in0=pb[6 + jp][:, 0:128],
                in1=p3_xp[:, jp * 128:(jp + 1) * 128], op=ALU.add),
                s_ob, reads=[f"pb{6 + jp}", f"xp{jp}"], writes=[f"ob{jp}"])
            dma("sync", out_t[mt * 128:(mt + 1) * 128, hb * 128:(hb + 1) * 128],
                p3_ob[:, jp * 128:(jp + 1) * 128], s_ow,
                reads=[f"ob{jp}"], writes=["hbm_out"], skey=f"ob{jp}")

    emit_xnc(0)
    for mc in range(NMC):
        par = mc % 2
        xncb = f"xnc{par}"
        for ft in range(16):
            gp = ft % 2
            dma("scalar", p3_gal[:, gp * 512:(gp + 1) * 512],
                sp_ga[ft][:, mc * 512:(mc + 1) * 512], s_gal,
                reads=["spga"], writes=[f"gal{gp}"])
            op("vector", lambda e, ft=ft, gp=gp, mc=mc: e.tensor_tensor(
                out=p3_gw[:, ft * 512:(ft + 1) * 512],
                in0=p3_gal[:, gp * 512:(gp + 1) * 512],
                in1=g_rstd[:, mc * 512:(mc + 1) * 512], op=ALU.mult),
                s_gw, reads=[f"gal{gp}", "rstd"], writes=["gw"])
        for it in range(NIT):
            if it == 40 and mc + 1 < NMC:
                emit_xnc(mc + 1)
            wp = it % 2
            wgb, wub = f"wg{wp}", f"wu{wp}"
            dma("gpsimd", p3_wg[:, wp * 2048:(wp + 1) * 2048],
                sp_wg[it], s_wgld, writes=[wgb])
            dma("gpsimd", p3_wu[:, wp * 2048:(wp + 1) * 2048],
                sp_wu[it], s_wuld, writes=[wub])

            def mmg(e, wp=wp, par=par):
                for dk in range(16):
                    ins = e.matmul(pb[0][:, :],
                                   p3_wg[:, wp * 2048 + dk * 128: wp * 2048 + dk * 128 + 128],
                                   p3_xnc[:, par * 8192 + dk * 512: par * 8192 + dk * 512 + 512],
                                   start=(dk == 0), stop=(dk == 15))
                return ins
            op("tensor", mmg, s_gps, reads=[wgb, xncb], writes=["pb0"])

            def mmu(e, wp=wp, par=par):
                for dk in range(16):
                    ins = e.matmul(pb[1][:, :],
                                   p3_wu[:, wp * 2048 + dk * 128: wp * 2048 + dk * 128 + 128],
                                   p3_xnc[:, par * 8192 + dk * 512: par * 8192 + dk * 512 + 512],
                                   start=(dk == 0), stop=(dk == 15))
                return ins
            op("tensor", mmu, s_ups, reads=[wub, xncb], writes=["pb1"])
            sgp = it % 2
            op("scalar", lambda e, it=it, sgp=sgp: e.activation(
                out=p3_sg[:, sgp * 512:(sgp + 1) * 512], in_=pb[0][:, :],
                func=AF.Silu, bias=0.0, scale=g_gsc[:, it:it + 1]),
                s_sg, reads=["pb0", "scales"], writes=[f"sg{sgp}"])
            op("vector", lambda e, it=it, sgp=sgp: e.scalar_tensor_tensor(
                out=big[:, it * 512:(it + 1) * 512], in0=pb[1][:, :],
                scalar=g_usc[:, it:it + 1],
                in1=p3_sg[:, sgp * 512:(sgp + 1) * 512],
                op0=ALU.mult, op1=ALU.mult),
                s_h, reads=["pb1", f"sg{sgp}", "scales"], writes=["hres"])
        for hb in range(NKT):
            hp2 = hb % 2
            wob = f"wo{hp2}"
            dma("sync", p3_wo[:, hp2 * 2048:(hp2 + 1) * 2048],
                sp_wo[hb], s_wold, writes=[wob])

            def mmat(e, hp2=hp2):
                for ft in range(16):
                    ins = e.matmul(pb[2 + hp2][:, :],
                                   p3_wo[:, hp2 * 2048 + ft * 128: hp2 * 2048 + ft * 128 + 128],
                                   p3_gw[:, ft * 512:(ft + 1) * 512],
                                   start=(ft == 0), stop=(ft == 15))
                return ins
            op("tensor", mmat, s_at, reads=[wob, "gw"], writes=[f"pb{2 + hp2}"])
            for q in range(4):
                qp = (hb * 4 + q) % 4
                wdb = f"wd{qp}"
                dma("scalar", p3_wd[:, qp * 2048:(qp + 1) * 2048],
                    sp_wd[hb][:, q * 2048:(q + 1) * 2048],
                    s_wdld, writes=[wdb])

                def mmdn(e, hp2=hp2, q=q, qp=qp):
                    for j in range(16):
                        it = q * 16 + j
                        ins = e.matmul(pb[4 + hp2][:, :],
                                       p3_wd[:, qp * 2048 + j * 128: qp * 2048 + j * 128 + 128],
                                       big[:, it * 512:(it + 1) * 512],
                                       start=(it == 0), stop=(it == 63))
                    return ins
                op("tensor", mmdn, s_dn, reads=[wdb, "hres"], writes=[f"pb{4 + hp2}"])
            if hb > 0:
                emit_p3_tail(mc, hb - 1)
        emit_p3_tail(mc, NKT - 1)

    em.barrier(s_bar)

    sb.finalize(es)
    em.replay()
    return nc, es


_CONSTS = None


def _consts():
    global _CONSTS
    if _CONSTS is None:
        m = np.arange(256)
        k = np.arange(256)
        diffp = np.maximum(m[None, :] - k[:, None], 0).astype(np.float32)
        maskp = (m[None, :] >= k[:, None]).astype(np.float32)
        _CONSTS = {
            "c_ident": np.eye(128, dtype=np.float32),
            "c_ones_bf": np.ones((128, 1), dtype=np.float32),
            "c_ones1": np.ones((1, 128), dtype=np.float32),
            "c_qdidx": np.broadcast_to((m + 1).astype(np.float32), (128, 256)).copy(),
            "c_diffp": diffp,
            "c_maskp": maskp,
            "c_kdidx": (255 - k).astype(np.float32).reshape(256, 1),
            "c_misc": np.broadcast_to(
                np.array([1e-5, 1e-6, -128.0, 0.0], dtype=np.float32), (128, 4)
            ).copy(),
        }
    return _CONSTS


def kernel(L0_attn_hidden_states, L0_attn_slope_rate, L0_attn_w_qkv,
           L0_attn_w_gate, L0_attn_w_out, L0_attn_norm_w,
           L0_ffn_gate_w, L0_ffn_gate_scale, L0_ffn_up_w, L0_ffn_up_scale,
           L0_ffn_down_w, L0_ffn_down_scale):
    x = np.ascontiguousarray(np.asarray(L0_attn_hidden_states, dtype=np.float32))
    shared = {
        "slope": np.asarray(L0_attn_slope_rate, dtype=np.float32),
        "w_qkv": np.asarray(L0_attn_w_qkv, dtype=np.float32),
        "w_gate": np.asarray(L0_attn_w_gate, dtype=np.float32),
        "w_out": np.asarray(L0_attn_w_out, dtype=np.float32),
        "norm_w": np.asarray(L0_attn_norm_w, dtype=np.float32),
        "gate_w": np.asarray(L0_ffn_gate_w, dtype=np.int32),
        "gate_sc": np.asarray(L0_ffn_gate_scale, dtype=np.float32),
        "up_w": np.asarray(L0_ffn_up_w, dtype=np.int32),
        "up_sc": np.asarray(L0_ffn_up_scale, dtype=np.float32),
        "down_w": np.asarray(L0_ffn_down_w, dtype=np.int32),
        "down_sc": np.asarray(L0_ffn_down_scale, dtype=np.float32),
    }
    shared.update(_consts())
    nc, es = build_program()
    in_maps = [dict(shared, x=x[c]) for c in range(8)]
    try:
        res = run_bass_kernel_spmd(nc, in_maps, core_ids=list(range(8)))
    finally:
        es.close()
    return np.stack([np.asarray(r["out"]) for r in res.results], axis=0)


# version 11
# speedup vs baseline: 1.1683x; 1.0951x over previous
import numpy as np
import ml_dtypes
import concourse.bass as bass
import concourse.mybir as mybir
from concourse.bass_utils import run_bass_kernel_spmd

F32 = mybir.dt.float32
BF16 = mybir.dt.bfloat16
I32 = mybir.dt.int32
U8 = mybir.dt.uint8
AF = mybir.ActivationFunctionType
ALU = mybir.AluOpType
AX = mybir.AxisListType

D = 2048
NTOK = 2048
NH = 16
DH = 128
NI = 8192
NKT = D // 128
NIT = NI // 128
MC = 512
NMC = NTOK // MC
BLK = 256
NBLK = NTOK // BLK
ENGS = ("sync", "scalar", "vector", "tensor", "gpsimd")
ENFORCE_SAME_ENGINE = True


class Sem:
    __slots__ = ("h", "name", "val", "_cm")

    def __init__(self, nc, es, name):
        self.h = es.enter_context(nc.semaphore(name))
        self.name = name
        self.val = 0


class Emitter:
    def __init__(self, nc, es):
        self.nc = nc
        self.q = {e: [] for e in ENGS}
        self.lastw = {e: {} for e in ENGS}
        self.R = {}
        self.W = {}
        self.sems = {}
        self._dsems = {}
        self._es = es

    def sem(self, name):
        s = Sem(self.nc, self._es, name)
        self.sems[name] = s
        return s

    def _wait(self, eng, sem, val):
        if val <= 0:
            return
        if self.lastw[eng].get(sem.name, 0) >= val:
            return
        self.lastw[eng][sem.name] = val
        self.q[eng].append(lambda e, s=sem.h, v=val: e.wait_ge(s, v))

    def op(self, eng, build, sem, reads=(), writes=(), amt=1, extra_waits=()):
        for b in reads:
            m = self.R.get(b)
            if m is not None:
                s, v, peng = m
                if peng != eng or ENFORCE_SAME_ENGINE:
                    self._wait(eng, s, v)
        for b in writes:
            m = self.R.get(b)
            if m is not None:
                s, v, peng = m
                if peng != eng or ENFORCE_SAME_ENGINE:
                    self._wait(eng, s, v)
            for (s, v, peng) in self.W.get(b, {}).values():
                if peng != eng or ENFORCE_SAME_ENGINE:
                    self._wait(eng, s, v)
        for (s, v) in extra_waits:
            self._wait(eng, s, v)
        sem.val += amt
        v = sem.val
        self.q[eng].append(lambda e, b=build, s=sem.h, a=amt: b(e).then_inc(s, a))
        for b in writes:
            self.R[b] = (sem, v, eng)
            self.W[b] = {}
        for b in reads:
            self.W.setdefault(b, {})[sem.name] = (sem, v, eng)
        return v

    def dma(self, eng, out, in_, sem, reads=(), writes=(), tp=False, extra_waits=(),
            skey=None):
        if skey is None:
            skey = writes[0] if writes else (reads[0] if reads else None)
        if skey is not None:
            s = self._dsems.get(skey)
            if s is None:
                s = self.sem("d_" + skey)
                self._dsems[skey] = s
            sem = s
        return self.op(
            eng,
            lambda e, o=out, i=in_, t=tp: e.dma_start(out=o, in_=i, transpose=t),
            sem, reads=reads, writes=writes, amt=16, extra_waits=extra_waits)

    def barrier(self, bsem):
        for e in ENGS:
            for s in self.sems.values():
                if s is not bsem:
                    self._wait(e, s, s.val)
            bsem.val += 1
            self.q[e].append(lambda eng, s=bsem.h: eng.sem_inc(s, 1))
        tot = bsem.val
        for e in ENGS:
            self._wait(e, bsem, tot)

    def replay(self):
        with self.nc.Block() as block:
            @block.sync
            def _(e):
                for f in self.q["sync"]:
                    f(e)

            @block.scalar
            def _(e):
                for f in self.q["scalar"]:
                    f(e)

            @block.vector
            def _(e):
                for f in self.q["vector"]:
                    f(e)

            @block.tensor
            def _(e):
                for f in self.q["tensor"]:
                    f(e)

            @block.gpsimd
            def _(e):
                for f in self.q["gpsimd"]:
                    f(e)


class SBAlloc:

    def __init__(self, nc, total_bytes):
        self.nc = nc
        self.total = total_bytes
        self.items = []
        self.regions = []

    def alloc(self, name, shape, dtype, phases=(0, 3)):
        per_part = int(np.prod(shape[1:])) * mybir.dt.size(dtype)
        per_part = (per_part + 63) // 64 * 64
        p0, p1 = phases
        taken = sorted(
            (s, e) for (s, e, q0, q1) in self.regions if not (q1 < p0 or q0 > p1))
        addr = 0
        for (s, e) in taken:
            if addr + per_part <= s:
                break
            addr = max(addr, e)
        if addr + per_part > self.total:
            raise MemoryError(f"SBUF overflow: {name} needs {per_part}B at {addr}")
        h = self.nc.virtual_sbuf_tensor(name, list(shape), dtype)
        self.items.append((h, addr))
        self.regions.append((addr, addr + per_part, p0, p1))
        return h

    def finalize(self, es):
        backing = es.enter_context(self.nc.sbuf_tensor("arena", [128, self.total], U8))
        base = self.nc.lookup_mloc(backing).addr
        for h, addr in self.items:
            self.nc._tensor(h.name, h.shape, h.dtype, type="SB")
            mloc = self.nc.lookup_mloc(h)
            mloc.allocated = not self.nc.target_bir_lowering
            mloc.addr = base + addr
        return backing


def build_program(debug=False, stop_after=3, no_prep=False):
    from contextlib import ExitStack
    es = ExitStack()
    nc = bass.Bass("TRN2", target_bir_lowering=False)

    x_in = nc.dram_tensor("x", [NTOK, D], F32, kind="ExternalInput").ap()
    slope = nc.dram_tensor("slope", [NH], F32, kind="ExternalInput").ap()
    w_qkv = nc.dram_tensor("w_qkv", [3 * D, D], F32, kind="ExternalInput").ap()
    w_gate = nc.dram_tensor("w_gate", [D, D], F32, kind="ExternalInput").ap()
    w_out = nc.dram_tensor("w_out", [D, D], F32, kind="ExternalInput").ap()
    norm_w = nc.dram_tensor("norm_w", [D], F32, kind="ExternalInput").ap()
    gate_w = nc.dram_tensor("gate_w", [NI, D], I32, kind="ExternalInput").ap()
    gate_sc = nc.dram_tensor("gate_sc", [NI], F32, kind="ExternalInput").ap()
    up_w = nc.dram_tensor("up_w", [NI, D], I32, kind="ExternalInput").ap()
    up_sc = nc.dram_tensor("up_sc", [NI], F32, kind="ExternalInput").ap()
    down_w = nc.dram_tensor("down_w", [D, NI], I32, kind="ExternalInput").ap()
    down_sc = nc.dram_tensor("down_sc", [D], F32, kind="ExternalInput").ap()
    c_ident = nc.dram_tensor("c_ident", [128, 128], F32, kind="ExternalInput").ap()
    c_ones_bf = nc.dram_tensor("c_ones_bf", [128, 1], F32, kind="ExternalInput").ap()
    c_ones1 = nc.dram_tensor("c_ones1", [1, 128], F32, kind="ExternalInput").ap()
    c_qdidx = nc.dram_tensor("c_qdidx", [128, 256], F32, kind="ExternalInput").ap()
    c_diffp = nc.dram_tensor("c_diffp", [256, 256], F32, kind="ExternalInput").ap()
    c_maskp = nc.dram_tensor("c_maskp", [256, 256], F32, kind="ExternalInput").ap()
    c_kdidx = nc.dram_tensor("c_kdidx", [256, 1], F32, kind="ExternalInput").ap()
    c_misc = nc.dram_tensor("c_misc", [128, 4], F32, kind="ExternalInput").ap()
    out_t = nc.dram_tensor("out", [NTOK, D], F32, kind="ExternalOutput").ap()
    spk = "ExternalOutput" if debug else "Internal"
    sp_xn = nc.dram_tensor("sp_xn", [NKT, 128, NTOK], BF16, kind=spk).ap()
    if debug:
        dbg_qkv = nc.dram_tensor("dbg_qkv", [128, 6144], BF16, kind="ExternalOutput").ap()
        dbg_G = nc.dram_tensor("dbg_G", [128, 2048], BF16, kind="ExternalOutput").ap()
        dbg_ssq = nc.dram_tensor("dbg_ssq", [1, 2048], F32, kind="ExternalOutput").ap()
    sp_ga = nc.dram_tensor("sp_ga", [NKT, 128, NTOK], BF16, kind=spk).ap()
    sp_wg = nc.dram_tensor("sp_wg", [NIT, 128, NKT * 128], BF16, kind=spk).ap()
    sp_wu = nc.dram_tensor("sp_wu", [NIT, 128, NKT * 128], BF16, kind=spk).ap()
    sp_wd = nc.dram_tensor("sp_wd", [NKT, 128, NIT * 128], BF16, kind=spk).ap()
    sp_wo = nc.dram_tensor("sp_wo", [NKT, 128, NKT * 128], BF16, kind=spk).ap()

    sb = SBAlloc(nc, 207 * 1024)
    A = sb.alloc
    g_ident = A("g_ident", [128, 128], F32, (0, 3))
    g_identb = A("g_identb", [128, 128], BF16, (0, 3))
    g_ones_f = A("g_ones_f", [128, 1], F32, (0, 3))
    g_ones_bf = A("g_ones_bf", [128, 1], BF16, (0, 3))
    g_ones1 = A("g_ones1", [1, 128], F32, (0, 3))
    g_qdidx = A("g_qdidx", [128, 256], F32, (0, 1))
    g_diffp = A("g_diffp", [128, 512], F32, (0, 1))
    g_maskp = A("g_maskp", [128, 512], F32, (0, 1))
    g_kdidx = A("g_kdidx", [128, 2], F32, (0, 1))
    g_srow = A("g_srow", [1, 16], F32, (0, 1))
    g_negs = A("g_negs", [128, 16], F32, (0, 1))
    g_nw = A("g_nw", [128, 16], F32, (0, 1))
    g_ssq = A("g_ssq", [1, 2048], F32, (0, 3))
    g_row = A("g_row", [1, 2048], F32, (2, 3))
    g_rstd = A("g_rstd", [128, 2048], F32, (2, 3))
    g_misc = A("g_misc", [128, 4], F32, (0, 3))
    g_gsc = A("g_gsc", [128, 64], F32, (0, 3))
    g_usc = A("g_usc", [128, 64], F32, (0, 3))
    g_dsc = A("g_dsc", [128, 16], F32, (0, 3))
    big = A("big", [128, 32768], BF16, (0, 3))
    p_x = A("p_x", [128, 4096], F32, (0, 0))
    p_sq = A("p_sq", [128, 2048], F32, (0, 0))
    p_xnt = A("p_xnt", [128, 4096], BF16, (0, 0))
    p_st = A("p_st", [128, 128], F32, (0, 0))
    p_scl = A("p_scl", [128, 512], F32, (0, 0))
    p_wland = A("p_wland", [128, 4096], F32, (1, 1))
    p_wT = A("p_wT", [128, 16384], BF16, (1, 1))
    p_qkv = A("p_qkv", [128, 8192], BF16, (1, 1))
    p_G = A("p_G", [128, 4096], BF16, (1, 1))
    p_GA = A("p_GA", [128, 2048], BF16, (1, 1))
    p_vtm = A("p_vtm", [128, 2048], BF16, (1, 1))
    p_kd = A("p_kd", [128, 2048], BF16, (1, 1))
    p_sb = A("p_sb", [128, 512], BF16, (1, 1))
    p_qd = A("p_qd", [128, 256], BF16, (1, 1))
    p_sqb = A("p_sqb", [128, 256], BF16, (1, 1))
    p_t1 = A("p_t1", [128, 256], F32, (1, 1))
    p_kv = A("p_kv", [128, 128], F32, (1, 1))
    p_kvbf = A("p_kvbf", [128, 256], BF16, (1, 1))
    p_ddt = A("p_ddt", [128, 512], F32, (1, 1))
    p_qdbc = A("p_qdbc", [128, 256], F32, (1, 1))
    p_kdt = A("p_kdt", [128, 2], F32, (1, 1))
    p_bd = A("p_bd", [128, 1], F32, (1, 1))
    p_iland = A("p_iland", [128, 2048], I32, (1, 1))
    p_pnat = A("p_pnat", [128, 2048], F32, (1, 1))
    p_ptr = A("p_ptr", [128, 2048], BF16, (1, 1))
    p3_xnc = A("p3_xnc", [128, 16384], BF16, (3, 3))
    p3_gw = A("p3_gw", [128, 8192], BF16, (3, 3))
    p3_gal = A("p3_gal", [128, 1024], BF16, (3, 3))
    p3_wg = A("p3_wg", [128, 4096], BF16, (3, 3))
    p3_wu = A("p3_wu", [128, 4096], BF16, (3, 3))
    p3_wd = A("p3_wd", [128, 8192], BF16, (3, 3))
    p3_wo = A("p3_wo", [128, 4096], BF16, (3, 3))
    p3_sg = A("p3_sg", [128, 1024], BF16, (3, 3))
    p3_s0 = A("p3_s0", [128, 1024], F32, (3, 3))
    p3_s1 = A("p3_s1", [128, 1024], F32, (3, 3))
    p3_xp = A("p3_xp", [128, 256], F32, (3, 3))
    p3_ob = A("p3_ob", [128, 256], F32, (3, 3))

    pb = [es.enter_context(nc.psum_tensor(f"pb{i}", [128, 512], F32)) for i in range(8)]

    em = Emitter(nc, es)
    S = em.sem
    s_bar = S("bar")
    s_ld = S("ld")
    s_sq = S("s_sq"); s_ln = S("s_ln"); s_xnt = S("s_xnt")
    s_tbl = S("s_tbl"); s_mmt = S("s_mmt")
    s_proj = S("s_proj"); s_pev = S("s_pev")
    s_kdd = S("s_kdd")
    s_S = S("s_S"); s_Sb = S("s_Sb"); s_O = S("s_O"); s_KV = S("s_KV")
    s_kvu = S("s_kvu"); s_kvb = S("s_kvb"); s_ssm = S("s_ssm"); s_ssa = S("s_ssa")
    s_act = S("s_act"); s_ga = S("s_ga")
    s_pcv = S("s_pcv"); s_ptp = S("s_ptp")
    s_rst = S("s_rst"); s_gw = S("s_gw")
    s_gps = S("s_gps"); s_ups = S("s_ups"); s_sg = S("s_sg"); s_h = S("s_h")
    s_at = S("s_at"); s_dn = S("s_dn"); s_s1 = S("s_s1")
    s_tp3 = S("s_tp3"); s_ob = S("s_ob")
    s_x = s_wld = s_pin = s_xnc = s_gal = s_wgld = s_wuld = s_wdld = None
    s_wold = s_xp = s_ow = s_xsp = s_gasp = s_xtp = s_psp = s_ktp = None
    s_wtp = s_wcst = None

    dma = em.dma
    op = em.op
    tgc = [0]
    TP_ROT = (6, 7, 2, 3)

    def emit_tp_group(srcs, dst_ap, srcbufs, dstbuf, bf=False, evac=None):
        bi = TP_ROT[tgc[0] % len(TP_ROT)]
        p = tgc[0] % 2
        tgc[0] += 1
        bank = pb[bi]
        bname = f"pb{bi}"
        bview = bank.ap().bitcast(BF16) if bf else bank.ap()
        ident = g_identb if bf else g_ident
        idb = "identb" if bf else "ident"
        n = len(srcs)

        def tps(e, srcs=srcs, bview=bview, ident=ident, n=n):
            for k in range(n):
                ins = e.transpose(out=bview[:, k * 128:(k + 1) * 128],
                                  in_=srcs[k], identity=ident[:, :])
            return ins
        op("tensor", tps, s_mmt, reads=list(srcbufs) + [idb], writes=[bname])
        if evac is None:
            evac = "vector" if p == 0 else "scalar"
        iv = bview[:, 0:n * 128]
        if evac == "vector":
            op("vector", lambda e, dst_ap=dst_ap, iv=iv: e.tensor_copy(
                out=dst_ap, in_=iv), s_ptp, reads=[bname], writes=[dstbuf])
        else:
            op("scalar", lambda e, dst_ap=dst_ap, iv=iv: e.copy(
                out=dst_ap, in_=iv), s_ptp, reads=[bname], writes=[dstbuf])

    dma("sync", g_ident[:, :], c_ident, s_ld, writes=["ident"], skey="_c")
    dma("sync", g_ones_f[:, :], c_ones_bf, s_ld, writes=["onesf"], skey="_c")
    op("vector", lambda e: e.tensor_copy(out=g_ones_bf[:, :], in_=g_ones_f[:, :]),
       s_tbl, reads=["onesf"], writes=["onesbf"])
    dma("sync", g_ones1[0:1, :], c_ones1, s_ld, writes=["ones1"], skey="_c")
    dma("sync", g_qdidx[:, :], c_qdidx, s_ld, writes=["qdidx"], skey="_c")
    dma("sync", g_diffp[:, 0:256], c_diffp[0:128, :], s_ld, writes=["diffp"], skey="_c")
    dma("sync", g_diffp[:, 256:512], c_diffp[128:256, :], s_ld, writes=["diffp"], skey="_c")
    dma("sync", g_maskp[:, 0:256], c_maskp[0:128, :], s_ld, writes=["maskp"], skey="_c")
    dma("sync", g_maskp[:, 256:512], c_maskp[128:256, :], s_ld, writes=["maskp"], skey="_c")
    dma("sync", g_kdidx[:, 0:1], c_kdidx[0:128, :], s_ld, writes=["kdidx"], skey="_c")
    dma("sync", g_kdidx[:, 1:2], c_kdidx[128:256, :], s_ld, writes=["kdidx"], skey="_c")
    dma("sync", g_srow[0:1, :], slope[None, :], s_ld, writes=["srow"], skey="_c")
    dma("sync", g_misc[:, :], c_misc, s_ld, writes=["misc"], skey="_c")
    _cs = em._dsems["_c"]
    for _b in ("ident", "onesf", "ones1", "qdidx", "diffp", "maskp", "kdidx",
               "srow", "misc"):
        em.R[_b] = (_cs, _cs.val, "dma")

    op("vector", lambda e: e.memset(g_ssq[0:1, :], 0.0), s_tbl, writes=["ssq"])
    op("vector", lambda e: e.tensor_copy(out=g_identb[:, :], in_=g_ident[:, :]),
       s_tbl, reads=["ident"], writes=["identb"])

    for tt in range(16):
        sl = (tt % 2) * 2048
        xb = f"xb{tt % 2}"
        st = p_st[:, tt * 8: tt * 8 + 8]
        stn = f"st{tt}"
        dma("sync", p_x[:, sl:sl + 2048], x_in[tt * 128:(tt + 1) * 128, :], s_x,
            writes=[xb])
        op("vector", lambda e, sl=sl, st=st: e.reduce_sum(
            out=st[:, 0:1], in_=p_x[:, sl:sl + 2048], axis=AX.X),
            s_ln, reads=[xb], writes=[stn])
        op("scalar", lambda e, sl=sl, st=st: e.activation(
            out=p_sq[:, :], in_=p_x[:, sl:sl + 2048], func=AF.Square,
            bias=0.0, scale=1.0),
            s_sq, reads=[xb], writes=["sqbuf"])
        op("vector", lambda e, st=st: e.reduce_sum(
            out=st[:, 1:2], in_=p_sq[:, :], axis=AX.X),
            s_ln, reads=["sqbuf"], writes=[stn + "q"])

        op("vector", lambda e, st=st: e.tensor_scalar_mul(
            out=st[:, 2:3], in0=st[:, 0:1], scalar1=1.0 / D),
            s_ln, reads=[stn], writes=[stn + "m"])
        op("vector", lambda e, st=st: e.tensor_scalar_mul(
            out=st[:, 3:4], in0=st[:, 1:2], scalar1=1.0 / D),
            s_ln, reads=[stn + "q"], writes=[stn + "m2"])
        op("vector", lambda e, st=st: e.scalar_tensor_tensor(
            out=st[:, 4:5], in0=st[:, 2:3], scalar=st[:, 2:3], in1=st[:, 3:4],
            op0=ALU.mult, op1=ALU.subtract),
            s_ln, reads=[stn + "m", stn + "m2"], writes=[stn + "v"])
        op("scalar", lambda e, st=st: e.activation(
            out=st[:, 5:6], in_=st[:, 4:5], func=AF.Sqrt,
            bias=g_misc[:, 0:1], scale=-1.0),
            s_sq, reads=[stn + "v", "misc"], writes=[stn + "s"])
        op("vector", lambda e, st=st: e.reciprocal(
            out=st[:, 6:7], in_=st[:, 5:6]),
            s_ln, reads=[stn + "s"], writes=[stn + "r"])
        op("vector", lambda e, st=st: e.tensor_scalar(
            out=st[:, 7:8], in0=st[:, 2:3], scalar1=st[:, 6:7], scalar2=-1.0,
            op0=ALU.mult, op1=ALU.mult),
            s_ln, reads=[stn + "m", stn + "r"], writes=[stn + "f"])
        xnb = f"xnt{tt % 2}"
        op("scalar", lambda e, sl=sl, st=st: e.activation(
            out=p_xnt[:, sl:sl + 2048], in_=p_x[:, sl:sl + 2048], func=AF.Identity,
            bias=st[:, 7:8], scale=st[:, 6:7]),
            s_xnt, reads=[xb, stn + "f", stn + "r"], writes=[xnb])
        big_r = big.ap().rearrange("p (dk c) -> p dk c", c=2048)
        for g in range(4):
            srcs = [p_xnt[:, sl + (4 * g + k) * 128: sl + (4 * g + k) * 128 + 128]
                    for k in range(4)]
            dst = big_r[:, 4 * g:4 * g + 4, tt * 128: tt * 128 + 128]
            emit_tp_group(srcs, dst, [xnb, "identb"],
                          f"xng{4 * g}", bf=True)
    for dk in range(16):
        dma("sync", sp_xn[dk], big[:, dk * 2048:(dk + 1) * 2048], s_xsp,
            reads=[f"xng{(dk // 4) * 4}"], writes=[f"spxn{dk}"], skey="spxn")

    op("vector", lambda e: e.tensor_scalar_mul(
        out=g_srow[0:1, :], in0=g_srow[0:1, :], scalar1=-1.0),
        s_tbl, reads=["srow"], writes=["srow"])
    op("tensor", lambda e: e.matmul(pb[6][:, 0:16], g_ones1[0:1, :],
                                    g_srow[0:1, 0:16], start=True, stop=True),
       s_mmt, reads=["ones1", "srow"], writes=["pb6"])
    op("vector", lambda e: e.tensor_copy(out=g_negs[:, :], in_=pb[6][:, 0:16]),
       s_tbl, reads=["pb6"], writes=["negs"])
    dma("sync", p_scl[0:64, 0:128], gate_sc.rearrange("(a b) -> a b", b=128), s_wld,
        writes=["scl"])
    dma("sync", p_scl[0:64, 128:256], up_sc.rearrange("(a b) -> a b", b=128), s_wld,
        writes=["scl"])
    dma("sync", p_scl[0:16, 256:384], down_sc.rearrange("(a b) -> a b", b=128), s_wld,
        writes=["scl"])
    dma("sync", p_scl[0:16, 384:512], norm_w.rearrange("(a b) -> a b", b=128), s_wld,
        writes=["scl"])

    def sc_tp(e):
        e.transpose(out=pb[6][:, 0:64], in_=p_scl[0:64, 0:128],
                    identity=g_ident[0:64, 0:64])
        e.transpose(out=pb[7][:, 0:64], in_=p_scl[0:64, 128:256],
                    identity=g_ident[0:64, 0:64])
        e.transpose(out=pb[6][:, 64:80], in_=p_scl[0:16, 256:384],
                    identity=g_ident[0:16, 0:16])
        return e.transpose(out=pb[7][:, 64:80], in_=p_scl[0:16, 384:512],
                           identity=g_ident[0:16, 0:16])
    op("tensor", sc_tp, s_mmt, reads=["scl", "ident"], writes=["pb6", "pb7"])

    def sc_ev(e):
        e.tensor_copy(out=g_gsc[:, :], in_=pb[6][:, 0:64])
        e.tensor_copy(out=g_usc[:, :], in_=pb[7][:, 0:64])
        e.tensor_copy(out=g_dsc[:, :], in_=pb[6][:, 64:80])
        return e.tensor_copy(out=g_nw[:, :], in_=pb[7][:, 64:80])
    op("vector", sc_ev, s_tbl, reads=["pb6", "pb7"], writes=["scales"])

    em.barrier(s_bar)

    if stop_after == 0:
        sb.finalize(es)
        em.replay()
        return nc, es

    prep_chunks = []
    for r in range(NIT):
        prep_chunks += [("wg", r, c) for c in range(2)]
    for r in range(NIT):
        prep_chunks += [("wu", r, c) for c in range(2)]
    for r in range(NKT):
        prep_chunks += [("wd", r, c) for c in range(8)]
    for r in range(NKT):
        prep_chunks += [("wo", r, c) for c in range(2)]
    pj = [0]

    def prep_stageA(j):
        kind, r, c = prep_chunks[j]
        par = j % 2
        srcw = {"wg": gate_w, "wu": up_w, "wd": down_w, "wo": w_out}[kind]
        ilb, pnb = f"il{par}", f"pn{par}"
        if kind == "wo":
            dma("gpsimd", p_pnat[:, par * 1024:(par + 1) * 1024],
                srcw[r * 128:(r + 1) * 128, c * 1024:(c + 1) * 1024], s_pin,
                writes=[pnb])
            return
        conv = {"wg": "vector", "wu": "scalar",
                "wd": ("vector", "scalar")[j % 2]}[kind]
        dma("gpsimd", p_iland[:, par * 1024:(par + 1) * 1024],
            srcw[r * 128:(r + 1) * 128, c * 1024:(c + 1) * 1024], s_pin,
            writes=[ilb])
        if conv == "scalar":
            op(conv, lambda e, par=par: e.activation(
                out=p_pnat[:, par * 1024:(par + 1) * 1024],
                in_=p_iland[:, par * 1024:(par + 1) * 1024],
                func=AF.Identity, bias=g_misc[:, 2:3], scale=1.0),
                s_pcv, reads=[ilb, "misc"], writes=[pnb])
        else:
            op(conv, lambda e, par=par: e.tensor_single_scalar(
                out=p_pnat[:, par * 1024:(par + 1) * 1024],
                in_=p_iland[:, par * 1024:(par + 1) * 1024],
                scalar=-128.0, op=ALU.add),
                s_pcv, reads=[ilb], writes=[pnb])

    def prep_stageB(j):
        kind, r, c = prep_chunks[j]
        par = j % 2
        dst = {"wg": sp_wg, "wu": sp_wu, "wd": sp_wd, "wo": sp_wo}[kind]
        pnb, ptb = f"pn{par}", f"pt{par}"
        for g in range(2):
            srcs = [p_pnat[:, par * 1024 + g * 512 + k * 128:
                           par * 1024 + g * 512 + (k + 1) * 128] for k in range(4)]
            emit_tp_group(srcs, p_ptr[:, par * 1024 + g * 512:
                                      par * 1024 + (g + 1) * 512], [pnb], ptb)
        dma("sync", dst[r][:, c * 1024:(c + 1) * 1024],
            p_ptr[:, par * 1024:(par + 1) * 1024],
            s_psp, reads=[ptb], writes=[f"sp_{kind}"], skey=ptb)

    def emit_prep_chunk():
        j = pj[0]
        if no_prep or j >= len(prep_chunks):
            pj[0] = len(prep_chunks)
            return
        pj[0] += 1
        if j == 0:
            prep_stageA(0)
        if j + 1 < len(prep_chunks):
            prep_stageA(j + 1)
        prep_stageB(j)

    def emit_head_weight_row(h, ri):
        hp = h % 2
        rows = [(w_qkv, h), (w_qkv, NH + h), (w_qkv, 2 * NH + h), (w_gate, h)]
        srcw, r = rows[ri]
        gi = h * 4 + ri
        par = gi % 2
        lb = f"wl{par}"
        dma("sync", p_wland[:, par * 2048:(par + 1) * 2048],
            srcw[r * 128:(r + 1) * 128, :], s_wld, writes=[lb])
        base = (hp * 4 + ri) * 2048
        for g in range(4):
            srcs = [p_wland[:, par * 2048 + g * 512 + k * 128:
                            par * 2048 + g * 512 + (k + 1) * 128]
                    for k in range(4)]
            emit_tp_group(srcs, p_wT[:, base + g * 512: base + (g + 1) * 512],
                          [lb], f"wT{hp}")

    def emit_head_weights(h):
        for ri in range(4):
            emit_head_weight_row(h, ri)

    def emit_tables(h):
        ng = g_negs[:, h:h + 1]
        op("scalar", lambda e: e.activation(
            out=p_qdbc[:, :], in_=g_qdidx[:, :], func=AF.Exp, bias=0.0, scale=ng),
            s_act, reads=["negs"], writes=["qdbc"])
        op("scalar", lambda e: e.activation(
            out=p_ddt[:, :], in_=g_diffp[:, :], func=AF.Exp, bias=0.0, scale=ng),
            s_act, reads=["negs"], writes=["ddt"])
        op("scalar", lambda e: e.activation(
            out=p_kdt[:, :], in_=g_kdidx[:, :], func=AF.Exp, bias=0.0, scale=ng),
            s_act, reads=["negs"], writes=["kdt"])
        op("scalar", lambda e: e.activation(
            out=p_bd[:, :], in_=ng, func=AF.Exp, bias=0.0, scale=256.0),
            s_act, reads=["negs"], writes=["bd"])
        op("vector", lambda e: e.tensor_tensor(
            out=p_ddt[:, :], in0=p_ddt[:, :], in1=g_maskp[:, :], op=ALU.mult),
            s_tbl, reads=["ddt"], writes=["ddt"])

    def emit_proj_group(h, gidx):
        hp = h % 2
        r, mc = divmod(gidx, 4)
        bank = gidx % 2
        pbuf = pb[bank]
        wtb = f"wT{hp}"
        base = (hp * 4 + r) * 2048

        if r == 2:
            def mmv(e, base=base, mc=mc, pbuf=pbuf):
                for smt in range(4):
                    mt = mc * 4 + smt
                    for dk in range(16):
                        ins = e.matmul(
                            pbuf[:, smt * 128:(smt + 1) * 128],
                            big[:, dk * 2048 + mt * 128: dk * 2048 + mt * 128 + 128],
                            p_wT[:, base + dk * 128: base + dk * 128 + 128],
                            start=(dk == 0), stop=(dk == 15))
                return ins
            op("tensor", mmv, s_proj, reads=[wtb, "xnres"], writes=[f"pb{bank}"])
            dst = p_vtm[:, mc * 512:(mc + 1) * 512]
            op("scalar", lambda e, dst=dst, pbuf=pbuf: e.activation(
                out=dst, in_=pbuf[:, :], func=AF.Silu, bias=0.0, scale=1.0),
                s_pev, reads=[f"pb{bank}"], writes=["vtm"])
            return

        def mm(e, base=base, mc=mc, pbuf=pbuf):
            for dk in range(16):
                ins = e.matmul(pbuf[:, :],
                               p_wT[:, base + dk * 128: base + dk * 128 + 128],
                               big[:, dk * 2048 + mc * 512: dk * 2048 + mc * 512 + 512],
                               start=(dk == 0), stop=(dk == 15))
            return ins
        op("tensor", mm, s_proj, reads=[wtb, "xnres"], writes=[f"pb{bank}"])
        if r < 2:
            dst = p_qkv[:, (hp * 2 + r) * 2048 + mc * 512:
                        (hp * 2 + r) * 2048 + mc * 512 + 512]
            op("scalar", lambda e, dst=dst, pbuf=pbuf: e.activation(
                out=dst, in_=pbuf[:, :], func=AF.Silu, bias=0.0, scale=1.0),
                s_pev, reads=[f"pb{bank}"], writes=[f"qkv{hp}"])
        else:
            dst = p_G[:, hp * 2048 + mc * 512: hp * 2048 + mc * 512 + 512]
            op("scalar", lambda e, dst=dst, pbuf=pbuf: e.activation(
                out=dst, in_=pbuf[:, :], func=AF.Sigmoid, bias=0.0, scale=1.0),
                s_pev, reads=[f"pb{bank}"], writes=[f"G{hp}"])

    def emit_ktm(h):
        hp = h % 2
        kbase = (hp * 2 + 1) * 2048
        for g in range(4):
            bi = TP_ROT[tgc[0] % len(TP_ROT)]
            tgc[0] += 1
            bank = pb[bi]
            bname = f"pb{bi}"
            bview = bank.ap().bitcast(BF16)

            def tps(e, g=g, bview=bview, kbase=kbase):
                for k in range(4):
                    tt = 4 * g + k
                    ins = e.transpose(
                        out=bview[:, k * 128:(k + 1) * 128],
                        in_=p_qkv[:, kbase + tt * 128: kbase + tt * 128 + 128],
                        identity=g_identb[:, :])
                return ins
            op("tensor", tps, s_mmt, reads=[f"qkv{hp}", "identb"], writes=[bname])
            for k in range(4):
                tt = 4 * g + k
                op("vector", lambda e, tt=tt, k=k, bview=bview: e.tensor_scalar_mul(
                    out=p_kd[:, tt * 128:(tt + 1) * 128],
                    in0=bview[:, k * 128:(k + 1) * 128],
                    scalar1=p_kdt[:, (tt % 2):(tt % 2) + 1]),
                    s_kdd, reads=[bname, "kdt"], writes=["kdbuf"])

    def emit_scan_init(h):
        def z(e):
            e.memset(p_kv[:, :], 0.0)
            return e.memset(p_kvbf[:, :], 0.0)
        op("vector", z, s_kvu, writes=["kv", "kvbf0", "kvbf1"])

    def emit_scan_block(h, b):
        hp = h % 2
        qbase = hp * 2 * 2048
        kbase = (hp * 2 + 1) * 2048
        m0 = b * BLK

        def mmS(e):
            e.matmul(pb[2][:, 0:256],
                     p_qkv[:, kbase + m0: kbase + m0 + 128],
                     p_qkv[:, qbase + m0: qbase + m0 + 256], start=True, stop=True)
            return e.matmul(pb[2][:, 256:512],
                            p_qkv[:, kbase + m0 + 128: kbase + m0 + 256],
                            p_qkv[:, qbase + m0: qbase + m0 + 256],
                            start=True, stop=True)
        op("tensor", mmS, s_S, reads=[f"qkv{hp}"], writes=["pb2"])

        def mmKV(e, b=b):
            e.matmul(pb[4][:, 0:128], p_kd[:, 2 * b * 128: 2 * b * 128 + 128],
                     p_vtm[:, 2 * b * 128: 2 * b * 128 + 128], start=True, stop=False)
            return e.matmul(pb[4][:, 0:128],
                            p_kd[:, (2 * b + 1) * 128: (2 * b + 1) * 128 + 128],
                            p_vtm[:, (2 * b + 1) * 128: (2 * b + 1) * 128 + 128],
                            start=False, stop=True)
        op("tensor", mmKV, s_KV, reads=["kdbuf", "vtm"], writes=["pb4"])

        for kc in range(2):
            op("vector", lambda e, kc=kc: e.tensor_tensor(
                out=p_sb[:, kc * 256:(kc + 1) * 256],
                in0=pb[2][:, kc * 256:(kc + 1) * 256],
                in1=p_ddt[:, kc * 256:(kc + 1) * 256], op=ALU.mult),
                s_Sb, reads=["pb2", "ddt"], writes=["sb"])
        op("vector", lambda e: e.tensor_tensor(
            out=p_qd[:, :], in0=p_qkv[:, qbase + m0: qbase + m0 + 256],
            in1=p_qdbc[:, :], op=ALU.mult),
            s_Sb, reads=[f"qkv{hp}", "qdbc"], writes=["qd"])

        kvslot = (b + 1) % 2

        def mmO(e, b=b, kvslot=kvslot):
            e.matmul(pb[3][:, 0:256], p_vtm[:, 2 * b * 128: 2 * b * 128 + 128],
                     p_sb[:, 0:256], start=True, stop=False)
            e.matmul(pb[3][:, 0:256],
                     p_vtm[:, (2 * b + 1) * 128: (2 * b + 1) * 128 + 128],
                     p_sb[:, 256:512], start=False, stop=False)
            return e.matmul(pb[3][:, 0:256],
                            p_kvbf[:, kvslot * 128: kvslot * 128 + 128],
                            p_qd[:, :], start=False, stop=True)
        op("tensor", mmO, s_O, reads=["vtm", "sb", "qd", f"kvbf{kvslot}"],
           writes=["pb3"])

        op("vector", lambda e: e.scalar_tensor_tensor(
            out=p_kv[:, :], in0=p_kv[:, :], scalar=p_bd[:, 0:1],
            in1=pb[4][:, 0:128], op0=ALU.mult, op1=ALU.add),
            s_kvu, reads=["pb4", "bd", "kv"], writes=["kv"])
        kvs = b % 2
        op("scalar", lambda e, kvs=kvs: e.copy(
            out=p_kvbf[:, kvs * 128: kvs * 128 + 128], in_=p_kv[:, :]),
            s_kvb, reads=["kv"], writes=[f"kvbf{kvs}"])

        op("scalar", lambda e: e.activation(
            out=p_sqb[:, :], in_=pb[3][:, 0:256], func=AF.Square, bias=0.0, scale=1.0),
            s_act, reads=["pb3"], writes=["sqb"])
        op("scalar", lambda e, h=h: e.activation(
            out=p_t1[:, :], in_=pb[3][:, 0:256], func=AF.Identity, bias=0.0,
            scale=g_nw[:, h:h + 1]),
            s_act, reads=["pb3", "scales"], writes=["t1"])
        op("tensor", lambda e: e.matmul(
            pb[5][0:1, 0:256], g_ones_bf[:, 0:1], p_sqb[:, :], start=True, stop=True),
            s_ssm, reads=["sqb", "onesbf"], writes=["pb5"])
        op("vector", lambda e, m0=m0: e.tensor_tensor(
            out=g_ssq[0:1, m0:m0 + 256], in0=g_ssq[0:1, m0:m0 + 256],
            in1=pb[5][0:1, 0:256], op=ALU.add),
            s_ssa, reads=["pb5", "ssq"], writes=["ssq"])
        op("vector", lambda e, m0=m0, hp=hp: e.tensor_tensor(
            out=p_GA[:, m0:m0 + 256], in0=p_t1[:, :],
            in1=p_G[:, hp * 2048 + m0: hp * 2048 + m0 + 256], op=ALU.mult),
            s_ga, reads=["t1", f"G{hp}"], writes=["GA"])

    def emit_ga_spill(h):
        dma("gpsimd", sp_ga[h], p_GA[:, :], s_gasp, reads=["GA"],
            writes=["spga"])

    em.R["xnres"] = (s_bar, s_bar.val, "none")
    emit_head_weights(0)
    for _ in range(8):
        emit_prep_chunk()
    for h in range(NH):
        scan_h = h - 1
        for sub in range(8):
            emit_proj_group(h, 2 * sub)
            emit_prep_chunk()
            emit_proj_group(h, 2 * sub + 1)
            emit_prep_chunk()
            if scan_h >= 0:
                emit_scan_block(scan_h, sub)
            if sub < 4 and h + 1 < NH:
                emit_head_weight_row(h + 1, sub)
            emit_prep_chunk()
        if scan_h >= 0:
            emit_ga_spill(scan_h)
        if debug and scan_h == 0:
            dma("gpsimd", dbg_qkv, p_qkv[:, 0:6144], s_gasp, reads=["qkv0"],
                writes=["dbgq"])
            dma("gpsimd", dbg_G, p_G[:, 0:2048], s_gasp, reads=["G0"],
                writes=["dbgg"])
        emit_tables(h)
        emit_ktm(h)
        emit_scan_init(h)
    for b in range(NBLK):
        emit_scan_block(NH - 1, b)
        emit_prep_chunk()
    emit_ga_spill(NH - 1)
    while pj[0] < len(prep_chunks):
        emit_prep_chunk()

    em.barrier(s_bar)

    if stop_after == 1:
        if debug:
            dma("gpsimd", dbg_ssq, g_ssq[0:1, :], s_gasp, reads=["ssq"],
                writes=["dbgs"])
            em.barrier(s_bar)
        sb.finalize(es)
        em.replay()
        return nc, es

    op("scalar", lambda e: e.activation(
        out=g_row[0:1, :], in_=g_ssq[0:1, :], func=AF.Sqrt,
        bias=g_misc[0:1, 1:2], scale=1.0 / D),
        s_rst, reads=["ssq", "misc"], writes=["row"])
    op("vector", lambda e: e.reciprocal(out=g_row[0:1, :], in_=g_row[0:1, :]),
       s_rst, reads=["row"], writes=["row"])
    for mc in range(NMC):
        op("tensor", lambda e, mc=mc: e.matmul(
            pb[6][:, :], g_ones1[0:1, :], g_row[0:1, mc * 512:(mc + 1) * 512],
            start=True, stop=True),
            s_mmt, reads=["row", "ones1"], writes=["pb6"])
        op("vector", lambda e, mc=mc: e.tensor_copy(
            out=g_rstd[:, mc * 512:(mc + 1) * 512], in_=pb[6][:, :]),
            s_rst, reads=["rstd", "pb6"], writes=["rstd"])


    def emit_xnc(mc):
        par = mc % 2
        for dk in range(16):
            dma("sync", p3_xnc[:, par * 8192 + dk * 512: par * 8192 + dk * 512 + 512],
                sp_xn[dk][:, mc * 512:(mc + 1) * 512], s_xnc,
                reads=[f"spxn{dk}"], writes=[f"xnc{par}"])

    def emit_p3_tail(mc, hb):
        hp2 = hb % 2
        op("vector", lambda e, hb=hb, hp2=hp2: e.tensor_scalar_mul(
            out=p3_s0[:, hp2 * 512:(hp2 + 1) * 512], in0=pb[4 + hp2][:, :],
            scalar1=g_dsc[:, hb:hb + 1]),
            s_s1, reads=[f"pb{4 + hp2}", "scales"], writes=[f"s0_{hp2}"])
        op("vector", lambda e, hp2=hp2: e.tensor_tensor(
            out=p3_s1[:, hp2 * 512:(hp2 + 1) * 512],
            in0=p3_s0[:, hp2 * 512:(hp2 + 1) * 512],
            in1=pb[2 + hp2][:, :], op=ALU.add),
            s_s1, reads=[f"s0_{hp2}", f"pb{2 + hp2}"], writes=[f"s1_{hp2}"])
        for j in range(4):
            jp = j % 2
            mt = mc * 4 + j
            dma("sync", p3_xp[:, jp * 128:(jp + 1) * 128],
                x_in[mt * 128:(mt + 1) * 128, hb * 128:(hb + 1) * 128], s_xp,
                writes=[f"xp{jp}"])
            op("tensor", lambda e, hp2=hp2, j=j, jp=jp: e.transpose(
                out=pb[6 + jp][:, 0:128],
                in_=p3_s1[:, hp2 * 512 + j * 128: hp2 * 512 + j * 128 + 128],
                identity=g_ident[:, :]),
                s_tp3, reads=[f"s1_{hp2}", "ident"], writes=[f"pb{6 + jp}"])
            op("vector", lambda e, jp=jp: e.tensor_tensor(
                out=p3_ob[:, jp * 128:(jp + 1) * 128], in0=pb[6 + jp][:, 0:128],
                in1=p3_xp[:, jp * 128:(jp + 1) * 128], op=ALU.add),
                s_ob, reads=[f"pb{6 + jp}", f"xp{jp}"], writes=[f"ob{jp}"])
            dma("sync", out_t[mt * 128:(mt + 1) * 128, hb * 128:(hb + 1) * 128],
                p3_ob[:, jp * 128:(jp + 1) * 128], s_ow,
                reads=[f"ob{jp}"], writes=["hbm_out"], skey=f"ob{jp}")

    emit_xnc(0)
    for mc in range(NMC):
        par = mc % 2
        xncb = f"xnc{par}"
        for ft in range(16):
            gp = ft % 2
            dma("scalar", p3_gal[:, gp * 512:(gp + 1) * 512],
                sp_ga[ft][:, mc * 512:(mc + 1) * 512], s_gal,
                reads=["spga"], writes=[f"gal{gp}"])
            op("vector", lambda e, ft=ft, gp=gp, mc=mc: e.tensor_tensor(
                out=p3_gw[:, ft * 512:(ft + 1) * 512],
                in0=p3_gal[:, gp * 512:(gp + 1) * 512],
                in1=g_rstd[:, mc * 512:(mc + 1) * 512], op=ALU.mult),
                s_gw, reads=[f"gal{gp}", "rstd"], writes=["gw"])
        for it in range(NIT):
            if it == 40 and mc + 1 < NMC:
                emit_xnc(mc + 1)
            wp = it % 2
            wgb, wub = f"wg{wp}", f"wu{wp}"
            dma("gpsimd", p3_wg[:, wp * 2048:(wp + 1) * 2048],
                sp_wg[it], s_wgld, writes=[wgb])
            dma("gpsimd", p3_wu[:, wp * 2048:(wp + 1) * 2048],
                sp_wu[it], s_wuld, writes=[wub])

            def mmg(e, wp=wp, par=par):
                for dk in range(16):
                    ins = e.matmul(pb[0][:, :],
                                   p3_wg[:, wp * 2048 + dk * 128: wp * 2048 + dk * 128 + 128],
                                   p3_xnc[:, par * 8192 + dk * 512: par * 8192 + dk * 512 + 512],
                                   start=(dk == 0), stop=(dk == 15))
                return ins
            op("tensor", mmg, s_gps, reads=[wgb, xncb], writes=["pb0"])

            def mmu(e, wp=wp, par=par):
                for dk in range(16):
                    ins = e.matmul(pb[1][:, :],
                                   p3_wu[:, wp * 2048 + dk * 128: wp * 2048 + dk * 128 + 128],
                                   p3_xnc[:, par * 8192 + dk * 512: par * 8192 + dk * 512 + 512],
                                   start=(dk == 0), stop=(dk == 15))
                return ins
            op("tensor", mmu, s_ups, reads=[wub, xncb], writes=["pb1"])
            sgp = it % 2
            op("scalar", lambda e, it=it, sgp=sgp: e.activation(
                out=p3_sg[:, sgp * 512:(sgp + 1) * 512], in_=pb[0][:, :],
                func=AF.Silu, bias=0.0, scale=g_gsc[:, it:it + 1]),
                s_sg, reads=["pb0", "scales"], writes=[f"sg{sgp}"])
            op("vector", lambda e, it=it, sgp=sgp: e.scalar_tensor_tensor(
                out=big[:, it * 512:(it + 1) * 512], in0=pb[1][:, :],
                scalar=g_usc[:, it:it + 1],
                in1=p3_sg[:, sgp * 512:(sgp + 1) * 512],
                op0=ALU.mult, op1=ALU.mult),
                s_h, reads=["pb1", f"sg{sgp}", "scales"], writes=["hres"])
        for hb in range(NKT):
            hp2 = hb % 2
            wob = f"wo{hp2}"
            dma("sync", p3_wo[:, hp2 * 2048:(hp2 + 1) * 2048],
                sp_wo[hb], s_wold, writes=[wob])

            def mmat(e, hp2=hp2):
                for ft in range(16):
                    ins = e.matmul(pb[2 + hp2][:, :],
                                   p3_wo[:, hp2 * 2048 + ft * 128: hp2 * 2048 + ft * 128 + 128],
                                   p3_gw[:, ft * 512:(ft + 1) * 512],
                                   start=(ft == 0), stop=(ft == 15))
                return ins
            op("tensor", mmat, s_at, reads=[wob, "gw"], writes=[f"pb{2 + hp2}"])
            for q in range(4):
                qp = (hb * 4 + q) % 4
                wdb = f"wd{qp}"
                dma("scalar", p3_wd[:, qp * 2048:(qp + 1) * 2048],
                    sp_wd[hb][:, q * 2048:(q + 1) * 2048],
                    s_wdld, writes=[wdb])

                def mmdn(e, hp2=hp2, q=q, qp=qp):
                    for j in range(16):
                        it = q * 16 + j
                        ins = e.matmul(pb[4 + hp2][:, :],
                                       p3_wd[:, qp * 2048 + j * 128: qp * 2048 + j * 128 + 128],
                                       big[:, it * 512:(it + 1) * 512],
                                       start=(it == 0), stop=(it == 63))
                    return ins
                op("tensor", mmdn, s_dn, reads=[wdb, "hres"], writes=[f"pb{4 + hp2}"])
            if hb > 0:
                emit_p3_tail(mc, hb - 1)
        emit_p3_tail(mc, NKT - 1)

    em.barrier(s_bar)

    sb.finalize(es)
    em.replay()
    return nc, es


_CONSTS = None


def _consts():
    global _CONSTS
    if _CONSTS is None:
        m = np.arange(256)
        k = np.arange(256)
        diffp = np.maximum(m[None, :] - k[:, None], 0).astype(np.float32)
        maskp = (m[None, :] >= k[:, None]).astype(np.float32)
        _CONSTS = {
            "c_ident": np.eye(128, dtype=np.float32),
            "c_ones_bf": np.ones((128, 1), dtype=np.float32),
            "c_ones1": np.ones((1, 128), dtype=np.float32),
            "c_qdidx": np.broadcast_to((m + 1).astype(np.float32), (128, 256)).copy(),
            "c_diffp": diffp,
            "c_maskp": maskp,
            "c_kdidx": (255 - k).astype(np.float32).reshape(256, 1),
            "c_misc": np.broadcast_to(
                np.array([1e-5, 1e-6, -128.0, 0.0], dtype=np.float32), (128, 4)
            ).copy(),
        }
    return _CONSTS


def kernel(L0_attn_hidden_states, L0_attn_slope_rate, L0_attn_w_qkv,
           L0_attn_w_gate, L0_attn_w_out, L0_attn_norm_w,
           L0_ffn_gate_w, L0_ffn_gate_scale, L0_ffn_up_w, L0_ffn_up_scale,
           L0_ffn_down_w, L0_ffn_down_scale):
    x = np.ascontiguousarray(np.asarray(L0_attn_hidden_states, dtype=np.float32))
    shared = {
        "slope": np.asarray(L0_attn_slope_rate, dtype=np.float32),
        "w_qkv": np.asarray(L0_attn_w_qkv, dtype=np.float32),
        "w_gate": np.asarray(L0_attn_w_gate, dtype=np.float32),
        "w_out": np.asarray(L0_attn_w_out, dtype=np.float32),
        "norm_w": np.asarray(L0_attn_norm_w, dtype=np.float32),
        "gate_w": np.asarray(L0_ffn_gate_w, dtype=np.int32),
        "gate_sc": np.asarray(L0_ffn_gate_scale, dtype=np.float32),
        "up_w": np.asarray(L0_ffn_up_w, dtype=np.int32),
        "up_sc": np.asarray(L0_ffn_up_scale, dtype=np.float32),
        "down_w": np.asarray(L0_ffn_down_w, dtype=np.int32),
        "down_sc": np.asarray(L0_ffn_down_scale, dtype=np.float32),
    }
    shared.update(_consts())
    nc, es = build_program()
    in_maps = [dict(shared, x=x[c]) for c in range(8)]
    try:
        res = run_bass_kernel_spmd(nc, in_maps, core_ids=list(range(8)))
    finally:
        es.close()
    return np.stack([np.asarray(r["out"]) for r in res.results], axis=0)


# version 12
# speedup vs baseline: 1.1797x; 1.0097x over previous
import numpy as np
import ml_dtypes
import concourse.bass as bass
import concourse.mybir as mybir
from concourse.bass_utils import run_bass_kernel_spmd

F32 = mybir.dt.float32
BF16 = mybir.dt.bfloat16
I32 = mybir.dt.int32
U8 = mybir.dt.uint8
AF = mybir.ActivationFunctionType
ALU = mybir.AluOpType
AX = mybir.AxisListType

D = 2048
NTOK = 2048
NH = 16
DH = 128
NI = 8192
NKT = D // 128
NIT = NI // 128
MC = 512
NMC = NTOK // MC
BLK = 256
NBLK = NTOK // BLK
ENGS = ("sync", "scalar", "vector", "tensor", "gpsimd")
ENFORCE_SAME_ENGINE = True


class Sem:
    __slots__ = ("h", "name", "val", "_cm")

    def __init__(self, nc, es, name):
        self.h = es.enter_context(nc.semaphore(name))
        self.name = name
        self.val = 0


class Emitter:
    def __init__(self, nc, es):
        self.nc = nc
        self.q = {e: [] for e in ENGS}
        self.lastw = {e: {} for e in ENGS}
        self.R = {}
        self.W = {}
        self.sems = {}
        self._dsems = {}
        self._es = es

    def sem(self, name):
        s = Sem(self.nc, self._es, name)
        self.sems[name] = s
        return s

    def _wait(self, eng, sem, val):
        if val <= 0:
            return
        if self.lastw[eng].get(sem.name, 0) >= val:
            return
        self.lastw[eng][sem.name] = val
        self.q[eng].append(lambda e, s=sem.h, v=val: e.wait_ge(s, v))

    def op(self, eng, build, sem, reads=(), writes=(), amt=1, extra_waits=()):
        for b in reads:
            m = self.R.get(b)
            if m is not None:
                s, v, peng = m
                if peng != eng or ENFORCE_SAME_ENGINE:
                    self._wait(eng, s, v)
        for b in writes:
            m = self.R.get(b)
            if m is not None:
                s, v, peng = m
                if peng != eng or ENFORCE_SAME_ENGINE:
                    self._wait(eng, s, v)
            for (s, v, peng) in self.W.get(b, {}).values():
                if peng != eng or ENFORCE_SAME_ENGINE:
                    self._wait(eng, s, v)
        for (s, v) in extra_waits:
            self._wait(eng, s, v)
        sem.val += amt
        v = sem.val
        self.q[eng].append(lambda e, b=build, s=sem.h, a=amt: b(e).then_inc(s, a))
        for b in writes:
            self.R[b] = (sem, v, eng)
            self.W[b] = {}
        for b in reads:
            self.W.setdefault(b, {})[sem.name] = (sem, v, eng)
        return v

    def dma(self, eng, out, in_, sem, reads=(), writes=(), tp=False, extra_waits=(),
            skey=None):
        if skey is None:
            skey = writes[0] if writes else (reads[0] if reads else None)
        if skey is not None:
            s = self._dsems.get(skey)
            if s is None:
                s = self.sem("d_" + skey)
                self._dsems[skey] = s
            sem = s
        return self.op(
            eng,
            lambda e, o=out, i=in_, t=tp: e.dma_start(out=o, in_=i, transpose=t),
            sem, reads=reads, writes=writes, amt=16, extra_waits=extra_waits)

    def barrier(self, bsem):
        for e in ENGS:
            for s in self.sems.values():
                if s is not bsem:
                    self._wait(e, s, s.val)
            bsem.val += 1
            self.q[e].append(lambda eng, s=bsem.h: eng.sem_inc(s, 1))
        tot = bsem.val
        for e in ENGS:
            self._wait(e, bsem, tot)

    def replay(self):
        with self.nc.Block() as block:
            @block.sync
            def _(e):
                for f in self.q["sync"]:
                    f(e)

            @block.scalar
            def _(e):
                for f in self.q["scalar"]:
                    f(e)

            @block.vector
            def _(e):
                for f in self.q["vector"]:
                    f(e)

            @block.tensor
            def _(e):
                for f in self.q["tensor"]:
                    f(e)

            @block.gpsimd
            def _(e):
                for f in self.q["gpsimd"]:
                    f(e)


class SBAlloc:

    def __init__(self, nc, total_bytes):
        self.nc = nc
        self.total = total_bytes
        self.items = []
        self.regions = []

    def alloc(self, name, shape, dtype, phases=(0, 3)):
        per_part = int(np.prod(shape[1:])) * mybir.dt.size(dtype)
        per_part = (per_part + 63) // 64 * 64
        p0, p1 = phases
        taken = sorted(
            (s, e) for (s, e, q0, q1) in self.regions if not (q1 < p0 or q0 > p1))
        addr = 0
        for (s, e) in taken:
            if addr + per_part <= s:
                break
            addr = max(addr, e)
        if addr + per_part > self.total:
            raise MemoryError(f"SBUF overflow: {name} needs {per_part}B at {addr}")
        h = self.nc.virtual_sbuf_tensor(name, list(shape), dtype)
        self.items.append((h, addr))
        self.regions.append((addr, addr + per_part, p0, p1))
        return h

    def finalize(self, es):
        backing = es.enter_context(self.nc.sbuf_tensor("arena", [128, self.total], U8))
        base = self.nc.lookup_mloc(backing).addr
        for h, addr in self.items:
            self.nc._tensor(h.name, h.shape, h.dtype, type="SB")
            mloc = self.nc.lookup_mloc(h)
            mloc.allocated = not self.nc.target_bir_lowering
            mloc.addr = base + addr
        return backing


def build_program(debug=False, stop_after=3, no_prep=False):
    from contextlib import ExitStack
    es = ExitStack()
    nc = bass.Bass("TRN2", target_bir_lowering=False)

    x_in = nc.dram_tensor("x", [NTOK, D], F32, kind="ExternalInput").ap()
    slope = nc.dram_tensor("slope", [NH], F32, kind="ExternalInput").ap()
    w_qkv = nc.dram_tensor("w_qkv", [3 * D, D], F32, kind="ExternalInput").ap()
    w_gate = nc.dram_tensor("w_gate", [D, D], F32, kind="ExternalInput").ap()
    w_out = nc.dram_tensor("w_out", [D, D], F32, kind="ExternalInput").ap()
    norm_w = nc.dram_tensor("norm_w", [D], F32, kind="ExternalInput").ap()
    gate_w = nc.dram_tensor("gate_w", [NI, D], I32, kind="ExternalInput").ap()
    gate_sc = nc.dram_tensor("gate_sc", [NI], F32, kind="ExternalInput").ap()
    up_w = nc.dram_tensor("up_w", [NI, D], I32, kind="ExternalInput").ap()
    up_sc = nc.dram_tensor("up_sc", [NI], F32, kind="ExternalInput").ap()
    down_w = nc.dram_tensor("down_w", [D, NI], I32, kind="ExternalInput").ap()
    down_sc = nc.dram_tensor("down_sc", [D], F32, kind="ExternalInput").ap()
    c_ident = nc.dram_tensor("c_ident", [128, 128], F32, kind="ExternalInput").ap()
    c_ones_bf = nc.dram_tensor("c_ones_bf", [128, 1], F32, kind="ExternalInput").ap()
    c_ones1 = nc.dram_tensor("c_ones1", [1, 128], F32, kind="ExternalInput").ap()
    c_qdidx = nc.dram_tensor("c_qdidx", [128, 256], F32, kind="ExternalInput").ap()
    c_diffp = nc.dram_tensor("c_diffp", [256, 256], F32, kind="ExternalInput").ap()
    c_maskp = nc.dram_tensor("c_maskp", [256, 256], F32, kind="ExternalInput").ap()
    c_kdidx = nc.dram_tensor("c_kdidx", [256, 1], F32, kind="ExternalInput").ap()
    c_misc = nc.dram_tensor("c_misc", [128, 4], F32, kind="ExternalInput").ap()
    out_t = nc.dram_tensor("out", [NTOK, D], F32, kind="ExternalOutput").ap()
    spk = "ExternalOutput" if debug else "Internal"
    sp_xn = nc.dram_tensor("sp_xn", [NKT, 128, NTOK], BF16, kind=spk).ap()
    if debug:
        dbg_qkv = nc.dram_tensor("dbg_qkv", [128, 6144], BF16, kind="ExternalOutput").ap()
        dbg_G = nc.dram_tensor("dbg_G", [128, 2048], BF16, kind="ExternalOutput").ap()
        dbg_ssq = nc.dram_tensor("dbg_ssq", [1, 2048], F32, kind="ExternalOutput").ap()
    sp_ga = nc.dram_tensor("sp_ga", [NKT, 128, NTOK], BF16, kind=spk).ap()
    sp_wg = nc.dram_tensor("sp_wg", [NIT, 128, NKT * 128], BF16, kind=spk).ap()
    sp_wu = nc.dram_tensor("sp_wu", [NIT, 128, NKT * 128], BF16, kind=spk).ap()
    sp_wd = nc.dram_tensor("sp_wd", [NKT, 128, NIT * 128], BF16, kind=spk).ap()
    sp_wo = nc.dram_tensor("sp_wo", [NKT, 128, NKT * 128], BF16, kind=spk).ap()

    sb = SBAlloc(nc, 207 * 1024)
    A = sb.alloc
    g_ident = A("g_ident", [128, 128], F32, (0, 3))
    g_identb = A("g_identb", [128, 128], BF16, (0, 3))
    g_ones_f = A("g_ones_f", [128, 1], F32, (0, 3))
    g_ones_bf = A("g_ones_bf", [128, 1], BF16, (0, 3))
    g_ones1 = A("g_ones1", [1, 128], F32, (0, 3))
    g_qdidx = A("g_qdidx", [128, 256], F32, (0, 1))
    g_diffp = A("g_diffp", [128, 512], F32, (0, 1))
    g_maskp = A("g_maskp", [128, 512], F32, (0, 1))
    g_kdidx = A("g_kdidx", [128, 2], F32, (0, 1))
    g_srow = A("g_srow", [1, 16], F32, (0, 1))
    g_negs = A("g_negs", [128, 16], F32, (0, 1))
    g_nw = A("g_nw", [128, 16], F32, (0, 1))
    g_ssq = A("g_ssq", [1, 2048], F32, (0, 3))
    g_row = A("g_row", [1, 2048], F32, (2, 3))
    g_rstd = A("g_rstd", [128, 2048], F32, (2, 3))
    g_misc = A("g_misc", [128, 4], F32, (0, 3))
    g_gsc = A("g_gsc", [128, 64], F32, (0, 3))
    g_usc = A("g_usc", [128, 64], F32, (0, 3))
    g_dsc = A("g_dsc", [128, 16], F32, (0, 3))
    big = A("big", [128, 32768], BF16, (0, 3))
    p_x = A("p_x", [128, 4096], F32, (0, 0))
    p_sq = A("p_sq", [128, 2048], F32, (0, 0))
    p_xnt = A("p_xnt", [128, 4096], BF16, (0, 0))
    p_st = A("p_st", [128, 128], F32, (0, 0))
    p_scl = A("p_scl", [128, 512], F32, (0, 0))
    p_wland = A("p_wland", [128, 4096], F32, (1, 1))
    p_wT = A("p_wT", [128, 16384], BF16, (1, 1))
    p_qkv = A("p_qkv", [128, 8192], BF16, (1, 1))
    p_G = A("p_G", [128, 4096], BF16, (1, 1))
    p_GA = A("p_GA", [128, 2048], BF16, (1, 1))
    p_vtm = A("p_vtm", [128, 2048], BF16, (1, 1))
    p_kd = A("p_kd", [128, 2048], BF16, (1, 1))
    p_sb = A("p_sb", [128, 512], BF16, (1, 1))
    p_qd = A("p_qd", [128, 256], BF16, (1, 1))
    p_sqb = A("p_sqb", [128, 256], BF16, (1, 1))
    p_t1 = A("p_t1", [128, 256], F32, (1, 1))
    p_kv = A("p_kv", [128, 128], F32, (1, 1))
    p_kvbf = A("p_kvbf", [128, 256], BF16, (1, 1))
    p_ddt = A("p_ddt", [128, 512], F32, (1, 1))
    p_qdbc = A("p_qdbc", [128, 256], F32, (1, 1))
    p_kdt = A("p_kdt", [128, 2], F32, (1, 1))
    p_bd = A("p_bd", [128, 1], F32, (1, 1))
    p_iland = A("p_iland", [128, 2048], I32, (1, 1))
    p_pnat = A("p_pnat", [128, 2048], F32, (1, 1))
    p_ptr = A("p_ptr", [128, 2048], BF16, (1, 1))
    p3_xnc = A("p3_xnc", [128, 16384], BF16, (3, 3))
    p3_gw = A("p3_gw", [128, 8192], BF16, (3, 3))
    p3_gal = A("p3_gal", [128, 1024], BF16, (3, 3))
    p3_wg = A("p3_wg", [128, 4096], BF16, (3, 3))
    p3_wu = A("p3_wu", [128, 4096], BF16, (3, 3))
    p3_wd = A("p3_wd", [128, 8192], BF16, (3, 3))
    p3_wo = A("p3_wo", [128, 4096], BF16, (3, 3))
    p3_sg = A("p3_sg", [128, 1024], BF16, (3, 3))
    p3_s0 = A("p3_s0", [128, 1024], F32, (3, 3))
    p3_s1 = A("p3_s1", [128, 1024], F32, (3, 3))
    p3_xp = A("p3_xp", [128, 256], F32, (3, 3))
    p3_ob = A("p3_ob", [128, 256], F32, (3, 3))

    pb = [es.enter_context(nc.psum_tensor(f"pb{i}", [128, 512], F32)) for i in range(8)]

    em = Emitter(nc, es)
    S = em.sem
    s_bar = S("bar")
    s_ld = S("ld")
    s_sq = S("s_sq"); s_ln = S("s_ln"); s_xnt = S("s_xnt")
    s_tbl = S("s_tbl"); s_mmt = S("s_mmt")
    s_proj = S("s_proj"); s_pev = S("s_pev")
    s_kdd = S("s_kdd")
    s_S = S("s_S"); s_Sb = S("s_Sb"); s_O = S("s_O"); s_KV = S("s_KV")
    s_kvu = S("s_kvu"); s_kvb = S("s_kvb"); s_ssm = S("s_ssm"); s_ssa = S("s_ssa")
    s_act = S("s_act"); s_ga = S("s_ga")
    s_pcv = S("s_pcv"); s_ptp = S("s_ptp")
    s_rst = S("s_rst"); s_gw = S("s_gw")
    s_gps = S("s_gps"); s_ups = S("s_ups"); s_sg = S("s_sg"); s_h = S("s_h")
    s_at = S("s_at"); s_dn = S("s_dn"); s_s1 = S("s_s1")
    s_tp3 = S("s_tp3"); s_ob = S("s_ob")
    s_x = s_wld = s_pin = s_xnc = s_gal = s_wgld = s_wuld = s_wdld = None
    s_wold = s_xp = s_ow = s_xsp = s_gasp = s_xtp = s_psp = s_ktp = None
    s_wtp = s_wcst = None

    dma = em.dma
    op = em.op
    tgc = [0]
    TP_ROT = (6, 7, 2, 3)

    def emit_tp_group(srcs, dst_ap, srcbufs, dstbuf, bf=False, evac=None):
        bi = TP_ROT[tgc[0] % len(TP_ROT)]
        p = tgc[0] % 2
        tgc[0] += 1
        bank = pb[bi]
        bname = f"pb{bi}"
        bview = bank.ap().bitcast(BF16) if bf else bank.ap()
        ident = g_identb if bf else g_ident
        idb = "identb" if bf else "ident"
        n = len(srcs)

        def tps(e, srcs=srcs, bview=bview, ident=ident, n=n):
            for k in range(n):
                ins = e.transpose(out=bview[:, k * 128:(k + 1) * 128],
                                  in_=srcs[k], identity=ident[:, :])
            return ins
        op("tensor", tps, s_mmt, reads=list(srcbufs) + [idb], writes=[bname])
        if evac is None:
            evac = "vector" if p == 0 else "scalar"
        iv = bview[:, 0:n * 128]
        if evac == "vector":
            op("vector", lambda e, dst_ap=dst_ap, iv=iv: e.tensor_copy(
                out=dst_ap, in_=iv), s_ptp, reads=[bname], writes=[dstbuf])
        else:
            op("scalar", lambda e, dst_ap=dst_ap, iv=iv: e.copy(
                out=dst_ap, in_=iv), s_ptp, reads=[bname], writes=[dstbuf])

    dma("sync", g_ident[:, :], c_ident, s_ld, writes=["ident"], skey="_c")
    dma("sync", g_ones_f[:, :], c_ones_bf, s_ld, writes=["onesf"], skey="_c")
    dma("sync", g_ones1[0:1, :], c_ones1, s_ld, writes=["ones1"], skey="_c")
    dma("sync", g_qdidx[:, :], c_qdidx, s_ld, writes=["qdidx"], skey="_c")
    dma("sync", g_diffp[:, 0:256], c_diffp[0:128, :], s_ld, writes=["diffp"], skey="_c")
    dma("sync", g_diffp[:, 256:512], c_diffp[128:256, :], s_ld, writes=["diffp"], skey="_c")
    dma("sync", g_maskp[:, 0:256], c_maskp[0:128, :], s_ld, writes=["maskp"], skey="_c")
    dma("sync", g_maskp[:, 256:512], c_maskp[128:256, :], s_ld, writes=["maskp"], skey="_c")
    dma("sync", g_kdidx[:, 0:1], c_kdidx[0:128, :], s_ld, writes=["kdidx"], skey="_c")
    dma("sync", g_kdidx[:, 1:2], c_kdidx[128:256, :], s_ld, writes=["kdidx"], skey="_c")
    dma("sync", g_srow[0:1, :], slope[None, :], s_ld, writes=["srow"], skey="_c")
    dma("sync", g_misc[:, :], c_misc, s_ld, writes=["misc"], skey="_c")
    _cs = em._dsems["_c"]
    for _b in ("ident", "onesf", "ones1", "qdidx", "diffp", "maskp", "kdidx",
               "srow", "misc"):
        em.R[_b] = (_cs, _cs.val, "dma")
    op("vector", lambda e: e.tensor_copy(out=g_ones_bf[:, :], in_=g_ones_f[:, :]),
       s_tbl, reads=["onesf"], writes=["onesbf"])
    op("vector", lambda e: e.tensor_copy(out=g_identb[:, :], in_=g_ident[:, :]),
       s_tbl, reads=["ident"], writes=["identb"])

    op("vector", lambda e: e.memset(g_ssq[0:1, :], 0.0), s_tbl, writes=["ssq"])

    for tt in range(16):
        sl = (tt % 2) * 2048
        xb = f"xb{tt % 2}"
        st = p_st[:, tt * 8: tt * 8 + 8]
        stn = f"st{tt}"
        dma("sync", p_x[:, sl:sl + 2048], x_in[tt * 128:(tt + 1) * 128, :], s_x,
            writes=[xb])
        op("vector", lambda e, sl=sl, st=st: e.reduce_sum(
            out=st[:, 0:1], in_=p_x[:, sl:sl + 2048], axis=AX.X),
            s_ln, reads=[xb], writes=[stn])
        op("scalar", lambda e, sl=sl, st=st: e.activation(
            out=p_sq[:, :], in_=p_x[:, sl:sl + 2048], func=AF.Square,
            bias=0.0, scale=1.0),
            s_sq, reads=[xb], writes=["sqbuf"])
        op("vector", lambda e, st=st: e.reduce_sum(
            out=st[:, 1:2], in_=p_sq[:, :], axis=AX.X),
            s_ln, reads=["sqbuf"], writes=[stn + "q"])

        op("vector", lambda e, st=st: e.tensor_scalar_mul(
            out=st[:, 2:3], in0=st[:, 0:1], scalar1=1.0 / D),
            s_ln, reads=[stn], writes=[stn + "m"])
        op("vector", lambda e, st=st: e.tensor_scalar_mul(
            out=st[:, 3:4], in0=st[:, 1:2], scalar1=1.0 / D),
            s_ln, reads=[stn + "q"], writes=[stn + "m2"])
        op("vector", lambda e, st=st: e.scalar_tensor_tensor(
            out=st[:, 4:5], in0=st[:, 2:3], scalar=st[:, 2:3], in1=st[:, 3:4],
            op0=ALU.mult, op1=ALU.subtract),
            s_ln, reads=[stn + "m", stn + "m2"], writes=[stn + "v"])
        op("scalar", lambda e, st=st: e.activation(
            out=st[:, 5:6], in_=st[:, 4:5], func=AF.Sqrt,
            bias=g_misc[:, 0:1], scale=-1.0),
            s_sq, reads=[stn + "v", "misc"], writes=[stn + "s"])
        op("vector", lambda e, st=st: e.reciprocal(
            out=st[:, 6:7], in_=st[:, 5:6]),
            s_ln, reads=[stn + "s"], writes=[stn + "r"])
        op("vector", lambda e, st=st: e.tensor_scalar(
            out=st[:, 7:8], in0=st[:, 2:3], scalar1=st[:, 6:7], scalar2=-1.0,
            op0=ALU.mult, op1=ALU.mult),
            s_ln, reads=[stn + "m", stn + "r"], writes=[stn + "f"])
        xnb = f"xnt{tt % 2}"
        op("scalar", lambda e, sl=sl, st=st: e.activation(
            out=p_xnt[:, sl:sl + 2048], in_=p_x[:, sl:sl + 2048], func=AF.Identity,
            bias=st[:, 7:8], scale=st[:, 6:7]),
            s_xnt, reads=[xb, stn + "f", stn + "r"], writes=[xnb])
        big_r = big.ap().rearrange("p (dk c) -> p dk c", c=2048)
        for g in range(4):
            srcs = [p_xnt[:, sl + (4 * g + k) * 128: sl + (4 * g + k) * 128 + 128]
                    for k in range(4)]
            dst = big_r[:, 4 * g:4 * g + 4, tt * 128: tt * 128 + 128]
            emit_tp_group(srcs, dst, [xnb, "identb"],
                          f"xng{4 * g}", bf=True)
    for dk in range(16):
        dma("sync", sp_xn[dk], big[:, dk * 2048:(dk + 1) * 2048], s_xsp,
            reads=[f"xng{(dk // 4) * 4}"], writes=[f"spxn{dk}"], skey="spxn")

    op("vector", lambda e: e.tensor_scalar_mul(
        out=g_srow[0:1, :], in0=g_srow[0:1, :], scalar1=-1.0),
        s_tbl, reads=["srow"], writes=["srow"])
    op("tensor", lambda e: e.matmul(pb[6][:, 0:16], g_ones1[0:1, :],
                                    g_srow[0:1, 0:16], start=True, stop=True),
       s_mmt, reads=["ones1", "srow"], writes=["pb6"])
    op("vector", lambda e: e.tensor_copy(out=g_negs[:, :], in_=pb[6][:, 0:16]),
       s_tbl, reads=["pb6"], writes=["negs"])
    dma("sync", p_scl[0:64, 0:128], gate_sc.rearrange("(a b) -> a b", b=128), s_wld,
        writes=["scl"])
    dma("sync", p_scl[0:64, 128:256], up_sc.rearrange("(a b) -> a b", b=128), s_wld,
        writes=["scl"])
    dma("sync", p_scl[0:16, 256:384], down_sc.rearrange("(a b) -> a b", b=128), s_wld,
        writes=["scl"])
    dma("sync", p_scl[0:16, 384:512], norm_w.rearrange("(a b) -> a b", b=128), s_wld,
        writes=["scl"])

    def sc_tp(e):
        e.transpose(out=pb[6][:, 0:64], in_=p_scl[0:64, 0:128],
                    identity=g_ident[0:64, 0:64])
        e.transpose(out=pb[7][:, 0:64], in_=p_scl[0:64, 128:256],
                    identity=g_ident[0:64, 0:64])
        e.transpose(out=pb[6][:, 64:80], in_=p_scl[0:16, 256:384],
                    identity=g_ident[0:16, 0:16])
        return e.transpose(out=pb[7][:, 64:80], in_=p_scl[0:16, 384:512],
                           identity=g_ident[0:16, 0:16])
    op("tensor", sc_tp, s_mmt, reads=["scl", "ident"], writes=["pb6", "pb7"])

    def sc_ev(e):
        e.tensor_copy(out=g_gsc[:, :], in_=pb[6][:, 0:64])
        e.tensor_copy(out=g_usc[:, :], in_=pb[7][:, 0:64])
        e.tensor_copy(out=g_dsc[:, :], in_=pb[6][:, 64:80])
        return e.tensor_copy(out=g_nw[:, :], in_=pb[7][:, 64:80])
    op("vector", sc_ev, s_tbl, reads=["pb6", "pb7"], writes=["scales"])

    em.barrier(s_bar)

    if stop_after == 0:
        sb.finalize(es)
        em.replay()
        return nc, es

    prep_chunks = []
    for r in range(NIT):
        prep_chunks += [("wg", r, c) for c in range(2)]
    for r in range(NIT):
        prep_chunks += [("wu", r, c) for c in range(2)]
    for r in range(NKT):
        prep_chunks += [("wd", r, c) for c in range(8)]
    for r in range(NKT):
        prep_chunks += [("wo", r, c) for c in range(2)]
    pj = [0]

    def prep_stageA(j):
        kind, r, c = prep_chunks[j]
        par = j % 2
        srcw = {"wg": gate_w, "wu": up_w, "wd": down_w, "wo": w_out}[kind]
        ilb, pnb = f"il{par}", f"pn{par}"
        if kind == "wo":
            dma("gpsimd", p_pnat[:, par * 1024:(par + 1) * 1024],
                srcw[r * 128:(r + 1) * 128, c * 1024:(c + 1) * 1024], s_pin,
                writes=[pnb])
            return
        conv = {"wg": "vector", "wu": "scalar",
                "wd": ("vector", "scalar")[j % 2]}[kind]
        dma("gpsimd", p_iland[:, par * 1024:(par + 1) * 1024],
            srcw[r * 128:(r + 1) * 128, c * 1024:(c + 1) * 1024], s_pin,
            writes=[ilb])
        if conv == "scalar":
            op(conv, lambda e, par=par: e.activation(
                out=p_pnat[:, par * 1024:(par + 1) * 1024],
                in_=p_iland[:, par * 1024:(par + 1) * 1024],
                func=AF.Identity, bias=g_misc[:, 2:3], scale=1.0),
                s_pcv, reads=[ilb, "misc"], writes=[pnb])
        else:
            op(conv, lambda e, par=par: e.tensor_single_scalar(
                out=p_pnat[:, par * 1024:(par + 1) * 1024],
                in_=p_iland[:, par * 1024:(par + 1) * 1024],
                scalar=-128.0, op=ALU.add),
                s_pcv, reads=[ilb], writes=[pnb])

    def prep_stageB(j):
        kind, r, c = prep_chunks[j]
        par = j % 2
        dst = {"wg": sp_wg, "wu": sp_wu, "wd": sp_wd, "wo": sp_wo}[kind]
        pnb, ptb = f"pn{par}", f"pt{par}"
        for g in range(2):
            srcs = [p_pnat[:, par * 1024 + g * 512 + k * 128:
                           par * 1024 + g * 512 + (k + 1) * 128] for k in range(4)]
            emit_tp_group(srcs, p_ptr[:, par * 1024 + g * 512:
                                      par * 1024 + (g + 1) * 512], [pnb], ptb)
        dma("sync", dst[r][:, c * 1024:(c + 1) * 1024],
            p_ptr[:, par * 1024:(par + 1) * 1024],
            s_psp, reads=[ptb], writes=[f"sp_{kind}"], skey=ptb)

    def emit_prep_chunk():
        j = pj[0]
        if no_prep or j >= len(prep_chunks):
            pj[0] = len(prep_chunks)
            return
        pj[0] += 1
        if j == 0:
            prep_stageA(0)
        if j + 1 < len(prep_chunks):
            prep_stageA(j + 1)
        prep_stageB(j)

    def emit_head_weight_row(h, ri):
        hp = h % 2
        rows = [(w_qkv, h), (w_qkv, NH + h), (w_qkv, 2 * NH + h), (w_gate, h)]
        srcw, r = rows[ri]
        gi = h * 4 + ri
        par = gi % 2
        lb = f"wl{par}"
        dma("sync", p_wland[:, par * 2048:(par + 1) * 2048],
            srcw[r * 128:(r + 1) * 128, :], s_wld, writes=[lb])
        base = (hp * 4 + ri) * 2048
        for g in range(4):
            srcs = [p_wland[:, par * 2048 + g * 512 + k * 128:
                            par * 2048 + g * 512 + (k + 1) * 128]
                    for k in range(4)]
            emit_tp_group(srcs, p_wT[:, base + g * 512: base + (g + 1) * 512],
                          [lb], f"wT{hp}")

    def emit_head_weights(h):
        for ri in range(4):
            emit_head_weight_row(h, ri)

    def emit_tables(h):
        ng = g_negs[:, h:h + 1]
        op("scalar", lambda e: e.activation(
            out=p_qdbc[:, :], in_=g_qdidx[:, :], func=AF.Exp, bias=0.0, scale=ng),
            s_act, reads=["negs"], writes=["qdbc"])
        op("scalar", lambda e: e.activation(
            out=p_ddt[:, :], in_=g_diffp[:, :], func=AF.Exp, bias=0.0, scale=ng),
            s_act, reads=["negs"], writes=["ddt"])
        op("scalar", lambda e: e.activation(
            out=p_kdt[:, :], in_=g_kdidx[:, :], func=AF.Exp, bias=0.0, scale=ng),
            s_act, reads=["negs"], writes=["kdt"])
        op("scalar", lambda e: e.activation(
            out=p_bd[:, :], in_=ng, func=AF.Exp, bias=0.0, scale=256.0),
            s_act, reads=["negs"], writes=["bd"])
        op("vector", lambda e: e.tensor_tensor(
            out=p_ddt[:, :], in0=p_ddt[:, :], in1=g_maskp[:, :], op=ALU.mult),
            s_tbl, reads=["ddt"], writes=["ddt"])

    def emit_proj_group(h, gidx):
        hp = h % 2
        r, mc = divmod(gidx, 4)
        bank = gidx % 2
        pbuf = pb[bank]
        wtb = f"wT{hp}"
        base = (hp * 4 + r) * 2048

        if r == 2:
            def mmv(e, base=base, mc=mc, pbuf=pbuf):
                for smt in range(4):
                    mt = mc * 4 + smt
                    for dk in range(16):
                        ins = e.matmul(
                            pbuf[:, smt * 128:(smt + 1) * 128],
                            big[:, dk * 2048 + mt * 128: dk * 2048 + mt * 128 + 128],
                            p_wT[:, base + dk * 128: base + dk * 128 + 128],
                            start=(dk == 0), stop=(dk == 15))
                return ins
            op("tensor", mmv, s_proj, reads=[wtb, "xnres"], writes=[f"pb{bank}"])
            dst = p_vtm[:, mc * 512:(mc + 1) * 512]
            op("scalar", lambda e, dst=dst, pbuf=pbuf: e.activation(
                out=dst, in_=pbuf[:, :], func=AF.Silu, bias=0.0, scale=1.0),
                s_pev, reads=[f"pb{bank}"], writes=["vtm"])
            return

        def mm(e, base=base, mc=mc, pbuf=pbuf):
            for dk in range(16):
                ins = e.matmul(pbuf[:, :],
                               p_wT[:, base + dk * 128: base + dk * 128 + 128],
                               big[:, dk * 2048 + mc * 512: dk * 2048 + mc * 512 + 512],
                               start=(dk == 0), stop=(dk == 15))
            return ins
        op("tensor", mm, s_proj, reads=[wtb, "xnres"], writes=[f"pb{bank}"])
        if r < 2:
            dst = p_qkv[:, (hp * 2 + r) * 2048 + mc * 512:
                        (hp * 2 + r) * 2048 + mc * 512 + 512]
            op("scalar", lambda e, dst=dst, pbuf=pbuf: e.activation(
                out=dst, in_=pbuf[:, :], func=AF.Silu, bias=0.0, scale=1.0),
                s_pev, reads=[f"pb{bank}"], writes=[f"qkv{hp}"])
        else:
            dst = p_G[:, hp * 2048 + mc * 512: hp * 2048 + mc * 512 + 512]
            op("scalar", lambda e, dst=dst, pbuf=pbuf: e.activation(
                out=dst, in_=pbuf[:, :], func=AF.Sigmoid, bias=0.0, scale=1.0),
                s_pev, reads=[f"pb{bank}"], writes=[f"G{hp}"])

    def emit_ktm(h):
        hp = h % 2
        kbase = (hp * 2 + 1) * 2048
        for g in range(4):
            bi = TP_ROT[tgc[0] % len(TP_ROT)]
            tgc[0] += 1
            bank = pb[bi]
            bname = f"pb{bi}"
            bview = bank.ap().bitcast(BF16)

            def tps(e, g=g, bview=bview, kbase=kbase):
                for k in range(4):
                    tt = 4 * g + k
                    ins = e.transpose(
                        out=bview[:, k * 128:(k + 1) * 128],
                        in_=p_qkv[:, kbase + tt * 128: kbase + tt * 128 + 128],
                        identity=g_identb[:, :])
                return ins
            op("tensor", tps, s_mmt, reads=[f"qkv{hp}", "identb"], writes=[bname])
            for k in range(4):
                tt = 4 * g + k
                op("vector", lambda e, tt=tt, k=k, bview=bview: e.tensor_scalar_mul(
                    out=p_kd[:, tt * 128:(tt + 1) * 128],
                    in0=bview[:, k * 128:(k + 1) * 128],
                    scalar1=p_kdt[:, (tt % 2):(tt % 2) + 1]),
                    s_kdd, reads=[bname, "kdt"], writes=["kdbuf"])

    def emit_scan_init(h):
        def z(e):
            e.memset(p_kv[:, :], 0.0)
            return e.memset(p_kvbf[:, :], 0.0)
        op("vector", z, s_kvu, writes=["kv", "kvbf0", "kvbf1"])

    def emit_scan_block(h, b):
        hp = h % 2
        qbase = hp * 2 * 2048
        kbase = (hp * 2 + 1) * 2048
        m0 = b * BLK

        def mmS(e):
            e.matmul(pb[2][:, 0:256],
                     p_qkv[:, kbase + m0: kbase + m0 + 128],
                     p_qkv[:, qbase + m0: qbase + m0 + 256], start=True, stop=True)
            return e.matmul(pb[2][:, 256:512],
                            p_qkv[:, kbase + m0 + 128: kbase + m0 + 256],
                            p_qkv[:, qbase + m0: qbase + m0 + 256],
                            start=True, stop=True)
        op("tensor", mmS, s_S, reads=[f"qkv{hp}"], writes=["pb2"])

        def mmKV(e, b=b):
            e.matmul(pb[4][:, 0:128], p_kd[:, 2 * b * 128: 2 * b * 128 + 128],
                     p_vtm[:, 2 * b * 128: 2 * b * 128 + 128], start=True, stop=False)
            return e.matmul(pb[4][:, 0:128],
                            p_kd[:, (2 * b + 1) * 128: (2 * b + 1) * 128 + 128],
                            p_vtm[:, (2 * b + 1) * 128: (2 * b + 1) * 128 + 128],
                            start=False, stop=True)
        op("tensor", mmKV, s_KV, reads=["kdbuf", "vtm"], writes=["pb4"])

        for kc in range(2):
            op("vector", lambda e, kc=kc: e.tensor_tensor(
                out=p_sb[:, kc * 256:(kc + 1) * 256],
                in0=pb[2][:, kc * 256:(kc + 1) * 256],
                in1=p_ddt[:, kc * 256:(kc + 1) * 256], op=ALU.mult),
                s_Sb, reads=["pb2", "ddt"], writes=["sb"])
        op("vector", lambda e: e.tensor_tensor(
            out=p_qd[:, :], in0=p_qkv[:, qbase + m0: qbase + m0 + 256],
            in1=p_qdbc[:, :], op=ALU.mult),
            s_Sb, reads=[f"qkv{hp}", "qdbc"], writes=["qd"])

        kvslot = (b + 1) % 2

        def mmO(e, b=b, kvslot=kvslot):
            e.matmul(pb[3][:, 0:256], p_vtm[:, 2 * b * 128: 2 * b * 128 + 128],
                     p_sb[:, 0:256], start=True, stop=False)
            e.matmul(pb[3][:, 0:256],
                     p_vtm[:, (2 * b + 1) * 128: (2 * b + 1) * 128 + 128],
                     p_sb[:, 256:512], start=False, stop=False)
            return e.matmul(pb[3][:, 0:256],
                            p_kvbf[:, kvslot * 128: kvslot * 128 + 128],
                            p_qd[:, :], start=False, stop=True)
        op("tensor", mmO, s_O, reads=["vtm", "sb", "qd", f"kvbf{kvslot}"],
           writes=["pb3"])

        op("vector", lambda e: e.scalar_tensor_tensor(
            out=p_kv[:, :], in0=p_kv[:, :], scalar=p_bd[:, 0:1],
            in1=pb[4][:, 0:128], op0=ALU.mult, op1=ALU.add),
            s_kvu, reads=["pb4", "bd", "kv"], writes=["kv"])
        kvs = b % 2
        op("scalar", lambda e, kvs=kvs: e.copy(
            out=p_kvbf[:, kvs * 128: kvs * 128 + 128], in_=p_kv[:, :]),
            s_kvb, reads=["kv"], writes=[f"kvbf{kvs}"])

        op("scalar", lambda e: e.activation(
            out=p_sqb[:, :], in_=pb[3][:, 0:256], func=AF.Square, bias=0.0, scale=1.0),
            s_act, reads=["pb3"], writes=["sqb"])
        op("scalar", lambda e, h=h: e.activation(
            out=p_t1[:, :], in_=pb[3][:, 0:256], func=AF.Identity, bias=0.0,
            scale=g_nw[:, h:h + 1]),
            s_act, reads=["pb3", "scales"], writes=["t1"])
        op("tensor", lambda e: e.matmul(
            pb[5][0:1, 0:256], g_ones_bf[:, 0:1], p_sqb[:, :], start=True, stop=True),
            s_ssm, reads=["sqb", "onesbf"], writes=["pb5"])
        op("vector", lambda e, m0=m0: e.tensor_tensor(
            out=g_ssq[0:1, m0:m0 + 256], in0=g_ssq[0:1, m0:m0 + 256],
            in1=pb[5][0:1, 0:256], op=ALU.add),
            s_ssa, reads=["pb5", "ssq"], writes=["ssq"])
        op("vector", lambda e, m0=m0, hp=hp: e.tensor_tensor(
            out=p_GA[:, m0:m0 + 256], in0=p_t1[:, :],
            in1=p_G[:, hp * 2048 + m0: hp * 2048 + m0 + 256], op=ALU.mult),
            s_ga, reads=["t1", f"G{hp}"], writes=["GA"])

    def emit_ga_spill(h):
        dma("gpsimd", sp_ga[h], p_GA[:, :], s_gasp, reads=["GA"],
            writes=["spga"])

    em.R["xnres"] = (s_bar, s_bar.val, "none")
    emit_head_weights(0)
    for _ in range(8):
        emit_prep_chunk()
    for h in range(NH):
        scan_h = h - 1
        for sub in range(8):
            emit_proj_group(h, 2 * sub)
            emit_prep_chunk()
            emit_proj_group(h, 2 * sub + 1)
            emit_prep_chunk()
            if scan_h >= 0:
                emit_scan_block(scan_h, sub)
            if sub < 4 and h + 1 < NH:
                emit_head_weight_row(h + 1, sub)
            emit_prep_chunk()
        if scan_h >= 0:
            emit_ga_spill(scan_h)
        if debug and scan_h == 0:
            dma("gpsimd", dbg_qkv, p_qkv[:, 0:6144], s_gasp, reads=["qkv0"],
                writes=["dbgq"])
            dma("gpsimd", dbg_G, p_G[:, 0:2048], s_gasp, reads=["G0"],
                writes=["dbgg"])
        emit_tables(h)
        emit_ktm(h)
        emit_scan_init(h)
    for b in range(NBLK):
        emit_scan_block(NH - 1, b)
        emit_prep_chunk()
    emit_ga_spill(NH - 1)
    while pj[0] < len(prep_chunks):
        emit_prep_chunk()

    em.barrier(s_bar)

    if stop_after == 1:
        if debug:
            dma("gpsimd", dbg_ssq, g_ssq[0:1, :], s_gasp, reads=["ssq"],
                writes=["dbgs"])
            em.barrier(s_bar)
        sb.finalize(es)
        em.replay()
        return nc, es

    op("scalar", lambda e: e.activation(
        out=g_row[0:1, :], in_=g_ssq[0:1, :], func=AF.Sqrt,
        bias=g_misc[0:1, 1:2], scale=1.0 / D),
        s_rst, reads=["ssq", "misc"], writes=["row"])
    op("vector", lambda e: e.reciprocal(out=g_row[0:1, :], in_=g_row[0:1, :]),
       s_rst, reads=["row"], writes=["row"])
    for mc in range(NMC):
        op("tensor", lambda e, mc=mc: e.matmul(
            pb[6][:, :], g_ones1[0:1, :], g_row[0:1, mc * 512:(mc + 1) * 512],
            start=True, stop=True),
            s_mmt, reads=["row", "ones1"], writes=["pb6"])
        op("vector", lambda e, mc=mc: e.tensor_copy(
            out=g_rstd[:, mc * 512:(mc + 1) * 512], in_=pb[6][:, :]),
            s_rst, reads=["rstd", "pb6"], writes=["rstd"])


    def emit_xnc(mc):
        par = mc % 2
        for dk in range(16):
            dma("sync", p3_xnc[:, par * 8192 + dk * 512: par * 8192 + dk * 512 + 512],
                sp_xn[dk][:, mc * 512:(mc + 1) * 512], s_xnc,
                reads=[f"spxn{dk}"], writes=[f"xnc{par}"])

    def emit_p3_tail(mc, hb):
        hp2 = hb % 2
        op("vector", lambda e, hb=hb, hp2=hp2: e.tensor_scalar_mul(
            out=p3_s0[:, hp2 * 512:(hp2 + 1) * 512], in0=pb[4 + hp2][:, :],
            scalar1=g_dsc[:, hb:hb + 1]),
            s_s1, reads=[f"pb{4 + hp2}", "scales"], writes=[f"s0_{hp2}"])
        op("vector", lambda e, hp2=hp2: e.tensor_tensor(
            out=p3_s1[:, hp2 * 512:(hp2 + 1) * 512],
            in0=p3_s0[:, hp2 * 512:(hp2 + 1) * 512],
            in1=pb[2 + hp2][:, :], op=ALU.add),
            s_s1, reads=[f"s0_{hp2}", f"pb{2 + hp2}"], writes=[f"s1_{hp2}"])
        for j in range(4):
            jp = j % 2
            mt = mc * 4 + j
            dma("sync", p3_xp[:, jp * 128:(jp + 1) * 128],
                x_in[mt * 128:(mt + 1) * 128, hb * 128:(hb + 1) * 128], s_xp,
                writes=[f"xp{jp}"])
            op("tensor", lambda e, hp2=hp2, j=j, jp=jp: e.transpose(
                out=pb[6 + jp][:, 0:128],
                in_=p3_s1[:, hp2 * 512 + j * 128: hp2 * 512 + j * 128 + 128],
                identity=g_ident[:, :]),
                s_tp3, reads=[f"s1_{hp2}", "ident"], writes=[f"pb{6 + jp}"])
            op("vector", lambda e, jp=jp: e.tensor_tensor(
                out=p3_ob[:, jp * 128:(jp + 1) * 128], in0=pb[6 + jp][:, 0:128],
                in1=p3_xp[:, jp * 128:(jp + 1) * 128], op=ALU.add),
                s_ob, reads=[f"pb{6 + jp}", f"xp{jp}"], writes=[f"ob{jp}"])
            dma("sync", out_t[mt * 128:(mt + 1) * 128, hb * 128:(hb + 1) * 128],
                p3_ob[:, jp * 128:(jp + 1) * 128], s_ow,
                reads=[f"ob{jp}"], writes=["hbm_out"], skey=f"ob{jp}")

    emit_xnc(0)
    for mc in range(NMC):
        par = mc % 2
        xncb = f"xnc{par}"
        for ft in range(16):
            gp = ft % 2
            dma("scalar", p3_gal[:, gp * 512:(gp + 1) * 512],
                sp_ga[ft][:, mc * 512:(mc + 1) * 512], s_gal,
                reads=["spga"], writes=[f"gal{gp}"])
            op("vector", lambda e, ft=ft, gp=gp, mc=mc: e.tensor_tensor(
                out=p3_gw[:, ft * 512:(ft + 1) * 512],
                in0=p3_gal[:, gp * 512:(gp + 1) * 512],
                in1=g_rstd[:, mc * 512:(mc + 1) * 512], op=ALU.mult),
                s_gw, reads=[f"gal{gp}", "rstd"], writes=["gw"])
        for it in range(NIT):
            if it == 40 and mc + 1 < NMC:
                emit_xnc(mc + 1)
            wp = it % 2
            wgb, wub = f"wg{wp}", f"wu{wp}"
            dma("gpsimd", p3_wg[:, wp * 2048:(wp + 1) * 2048],
                sp_wg[it], s_wgld, writes=[wgb])
            dma("gpsimd", p3_wu[:, wp * 2048:(wp + 1) * 2048],
                sp_wu[it], s_wuld, writes=[wub])

            def mmg(e, wp=wp, par=par):
                for dk in range(16):
                    ins = e.matmul(pb[0][:, :],
                                   p3_wg[:, wp * 2048 + dk * 128: wp * 2048 + dk * 128 + 128],
                                   p3_xnc[:, par * 8192 + dk * 512: par * 8192 + dk * 512 + 512],
                                   start=(dk == 0), stop=(dk == 15))
                return ins
            op("tensor", mmg, s_gps, reads=[wgb, xncb], writes=["pb0"])

            def mmu(e, wp=wp, par=par):
                for dk in range(16):
                    ins = e.matmul(pb[1][:, :],
                                   p3_wu[:, wp * 2048 + dk * 128: wp * 2048 + dk * 128 + 128],
                                   p3_xnc[:, par * 8192 + dk * 512: par * 8192 + dk * 512 + 512],
                                   start=(dk == 0), stop=(dk == 15))
                return ins
            op("tensor", mmu, s_ups, reads=[wub, xncb], writes=["pb1"])
            sgp = it % 2
            op("scalar", lambda e, it=it, sgp=sgp: e.activation(
                out=p3_sg[:, sgp * 512:(sgp + 1) * 512], in_=pb[0][:, :],
                func=AF.Silu, bias=0.0, scale=g_gsc[:, it:it + 1]),
                s_sg, reads=["pb0", "scales"], writes=[f"sg{sgp}"])
            op("vector", lambda e, it=it, sgp=sgp: e.scalar_tensor_tensor(
                out=big[:, it * 512:(it + 1) * 512], in0=pb[1][:, :],
                scalar=g_usc[:, it:it + 1],
                in1=p3_sg[:, sgp * 512:(sgp + 1) * 512],
                op0=ALU.mult, op1=ALU.mult),
                s_h, reads=["pb1", f"sg{sgp}", "scales"], writes=["hres"])
        for hb in range(NKT):
            hp2 = hb % 2
            wob = f"wo{hp2}"
            dma("sync", p3_wo[:, hp2 * 2048:(hp2 + 1) * 2048],
                sp_wo[hb], s_wold, writes=[wob])

            def mmat(e, hp2=hp2):
                for ft in range(16):
                    ins = e.matmul(pb[2 + hp2][:, :],
                                   p3_wo[:, hp2 * 2048 + ft * 128: hp2 * 2048 + ft * 128 + 128],
                                   p3_gw[:, ft * 512:(ft + 1) * 512],
                                   start=(ft == 0), stop=(ft == 15))
                return ins
            op("tensor", mmat, s_at, reads=[wob, "gw"], writes=[f"pb{2 + hp2}"])
            for q in range(4):
                qp = (hb * 4 + q) % 4
                wdb = f"wd{qp}"
                dma("scalar", p3_wd[:, qp * 2048:(qp + 1) * 2048],
                    sp_wd[hb][:, q * 2048:(q + 1) * 2048],
                    s_wdld, writes=[wdb])

                def mmdn(e, hp2=hp2, q=q, qp=qp):
                    for j in range(16):
                        it = q * 16 + j
                        ins = e.matmul(pb[4 + hp2][:, :],
                                       p3_wd[:, qp * 2048 + j * 128: qp * 2048 + j * 128 + 128],
                                       big[:, it * 512:(it + 1) * 512],
                                       start=(it == 0), stop=(it == 63))
                    return ins
                op("tensor", mmdn, s_dn, reads=[wdb, "hres"], writes=[f"pb{4 + hp2}"])
            if hb > 0:
                emit_p3_tail(mc, hb - 1)
        emit_p3_tail(mc, NKT - 1)

    em.barrier(s_bar)

    sb.finalize(es)
    em.replay()
    return nc, es


_CONSTS = None


def _consts():
    global _CONSTS
    if _CONSTS is None:
        m = np.arange(256)
        k = np.arange(256)
        diffp = np.maximum(m[None, :] - k[:, None], 0).astype(np.float32)
        maskp = (m[None, :] >= k[:, None]).astype(np.float32)
        _CONSTS = {
            "c_ident": np.eye(128, dtype=np.float32),
            "c_ones_bf": np.ones((128, 1), dtype=np.float32),
            "c_ones1": np.ones((1, 128), dtype=np.float32),
            "c_qdidx": np.broadcast_to((m + 1).astype(np.float32), (128, 256)).copy(),
            "c_diffp": diffp,
            "c_maskp": maskp,
            "c_kdidx": (255 - k).astype(np.float32).reshape(256, 1),
            "c_misc": np.broadcast_to(
                np.array([1e-5, 1e-6, -128.0, 0.0], dtype=np.float32), (128, 4)
            ).copy(),
        }
    return _CONSTS


def kernel(L0_attn_hidden_states, L0_attn_slope_rate, L0_attn_w_qkv,
           L0_attn_w_gate, L0_attn_w_out, L0_attn_norm_w,
           L0_ffn_gate_w, L0_ffn_gate_scale, L0_ffn_up_w, L0_ffn_up_scale,
           L0_ffn_down_w, L0_ffn_down_scale):
    x = np.ascontiguousarray(np.asarray(L0_attn_hidden_states, dtype=np.float32))
    shared = {
        "slope": np.asarray(L0_attn_slope_rate, dtype=np.float32),
        "w_qkv": np.asarray(L0_attn_w_qkv, dtype=np.float32),
        "w_gate": np.asarray(L0_attn_w_gate, dtype=np.float32),
        "w_out": np.asarray(L0_attn_w_out, dtype=np.float32),
        "norm_w": np.asarray(L0_attn_norm_w, dtype=np.float32),
        "gate_w": np.asarray(L0_ffn_gate_w, dtype=np.int32),
        "gate_sc": np.asarray(L0_ffn_gate_scale, dtype=np.float32),
        "up_w": np.asarray(L0_ffn_up_w, dtype=np.int32),
        "up_sc": np.asarray(L0_ffn_up_scale, dtype=np.float32),
        "down_w": np.asarray(L0_ffn_down_w, dtype=np.int32),
        "down_sc": np.asarray(L0_ffn_down_scale, dtype=np.float32),
    }
    shared.update(_consts())
    nc, es = build_program()
    in_maps = [dict(shared, x=x[c]) for c in range(8)]
    try:
        res = run_bass_kernel_spmd(nc, in_maps, core_ids=list(range(8)))
    finally:
        es.close()
    return np.stack([np.asarray(r["out"]) for r in res.results], axis=0)


# version 13 (final)
# speedup vs baseline: 1.1900x; 1.0087x over previous
import numpy as np
import ml_dtypes
import concourse.bass as bass
import concourse.mybir as mybir
from concourse.bass_utils import run_bass_kernel_spmd

F32 = mybir.dt.float32
BF16 = mybir.dt.bfloat16
I32 = mybir.dt.int32
U8 = mybir.dt.uint8
AF = mybir.ActivationFunctionType
ALU = mybir.AluOpType
AX = mybir.AxisListType

D = 2048
NTOK = 2048
NH = 16
DH = 128
NI = 8192
NKT = D // 128
NIT = NI // 128
MC = 512
NMC = NTOK // MC
BLK = 256
NBLK = NTOK // BLK
ENGS = ("sync", "scalar", "vector", "tensor", "gpsimd")
ENFORCE_SAME_ENGINE = True


class Sem:
    __slots__ = ("h", "name", "val", "_cm")

    def __init__(self, nc, es, name):
        self.h = es.enter_context(nc.semaphore(name))
        self.name = name
        self.val = 0


class Emitter:
    def __init__(self, nc, es):
        self.nc = nc
        self.q = {e: [] for e in ENGS}
        self.lastw = {e: {} for e in ENGS}
        self.R = {}
        self.W = {}
        self.sems = {}
        self._dsems = {}
        self._es = es

    def sem(self, name):
        s = Sem(self.nc, self._es, name)
        self.sems[name] = s
        return s

    def _wait(self, eng, sem, val):
        if val <= 0:
            return
        if self.lastw[eng].get(sem.name, 0) >= val:
            return
        self.lastw[eng][sem.name] = val
        self.q[eng].append(lambda e, s=sem.h, v=val: e.wait_ge(s, v))

    def op(self, eng, build, sem, reads=(), writes=(), amt=1, extra_waits=()):
        for b in reads:
            m = self.R.get(b)
            if m is not None:
                s, v, peng = m
                if peng != eng or ENFORCE_SAME_ENGINE:
                    self._wait(eng, s, v)
        for b in writes:
            m = self.R.get(b)
            if m is not None:
                s, v, peng = m
                if peng != eng or ENFORCE_SAME_ENGINE:
                    self._wait(eng, s, v)
            for (s, v, peng) in self.W.get(b, {}).values():
                if peng != eng or ENFORCE_SAME_ENGINE:
                    self._wait(eng, s, v)
        for (s, v) in extra_waits:
            self._wait(eng, s, v)
        sem.val += amt
        v = sem.val
        self.q[eng].append(lambda e, b=build, s=sem.h, a=amt: b(e).then_inc(s, a))
        for b in writes:
            self.R[b] = (sem, v, eng)
            self.W[b] = {}
        for b in reads:
            self.W.setdefault(b, {})[sem.name] = (sem, v, eng)
        return v

    def dma(self, eng, out, in_, sem, reads=(), writes=(), tp=False, extra_waits=(),
            skey=None):
        if skey is None:
            skey = writes[0] if writes else (reads[0] if reads else None)
        if skey is not None:
            s = self._dsems.get(skey)
            if s is None:
                s = self.sem("d_" + skey)
                self._dsems[skey] = s
            sem = s
        return self.op(
            eng,
            lambda e, o=out, i=in_, t=tp: e.dma_start(out=o, in_=i, transpose=t),
            sem, reads=reads, writes=writes, amt=16, extra_waits=extra_waits)

    def barrier(self, bsem):
        for e in ENGS:
            for s in self.sems.values():
                if s is not bsem:
                    self._wait(e, s, s.val)
            bsem.val += 1
            self.q[e].append(lambda eng, s=bsem.h: eng.sem_inc(s, 1))
        tot = bsem.val
        for e in ENGS:
            self._wait(e, bsem, tot)

    def replay(self):
        with self.nc.Block() as block:
            @block.sync
            def _(e):
                for f in self.q["sync"]:
                    f(e)

            @block.scalar
            def _(e):
                for f in self.q["scalar"]:
                    f(e)

            @block.vector
            def _(e):
                for f in self.q["vector"]:
                    f(e)

            @block.tensor
            def _(e):
                for f in self.q["tensor"]:
                    f(e)

            @block.gpsimd
            def _(e):
                for f in self.q["gpsimd"]:
                    f(e)


class SBAlloc:

    def __init__(self, nc, total_bytes):
        self.nc = nc
        self.total = total_bytes
        self.items = []
        self.regions = []

    def alloc(self, name, shape, dtype, phases=(0, 3)):
        per_part = int(np.prod(shape[1:])) * mybir.dt.size(dtype)
        per_part = (per_part + 63) // 64 * 64
        p0, p1 = phases
        taken = sorted(
            (s, e) for (s, e, q0, q1) in self.regions if not (q1 < p0 or q0 > p1))
        addr = 0
        for (s, e) in taken:
            if addr + per_part <= s:
                break
            addr = max(addr, e)
        if addr + per_part > self.total:
            raise MemoryError(f"SBUF overflow: {name} needs {per_part}B at {addr}")
        h = self.nc.virtual_sbuf_tensor(name, list(shape), dtype)
        self.items.append((h, addr))
        self.regions.append((addr, addr + per_part, p0, p1))
        return h

    def finalize(self, es):
        backing = es.enter_context(self.nc.sbuf_tensor("arena", [128, self.total], U8))
        base = self.nc.lookup_mloc(backing).addr
        for h, addr in self.items:
            self.nc._tensor(h.name, h.shape, h.dtype, type="SB")
            mloc = self.nc.lookup_mloc(h)
            mloc.allocated = not self.nc.target_bir_lowering
            mloc.addr = base + addr
        return backing


def build_program(debug=False, stop_after=3, no_prep=False):
    from contextlib import ExitStack
    es = ExitStack()
    nc = bass.Bass("TRN2", target_bir_lowering=False)

    x_in = nc.dram_tensor("x", [NTOK, D], F32, kind="ExternalInput").ap()
    slope = nc.dram_tensor("slope", [NH], F32, kind="ExternalInput").ap()
    w_qkv = nc.dram_tensor("w_qkv", [3 * D, D], F32, kind="ExternalInput").ap()
    w_gate = nc.dram_tensor("w_gate", [D, D], F32, kind="ExternalInput").ap()
    w_out = nc.dram_tensor("w_out", [D, D], F32, kind="ExternalInput").ap()
    norm_w = nc.dram_tensor("norm_w", [D], F32, kind="ExternalInput").ap()
    gate_w = nc.dram_tensor("gate_w", [NI, D], I32, kind="ExternalInput").ap()
    gate_sc = nc.dram_tensor("gate_sc", [NI], F32, kind="ExternalInput").ap()
    up_w = nc.dram_tensor("up_w", [NI, D], I32, kind="ExternalInput").ap()
    up_sc = nc.dram_tensor("up_sc", [NI], F32, kind="ExternalInput").ap()
    down_w = nc.dram_tensor("down_w", [D, NI], I32, kind="ExternalInput").ap()
    down_sc = nc.dram_tensor("down_sc", [D], F32, kind="ExternalInput").ap()
    c_ident = nc.dram_tensor("c_ident", [128, 128], F32, kind="ExternalInput").ap()
    c_ones_bf = nc.dram_tensor("c_ones_bf", [128, 1], F32, kind="ExternalInput").ap()
    c_ones1 = nc.dram_tensor("c_ones1", [1, 128], F32, kind="ExternalInput").ap()
    c_qdidx = nc.dram_tensor("c_qdidx", [128, 256], F32, kind="ExternalInput").ap()
    c_diffp = nc.dram_tensor("c_diffp", [256, 256], F32, kind="ExternalInput").ap()
    c_maskp = nc.dram_tensor("c_maskp", [256, 256], F32, kind="ExternalInput").ap()
    c_kdidx = nc.dram_tensor("c_kdidx", [256, 1], F32, kind="ExternalInput").ap()
    c_misc = nc.dram_tensor("c_misc", [128, 4], F32, kind="ExternalInput").ap()
    out_t = nc.dram_tensor("out", [NTOK, D], F32, kind="ExternalOutput").ap()
    spk = "ExternalOutput" if debug else "Internal"
    sp_xn = nc.dram_tensor("sp_xn", [NKT, 128, NTOK], BF16, kind=spk).ap()
    if debug:
        dbg_qkv = nc.dram_tensor("dbg_qkv", [128, 6144], BF16, kind="ExternalOutput").ap()
        dbg_G = nc.dram_tensor("dbg_G", [128, 2048], BF16, kind="ExternalOutput").ap()
        dbg_ssq = nc.dram_tensor("dbg_ssq", [1, 2048], F32, kind="ExternalOutput").ap()
    sp_ga = nc.dram_tensor("sp_ga", [NKT, 128, NTOK], BF16, kind=spk).ap()
    sp_wg = nc.dram_tensor("sp_wg", [NIT, 128, NKT * 128], BF16, kind=spk).ap()
    sp_wu = nc.dram_tensor("sp_wu", [NIT, 128, NKT * 128], BF16, kind=spk).ap()
    sp_wd = nc.dram_tensor("sp_wd", [NKT, 128, NIT * 128], BF16, kind=spk).ap()
    sp_wo = nc.dram_tensor("sp_wo", [NKT, 128, NKT * 128], BF16, kind=spk).ap()

    sb = SBAlloc(nc, 207 * 1024)
    A = sb.alloc
    g_ident = A("g_ident", [128, 128], F32, (0, 3))
    g_identb = A("g_identb", [128, 128], BF16, (0, 3))
    g_ones_f = A("g_ones_f", [128, 1], F32, (0, 3))
    g_ones_bf = A("g_ones_bf", [128, 1], BF16, (0, 3))
    g_ones1 = A("g_ones1", [1, 128], F32, (0, 3))
    g_qdidx = A("g_qdidx", [128, 256], F32, (0, 1))
    g_diffp = A("g_diffp", [128, 512], F32, (0, 1))
    g_maskp = A("g_maskp", [128, 512], F32, (0, 1))
    g_kdidx = A("g_kdidx", [128, 2], F32, (0, 1))
    g_srow = A("g_srow", [1, 16], F32, (0, 1))
    g_negs = A("g_negs", [128, 16], F32, (0, 1))
    g_nw = A("g_nw", [128, 16], F32, (0, 1))
    g_ssq = A("g_ssq", [1, 2048], F32, (0, 3))
    g_row = A("g_row", [1, 2048], F32, (2, 3))
    g_rstd = A("g_rstd", [128, 2048], F32, (2, 3))
    g_misc = A("g_misc", [128, 4], F32, (0, 3))
    g_gsc = A("g_gsc", [128, 64], F32, (0, 3))
    g_usc = A("g_usc", [128, 64], F32, (0, 3))
    g_dsc = A("g_dsc", [128, 16], F32, (0, 3))
    big = A("big", [128, 32768], BF16, (0, 3))
    p_x = A("p_x", [128, 4096], F32, (0, 0))
    p_sq = A("p_sq", [128, 2048], F32, (0, 0))
    p_xnt = A("p_xnt", [128, 4096], BF16, (0, 0))
    p_st = A("p_st", [128, 128], F32, (0, 0))
    p_scl = A("p_scl", [128, 512], F32, (0, 0))
    p_wland = A("p_wland", [128, 4096], F32, (1, 1))
    p_wT = A("p_wT", [128, 16384], BF16, (1, 1))
    p_qkv = A("p_qkv", [128, 8192], BF16, (1, 1))
    p_G = A("p_G", [128, 4096], BF16, (1, 1))
    p_GA = A("p_GA", [128, 2048], BF16, (1, 1))
    p_vtm = A("p_vtm", [128, 2048], BF16, (1, 1))
    p_kd = A("p_kd", [128, 2048], BF16, (1, 1))
    p_sb = A("p_sb", [128, 512], BF16, (1, 1))
    p_qd = A("p_qd", [128, 256], BF16, (1, 1))
    p_sqb = A("p_sqb", [128, 256], BF16, (1, 1))
    p_t1 = A("p_t1", [128, 256], F32, (1, 1))
    p_kv = A("p_kv", [128, 128], F32, (1, 1))
    p_kvbf = A("p_kvbf", [128, 256], BF16, (1, 1))
    p_ddt = A("p_ddt", [128, 512], F32, (1, 1))
    p_qdbc = A("p_qdbc", [128, 256], F32, (1, 1))
    p_kdt = A("p_kdt", [128, 2], F32, (1, 1))
    p_bd = A("p_bd", [128, 1], F32, (1, 1))
    p_iland = A("p_iland", [128, 2048], I32, (1, 1))
    p_pnat = A("p_pnat", [128, 2048], F32, (1, 1))
    p_ptr = A("p_ptr", [128, 2048], BF16, (1, 1))
    p3_xnc = A("p3_xnc", [128, 16384], BF16, (3, 3))
    p3_gw = A("p3_gw", [128, 8192], BF16, (3, 3))
    p3_gal = A("p3_gal", [128, 1024], BF16, (3, 3))
    p3_wg = A("p3_wg", [128, 4096], BF16, (3, 3))
    p3_wu = A("p3_wu", [128, 4096], BF16, (3, 3))
    p3_wd = A("p3_wd", [128, 8192], BF16, (3, 3))
    p3_wo = A("p3_wo", [128, 4096], BF16, (3, 3))
    p3_sg = A("p3_sg", [128, 1024], BF16, (3, 3))
    p3_s0 = A("p3_s0", [128, 1024], F32, (3, 3))
    p3_s1 = A("p3_s1", [128, 1024], F32, (3, 3))
    p3_xp = A("p3_xp", [128, 256], F32, (3, 3))
    p3_ob = A("p3_ob", [128, 256], F32, (3, 3))

    pb = [es.enter_context(nc.psum_tensor(f"pb{i}", [128, 512], F32)) for i in range(8)]

    em = Emitter(nc, es)
    S = em.sem
    s_bar = S("bar")
    s_ld = S("ld")
    s_sq = S("s_sq"); s_ln = S("s_ln"); s_xnt = S("s_xnt")
    s_tbl = S("s_tbl"); s_mmt = S("s_mmt")
    s_proj = S("s_proj"); s_pev = S("s_pev")
    s_kdd = S("s_kdd")
    s_S = S("s_S"); s_Sb = S("s_Sb"); s_O = S("s_O"); s_KV = S("s_KV")
    s_kvu = S("s_kvu"); s_kvb = S("s_kvb"); s_ssm = S("s_ssm"); s_ssa = S("s_ssa")
    s_act = S("s_act"); s_ga = S("s_ga")
    s_pcv = S("s_pcv"); s_ptp = S("s_ptp")
    s_pcv2 = S("s_pcv2"); s_ptp2 = S("s_ptp2")
    s_rst = S("s_rst"); s_gw = S("s_gw")
    s_gps = S("s_gps"); s_ups = S("s_ups"); s_sg = S("s_sg"); s_h = S("s_h")
    s_at = S("s_at"); s_dn = S("s_dn"); s_s1 = S("s_s1")
    s_tp3 = S("s_tp3"); s_ob = S("s_ob")
    s_x = s_wld = s_pin = s_xnc = s_gal = s_wgld = s_wuld = s_wdld = None
    s_wold = s_xp = s_ow = s_xsp = s_gasp = s_xtp = s_psp = s_ktp = None
    s_wtp = s_wcst = None

    dma = em.dma
    op = em.op
    tgc = [0]
    TP_ROT = (6, 7, 2, 3)

    def emit_tp_group(srcs, dst_ap, srcbufs, dstbuf, bf=False, evac=None):
        bi = TP_ROT[tgc[0] % len(TP_ROT)]
        p = tgc[0] % 2
        tgc[0] += 1
        bank = pb[bi]
        bname = f"pb{bi}"
        bview = bank.ap().bitcast(BF16) if bf else bank.ap()
        ident = g_identb if bf else g_ident
        idb = "identb" if bf else "ident"
        n = len(srcs)

        def tps(e, srcs=srcs, bview=bview, ident=ident, n=n):
            for k in range(n):
                ins = e.transpose(out=bview[:, k * 128:(k + 1) * 128],
                                  in_=srcs[k], identity=ident[:, :])
            return ins
        op("tensor", tps, s_mmt, reads=list(srcbufs) + [idb], writes=[bname])
        if evac is None:
            evac = "vector" if p == 0 else "scalar"
        iv = bview[:, 0:n * 128]
        if evac == "vector":
            op("vector", lambda e, dst_ap=dst_ap, iv=iv: e.tensor_copy(
                out=dst_ap, in_=iv), s_ptp, reads=[bname], writes=[dstbuf])
        else:
            op("scalar", lambda e, dst_ap=dst_ap, iv=iv: e.copy(
                out=dst_ap, in_=iv), s_ptp2, reads=[bname], writes=[dstbuf])

    dma("sync", g_ident[:, :], c_ident, s_ld, writes=["ident"], skey="_c")
    dma("sync", g_ones_f[:, :], c_ones_bf, s_ld, writes=["onesf"], skey="_c")
    dma("sync", g_ones1[0:1, :], c_ones1, s_ld, writes=["ones1"], skey="_c")
    dma("sync", g_qdidx[:, :], c_qdidx, s_ld, writes=["qdidx"], skey="_c")
    dma("sync", g_diffp[:, 0:256], c_diffp[0:128, :], s_ld, writes=["diffp"], skey="_c")
    dma("sync", g_diffp[:, 256:512], c_diffp[128:256, :], s_ld, writes=["diffp"], skey="_c")
    dma("sync", g_maskp[:, 0:256], c_maskp[0:128, :], s_ld, writes=["maskp"], skey="_c")
    dma("sync", g_maskp[:, 256:512], c_maskp[128:256, :], s_ld, writes=["maskp"], skey="_c")
    dma("sync", g_kdidx[:, 0:1], c_kdidx[0:128, :], s_ld, writes=["kdidx"], skey="_c")
    dma("sync", g_kdidx[:, 1:2], c_kdidx[128:256, :], s_ld, writes=["kdidx"], skey="_c")
    dma("sync", g_srow[0:1, :], slope[None, :], s_ld, writes=["srow"], skey="_c")
    dma("sync", g_misc[:, :], c_misc, s_ld, writes=["misc"], skey="_c")
    _cs = em._dsems["_c"]
    for _b in ("ident", "onesf", "ones1", "qdidx", "diffp", "maskp", "kdidx",
               "srow", "misc"):
        em.R[_b] = (_cs, _cs.val, "dma")
    op("vector", lambda e: e.tensor_copy(out=g_ones_bf[:, :], in_=g_ones_f[:, :]),
       s_tbl, reads=["onesf"], writes=["onesbf"])
    op("vector", lambda e: e.tensor_copy(out=g_identb[:, :], in_=g_ident[:, :]),
       s_tbl, reads=["ident"], writes=["identb"])

    op("vector", lambda e: e.memset(g_ssq[0:1, :], 0.0), s_tbl, writes=["ssq"])

    for tt in range(16):
        sl = (tt % 2) * 2048
        xb = f"xb{tt % 2}"
        st = p_st[:, tt * 8: tt * 8 + 8]
        stn = f"st{tt}"
        dma("sync", p_x[:, sl:sl + 2048], x_in[tt * 128:(tt + 1) * 128, :], s_x,
            writes=[xb])
        op("vector", lambda e, sl=sl, st=st: e.reduce_sum(
            out=st[:, 0:1], in_=p_x[:, sl:sl + 2048], axis=AX.X),
            s_ln, reads=[xb], writes=[stn])
        op("scalar", lambda e, sl=sl, st=st: e.activation(
            out=p_sq[:, :], in_=p_x[:, sl:sl + 2048], func=AF.Square,
            bias=0.0, scale=1.0),
            s_sq, reads=[xb], writes=["sqbuf"])
        op("vector", lambda e, st=st: e.reduce_sum(
            out=st[:, 1:2], in_=p_sq[:, :], axis=AX.X),
            s_ln, reads=["sqbuf"], writes=[stn + "q"])

        op("vector", lambda e, st=st: e.tensor_scalar_mul(
            out=st[:, 2:3], in0=st[:, 0:1], scalar1=1.0 / D),
            s_ln, reads=[stn], writes=[stn + "m"])
        op("vector", lambda e, st=st: e.tensor_scalar_mul(
            out=st[:, 3:4], in0=st[:, 1:2], scalar1=1.0 / D),
            s_ln, reads=[stn + "q"], writes=[stn + "m2"])
        op("vector", lambda e, st=st: e.scalar_tensor_tensor(
            out=st[:, 4:5], in0=st[:, 2:3], scalar=st[:, 2:3], in1=st[:, 3:4],
            op0=ALU.mult, op1=ALU.subtract),
            s_ln, reads=[stn + "m", stn + "m2"], writes=[stn + "v"])
        op("scalar", lambda e, st=st: e.activation(
            out=st[:, 5:6], in_=st[:, 4:5], func=AF.Sqrt,
            bias=g_misc[:, 0:1], scale=-1.0),
            s_sq, reads=[stn + "v", "misc"], writes=[stn + "s"])
        op("vector", lambda e, st=st: e.reciprocal(
            out=st[:, 6:7], in_=st[:, 5:6]),
            s_ln, reads=[stn + "s"], writes=[stn + "r"])
        op("vector", lambda e, st=st: e.tensor_scalar(
            out=st[:, 7:8], in0=st[:, 2:3], scalar1=st[:, 6:7], scalar2=-1.0,
            op0=ALU.mult, op1=ALU.mult),
            s_ln, reads=[stn + "m", stn + "r"], writes=[stn + "f"])
        xnb = f"xnt{tt % 2}"
        op("scalar", lambda e, sl=sl, st=st: e.activation(
            out=p_xnt[:, sl:sl + 2048], in_=p_x[:, sl:sl + 2048], func=AF.Identity,
            bias=st[:, 7:8], scale=st[:, 6:7]),
            s_xnt, reads=[xb, stn + "f", stn + "r"], writes=[xnb])
        big_r = big.ap().rearrange("p (dk c) -> p dk c", c=2048)
        for g in range(4):
            srcs = [p_xnt[:, sl + (4 * g + k) * 128: sl + (4 * g + k) * 128 + 128]
                    for k in range(4)]
            dst = big_r[:, 4 * g:4 * g + 4, tt * 128: tt * 128 + 128]
            emit_tp_group(srcs, dst, [xnb, "identb"],
                          f"xng{4 * g}", bf=True)
    for dk in range(16):
        dma("sync", sp_xn[dk], big[:, dk * 2048:(dk + 1) * 2048], s_xsp,
            reads=[f"xng{(dk // 4) * 4}"], writes=[f"spxn{dk}"], skey="spxn")

    op("vector", lambda e: e.tensor_scalar_mul(
        out=g_srow[0:1, :], in0=g_srow[0:1, :], scalar1=-1.0),
        s_tbl, reads=["srow"], writes=["srow"])
    op("tensor", lambda e: e.matmul(pb[6][:, 0:16], g_ones1[0:1, :],
                                    g_srow[0:1, 0:16], start=True, stop=True),
       s_mmt, reads=["ones1", "srow"], writes=["pb6"])
    op("vector", lambda e: e.tensor_copy(out=g_negs[:, :], in_=pb[6][:, 0:16]),
       s_tbl, reads=["pb6"], writes=["negs"])
    dma("sync", p_scl[0:64, 0:128], gate_sc.rearrange("(a b) -> a b", b=128), s_wld,
        writes=["scl"])
    dma("sync", p_scl[0:64, 128:256], up_sc.rearrange("(a b) -> a b", b=128), s_wld,
        writes=["scl"])
    dma("sync", p_scl[0:16, 256:384], down_sc.rearrange("(a b) -> a b", b=128), s_wld,
        writes=["scl"])
    dma("sync", p_scl[0:16, 384:512], norm_w.rearrange("(a b) -> a b", b=128), s_wld,
        writes=["scl"])

    def sc_tp(e):
        e.transpose(out=pb[6][:, 0:64], in_=p_scl[0:64, 0:128],
                    identity=g_ident[0:64, 0:64])
        e.transpose(out=pb[7][:, 0:64], in_=p_scl[0:64, 128:256],
                    identity=g_ident[0:64, 0:64])
        e.transpose(out=pb[6][:, 64:80], in_=p_scl[0:16, 256:384],
                    identity=g_ident[0:16, 0:16])
        return e.transpose(out=pb[7][:, 64:80], in_=p_scl[0:16, 384:512],
                           identity=g_ident[0:16, 0:16])
    op("tensor", sc_tp, s_mmt, reads=["scl", "ident"], writes=["pb6", "pb7"])

    def sc_ev(e):
        e.tensor_copy(out=g_gsc[:, :], in_=pb[6][:, 0:64])
        e.tensor_copy(out=g_usc[:, :], in_=pb[7][:, 0:64])
        e.tensor_copy(out=g_dsc[:, :], in_=pb[6][:, 64:80])
        return e.tensor_copy(out=g_nw[:, :], in_=pb[7][:, 64:80])
    op("vector", sc_ev, s_tbl, reads=["pb6", "pb7"], writes=["scales"])

    em.barrier(s_bar)

    if stop_after == 0:
        sb.finalize(es)
        em.replay()
        return nc, es

    prep_chunks = []
    for r in range(NIT):
        prep_chunks += [("wg", r, c) for c in range(2)]
    for r in range(NIT):
        prep_chunks += [("wu", r, c) for c in range(2)]
    for r in range(NKT):
        prep_chunks += [("wd", r, c) for c in range(8)]
    for r in range(NKT):
        prep_chunks += [("wo", r, c) for c in range(2)]
    pj = [0]

    def prep_stageA(j):
        kind, r, c = prep_chunks[j]
        par = j % 2
        srcw = {"wg": gate_w, "wu": up_w, "wd": down_w, "wo": w_out}[kind]
        ilb, pnb = f"il{par}", f"pn{par}"
        if kind == "wo":
            dma("gpsimd", p_pnat[:, par * 1024:(par + 1) * 1024],
                srcw[r * 128:(r + 1) * 128, c * 1024:(c + 1) * 1024], s_pin,
                writes=[pnb])
            return
        conv = {"wg": "vector", "wu": "scalar",
                "wd": ("vector", "scalar")[j % 2]}[kind]
        dma("gpsimd", p_iland[:, par * 1024:(par + 1) * 1024],
            srcw[r * 128:(r + 1) * 128, c * 1024:(c + 1) * 1024], s_pin,
            writes=[ilb])
        if conv == "scalar":
            op(conv, lambda e, par=par: e.activation(
                out=p_pnat[:, par * 1024:(par + 1) * 1024],
                in_=p_iland[:, par * 1024:(par + 1) * 1024],
                func=AF.Identity, bias=g_misc[:, 2:3], scale=1.0),
                s_pcv2, reads=[ilb, "misc"], writes=[pnb])
        else:
            op(conv, lambda e, par=par: e.tensor_single_scalar(
                out=p_pnat[:, par * 1024:(par + 1) * 1024],
                in_=p_iland[:, par * 1024:(par + 1) * 1024],
                scalar=-128.0, op=ALU.add),
                s_pcv, reads=[ilb], writes=[pnb])

    def prep_stageB(j):
        kind, r, c = prep_chunks[j]
        par = j % 2
        dst = {"wg": sp_wg, "wu": sp_wu, "wd": sp_wd, "wo": sp_wo}[kind]
        pnb, ptb = f"pn{par}", f"pt{par}"
        for g in range(2):
            srcs = [p_pnat[:, par * 1024 + g * 512 + k * 128:
                           par * 1024 + g * 512 + (k + 1) * 128] for k in range(4)]
            emit_tp_group(srcs, p_ptr[:, par * 1024 + g * 512:
                                      par * 1024 + (g + 1) * 512], [pnb], ptb)
        dma("sync", dst[r][:, c * 1024:(c + 1) * 1024],
            p_ptr[:, par * 1024:(par + 1) * 1024],
            s_psp, reads=[ptb], writes=[f"sp_{kind}"], skey=ptb)

    def emit_prep_chunk():
        j = pj[0]
        if no_prep or j >= len(prep_chunks):
            pj[0] = len(prep_chunks)
            return
        pj[0] += 1
        if j == 0:
            prep_stageA(0)
        if j + 1 < len(prep_chunks):
            prep_stageA(j + 1)
        prep_stageB(j)

    def emit_head_weight_row(h, ri):
        hp = h % 2
        rows = [(w_qkv, h), (w_qkv, NH + h), (w_qkv, 2 * NH + h), (w_gate, h)]
        srcw, r = rows[ri]
        gi = h * 4 + ri
        par = gi % 2
        lb = f"wl{par}"
        dma("sync", p_wland[:, par * 2048:(par + 1) * 2048],
            srcw[r * 128:(r + 1) * 128, :], s_wld, writes=[lb])
        base = (hp * 4 + ri) * 2048
        for g in range(4):
            srcs = [p_wland[:, par * 2048 + g * 512 + k * 128:
                            par * 2048 + g * 512 + (k + 1) * 128]
                    for k in range(4)]
            emit_tp_group(srcs, p_wT[:, base + g * 512: base + (g + 1) * 512],
                          [lb], f"wT{hp}")

    def emit_head_weights(h):
        for ri in range(4):
            emit_head_weight_row(h, ri)

    def emit_tables(h):
        ng = g_negs[:, h:h + 1]
        op("scalar", lambda e: e.activation(
            out=p_qdbc[:, :], in_=g_qdidx[:, :], func=AF.Exp, bias=0.0, scale=ng),
            s_act, reads=["negs"], writes=["qdbc"])
        op("scalar", lambda e: e.activation(
            out=p_ddt[:, :], in_=g_diffp[:, :], func=AF.Exp, bias=0.0, scale=ng),
            s_act, reads=["negs"], writes=["ddt"])
        op("scalar", lambda e: e.activation(
            out=p_kdt[:, :], in_=g_kdidx[:, :], func=AF.Exp, bias=0.0, scale=ng),
            s_act, reads=["negs"], writes=["kdt"])
        op("scalar", lambda e: e.activation(
            out=p_bd[:, :], in_=ng, func=AF.Exp, bias=0.0, scale=256.0),
            s_act, reads=["negs"], writes=["bd"])
        op("vector", lambda e: e.tensor_tensor(
            out=p_ddt[:, :], in0=p_ddt[:, :], in1=g_maskp[:, :], op=ALU.mult),
            s_tbl, reads=["ddt"], writes=["ddt"])

    def emit_proj_group(h, gidx):
        hp = h % 2
        r, mc = divmod(gidx, 4)
        bank = gidx % 2
        pbuf = pb[bank]
        wtb = f"wT{hp}"
        base = (hp * 4 + r) * 2048

        if r == 2:
            def mmv(e, base=base, mc=mc, pbuf=pbuf):
                for smt in range(4):
                    mt = mc * 4 + smt
                    for dk in range(16):
                        ins = e.matmul(
                            pbuf[:, smt * 128:(smt + 1) * 128],
                            big[:, dk * 2048 + mt * 128: dk * 2048 + mt * 128 + 128],
                            p_wT[:, base + dk * 128: base + dk * 128 + 128],
                            start=(dk == 0), stop=(dk == 15))
                return ins
            op("tensor", mmv, s_proj, reads=[wtb, "xnres"], writes=[f"pb{bank}"])
            dst = p_vtm[:, mc * 512:(mc + 1) * 512]
            op("scalar", lambda e, dst=dst, pbuf=pbuf: e.activation(
                out=dst, in_=pbuf[:, :], func=AF.Silu, bias=0.0, scale=1.0),
                s_pev, reads=[f"pb{bank}"], writes=["vtm"])
            return

        def mm(e, base=base, mc=mc, pbuf=pbuf):
            for dk in range(16):
                ins = e.matmul(pbuf[:, :],
                               p_wT[:, base + dk * 128: base + dk * 128 + 128],
                               big[:, dk * 2048 + mc * 512: dk * 2048 + mc * 512 + 512],
                               start=(dk == 0), stop=(dk == 15))
            return ins
        op("tensor", mm, s_proj, reads=[wtb, "xnres"], writes=[f"pb{bank}"])
        if r < 2:
            dst = p_qkv[:, (hp * 2 + r) * 2048 + mc * 512:
                        (hp * 2 + r) * 2048 + mc * 512 + 512]
            op("scalar", lambda e, dst=dst, pbuf=pbuf: e.activation(
                out=dst, in_=pbuf[:, :], func=AF.Silu, bias=0.0, scale=1.0),
                s_pev, reads=[f"pb{bank}"], writes=[f"qkv{hp}"])
        else:
            dst = p_G[:, hp * 2048 + mc * 512: hp * 2048 + mc * 512 + 512]
            op("scalar", lambda e, dst=dst, pbuf=pbuf: e.activation(
                out=dst, in_=pbuf[:, :], func=AF.Sigmoid, bias=0.0, scale=1.0),
                s_pev, reads=[f"pb{bank}"], writes=[f"G{hp}"])

    def emit_ktm(h):
        hp = h % 2
        kbase = (hp * 2 + 1) * 2048
        for g in range(4):
            bi = TP_ROT[tgc[0] % len(TP_ROT)]
            tgc[0] += 1
            bank = pb[bi]
            bname = f"pb{bi}"
            bview = bank.ap().bitcast(BF16)

            def tps(e, g=g, bview=bview, kbase=kbase):
                for k in range(4):
                    tt = 4 * g + k
                    ins = e.transpose(
                        out=bview[:, k * 128:(k + 1) * 128],
                        in_=p_qkv[:, kbase + tt * 128: kbase + tt * 128 + 128],
                        identity=g_identb[:, :])
                return ins
            op("tensor", tps, s_mmt, reads=[f"qkv{hp}", "identb"], writes=[bname])
            for k in range(4):
                tt = 4 * g + k
                op("vector", lambda e, tt=tt, k=k, bview=bview: e.tensor_scalar_mul(
                    out=p_kd[:, tt * 128:(tt + 1) * 128],
                    in0=bview[:, k * 128:(k + 1) * 128],
                    scalar1=p_kdt[:, (tt % 2):(tt % 2) + 1]),
                    s_kdd, reads=[bname, "kdt"], writes=["kdbuf"])

    def emit_scan_init(h):
        def z(e):
            e.memset(p_kv[:, :], 0.0)
            return e.memset(p_kvbf[:, :], 0.0)
        op("vector", z, s_kvu, writes=["kv", "kvbf0", "kvbf1"])

    def emit_scan_block(h, b):
        hp = h % 2
        qbase = hp * 2 * 2048
        kbase = (hp * 2 + 1) * 2048
        m0 = b * BLK

        def mmS(e):
            e.matmul(pb[2][:, 0:256],
                     p_qkv[:, kbase + m0: kbase + m0 + 128],
                     p_qkv[:, qbase + m0: qbase + m0 + 256], start=True, stop=True)
            return e.matmul(pb[2][:, 256:512],
                            p_qkv[:, kbase + m0 + 128: kbase + m0 + 256],
                            p_qkv[:, qbase + m0: qbase + m0 + 256],
                            start=True, stop=True)
        op("tensor", mmS, s_S, reads=[f"qkv{hp}"], writes=["pb2"])

        def mmKV(e, b=b):
            e.matmul(pb[4][:, 0:128], p_kd[:, 2 * b * 128: 2 * b * 128 + 128],
                     p_vtm[:, 2 * b * 128: 2 * b * 128 + 128], start=True, stop=False)
            return e.matmul(pb[4][:, 0:128],
                            p_kd[:, (2 * b + 1) * 128: (2 * b + 1) * 128 + 128],
                            p_vtm[:, (2 * b + 1) * 128: (2 * b + 1) * 128 + 128],
                            start=False, stop=True)
        op("tensor", mmKV, s_KV, reads=["kdbuf", "vtm"], writes=["pb4"])

        for kc in range(2):
            op("vector", lambda e, kc=kc: e.tensor_tensor(
                out=p_sb[:, kc * 256:(kc + 1) * 256],
                in0=pb[2][:, kc * 256:(kc + 1) * 256],
                in1=p_ddt[:, kc * 256:(kc + 1) * 256], op=ALU.mult),
                s_Sb, reads=["pb2", "ddt"], writes=["sb"])
        op("vector", lambda e: e.tensor_tensor(
            out=p_qd[:, :], in0=p_qkv[:, qbase + m0: qbase + m0 + 256],
            in1=p_qdbc[:, :], op=ALU.mult),
            s_Sb, reads=[f"qkv{hp}", "qdbc"], writes=["qd"])

        kvslot = (b + 1) % 2

        def mmO(e, b=b, kvslot=kvslot):
            e.matmul(pb[3][:, 0:256], p_vtm[:, 2 * b * 128: 2 * b * 128 + 128],
                     p_sb[:, 0:256], start=True, stop=False)
            e.matmul(pb[3][:, 0:256],
                     p_vtm[:, (2 * b + 1) * 128: (2 * b + 1) * 128 + 128],
                     p_sb[:, 256:512], start=False, stop=False)
            return e.matmul(pb[3][:, 0:256],
                            p_kvbf[:, kvslot * 128: kvslot * 128 + 128],
                            p_qd[:, :], start=False, stop=True)
        op("tensor", mmO, s_O, reads=["vtm", "sb", "qd", f"kvbf{kvslot}"],
           writes=["pb3"])

        op("vector", lambda e: e.scalar_tensor_tensor(
            out=p_kv[:, :], in0=p_kv[:, :], scalar=p_bd[:, 0:1],
            in1=pb[4][:, 0:128], op0=ALU.mult, op1=ALU.add),
            s_kvu, reads=["pb4", "bd", "kv"], writes=["kv"])
        kvs = b % 2
        op("scalar", lambda e, kvs=kvs: e.copy(
            out=p_kvbf[:, kvs * 128: kvs * 128 + 128], in_=p_kv[:, :]),
            s_kvb, reads=["kv"], writes=[f"kvbf{kvs}"])

        op("scalar", lambda e: e.activation(
            out=p_sqb[:, :], in_=pb[3][:, 0:256], func=AF.Square, bias=0.0, scale=1.0),
            s_act, reads=["pb3"], writes=["sqb"])
        op("scalar", lambda e, h=h: e.activation(
            out=p_t1[:, :], in_=pb[3][:, 0:256], func=AF.Identity, bias=0.0,
            scale=g_nw[:, h:h + 1]),
            s_act, reads=["pb3", "scales"], writes=["t1"])
        op("tensor", lambda e: e.matmul(
            pb[5][0:1, 0:256], g_ones_bf[:, 0:1], p_sqb[:, :], start=True, stop=True),
            s_ssm, reads=["sqb", "onesbf"], writes=["pb5"])
        op("vector", lambda e, m0=m0: e.tensor_tensor(
            out=g_ssq[0:1, m0:m0 + 256], in0=g_ssq[0:1, m0:m0 + 256],
            in1=pb[5][0:1, 0:256], op=ALU.add),
            s_ssa, reads=["pb5", "ssq"], writes=["ssq"])
        op("vector", lambda e, m0=m0, hp=hp: e.tensor_tensor(
            out=p_GA[:, m0:m0 + 256], in0=p_t1[:, :],
            in1=p_G[:, hp * 2048 + m0: hp * 2048 + m0 + 256], op=ALU.mult),
            s_ga, reads=["t1", f"G{hp}"], writes=["GA"])

    def emit_ga_spill(h):
        dma("gpsimd", sp_ga[h], p_GA[:, :], s_gasp, reads=["GA"],
            writes=["spga"])

    em.R["xnres"] = (s_bar, s_bar.val, "none")
    emit_head_weights(0)
    for _ in range(8):
        emit_prep_chunk()
    for h in range(NH):
        scan_h = h - 1
        for sub in range(8):
            emit_proj_group(h, 2 * sub)
            emit_prep_chunk()
            emit_proj_group(h, 2 * sub + 1)
            emit_prep_chunk()
            if scan_h >= 0:
                emit_scan_block(scan_h, sub)
            if sub < 4 and h + 1 < NH:
                emit_head_weight_row(h + 1, sub)
            emit_prep_chunk()
        if scan_h >= 0:
            emit_ga_spill(scan_h)
        if debug and scan_h == 0:
            dma("gpsimd", dbg_qkv, p_qkv[:, 0:6144], s_gasp, reads=["qkv0"],
                writes=["dbgq"])
            dma("gpsimd", dbg_G, p_G[:, 0:2048], s_gasp, reads=["G0"],
                writes=["dbgg"])
        emit_tables(h)
        emit_ktm(h)
        emit_scan_init(h)
    for b in range(NBLK):
        emit_scan_block(NH - 1, b)
        emit_prep_chunk()
    emit_ga_spill(NH - 1)
    while pj[0] < len(prep_chunks):
        emit_prep_chunk()

    em.barrier(s_bar)

    if stop_after == 1:
        if debug:
            dma("gpsimd", dbg_ssq, g_ssq[0:1, :], s_gasp, reads=["ssq"],
                writes=["dbgs"])
            em.barrier(s_bar)
        sb.finalize(es)
        em.replay()
        return nc, es

    op("scalar", lambda e: e.activation(
        out=g_row[0:1, :], in_=g_ssq[0:1, :], func=AF.Sqrt,
        bias=g_misc[0:1, 1:2], scale=1.0 / D),
        s_rst, reads=["ssq", "misc"], writes=["row"])
    op("vector", lambda e: e.reciprocal(out=g_row[0:1, :], in_=g_row[0:1, :]),
       s_rst, reads=["row"], writes=["row"])
    for mc in range(NMC):
        op("tensor", lambda e, mc=mc: e.matmul(
            pb[6][:, :], g_ones1[0:1, :], g_row[0:1, mc * 512:(mc + 1) * 512],
            start=True, stop=True),
            s_mmt, reads=["row", "ones1"], writes=["pb6"])
        op("vector", lambda e, mc=mc: e.tensor_copy(
            out=g_rstd[:, mc * 512:(mc + 1) * 512], in_=pb[6][:, :]),
            s_rst, reads=["rstd", "pb6"], writes=["rstd"])


    def emit_xnc(mc):
        par = mc % 2
        for dk in range(16):
            dma("sync", p3_xnc[:, par * 8192 + dk * 512: par * 8192 + dk * 512 + 512],
                sp_xn[dk][:, mc * 512:(mc + 1) * 512], s_xnc,
                reads=[f"spxn{dk}"], writes=[f"xnc{par}"])

    def emit_p3_tail(mc, hb):
        hp2 = hb % 2
        op("vector", lambda e, hb=hb, hp2=hp2: e.tensor_scalar_mul(
            out=p3_s0[:, hp2 * 512:(hp2 + 1) * 512], in0=pb[4 + hp2][:, :],
            scalar1=g_dsc[:, hb:hb + 1]),
            s_s1, reads=[f"pb{4 + hp2}", "scales"], writes=[f"s0_{hp2}"])
        op("vector", lambda e, hp2=hp2: e.tensor_tensor(
            out=p3_s1[:, hp2 * 512:(hp2 + 1) * 512],
            in0=p3_s0[:, hp2 * 512:(hp2 + 1) * 512],
            in1=pb[2 + hp2][:, :], op=ALU.add),
            s_s1, reads=[f"s0_{hp2}", f"pb{2 + hp2}"], writes=[f"s1_{hp2}"])
        for j in range(4):
            jp = j % 2
            mt = mc * 4 + j
            dma("sync", p3_xp[:, jp * 128:(jp + 1) * 128],
                x_in[mt * 128:(mt + 1) * 128, hb * 128:(hb + 1) * 128], s_xp,
                writes=[f"xp{jp}"])
            op("tensor", lambda e, hp2=hp2, j=j, jp=jp: e.transpose(
                out=pb[6 + jp][:, 0:128],
                in_=p3_s1[:, hp2 * 512 + j * 128: hp2 * 512 + j * 128 + 128],
                identity=g_ident[:, :]),
                s_tp3, reads=[f"s1_{hp2}", "ident"], writes=[f"pb{6 + jp}"])
            op("vector", lambda e, jp=jp: e.tensor_tensor(
                out=p3_ob[:, jp * 128:(jp + 1) * 128], in0=pb[6 + jp][:, 0:128],
                in1=p3_xp[:, jp * 128:(jp + 1) * 128], op=ALU.add),
                s_ob, reads=[f"pb{6 + jp}", f"xp{jp}"], writes=[f"ob{jp}"])
            dma("sync", out_t[mt * 128:(mt + 1) * 128, hb * 128:(hb + 1) * 128],
                p3_ob[:, jp * 128:(jp + 1) * 128], s_ow,
                reads=[f"ob{jp}"], writes=["hbm_out"], skey=f"ob{jp}")

    emit_xnc(0)
    for mc in range(NMC):
        par = mc % 2
        xncb = f"xnc{par}"
        for ft in range(16):
            gp = ft % 2
            dma("scalar", p3_gal[:, gp * 512:(gp + 1) * 512],
                sp_ga[ft][:, mc * 512:(mc + 1) * 512], s_gal,
                reads=["spga"], writes=[f"gal{gp}"])
            op("vector", lambda e, ft=ft, gp=gp, mc=mc: e.tensor_tensor(
                out=p3_gw[:, ft * 512:(ft + 1) * 512],
                in0=p3_gal[:, gp * 512:(gp + 1) * 512],
                in1=g_rstd[:, mc * 512:(mc + 1) * 512], op=ALU.mult),
                s_gw, reads=[f"gal{gp}", "rstd"], writes=["gw"])
        for it in range(NIT):
            if it == 40 and mc + 1 < NMC:
                emit_xnc(mc + 1)
            wp = it % 2
            wgb, wub = f"wg{wp}", f"wu{wp}"
            dma("gpsimd", p3_wg[:, wp * 2048:(wp + 1) * 2048],
                sp_wg[it], s_wgld, writes=[wgb])
            dma("gpsimd", p3_wu[:, wp * 2048:(wp + 1) * 2048],
                sp_wu[it], s_wuld, writes=[wub])

            def mmg(e, wp=wp, par=par):
                for dk in range(16):
                    ins = e.matmul(pb[0][:, :],
                                   p3_wg[:, wp * 2048 + dk * 128: wp * 2048 + dk * 128 + 128],
                                   p3_xnc[:, par * 8192 + dk * 512: par * 8192 + dk * 512 + 512],
                                   start=(dk == 0), stop=(dk == 15))
                return ins
            op("tensor", mmg, s_gps, reads=[wgb, xncb], writes=["pb0"])

            def mmu(e, wp=wp, par=par):
                for dk in range(16):
                    ins = e.matmul(pb[1][:, :],
                                   p3_wu[:, wp * 2048 + dk * 128: wp * 2048 + dk * 128 + 128],
                                   p3_xnc[:, par * 8192 + dk * 512: par * 8192 + dk * 512 + 512],
                                   start=(dk == 0), stop=(dk == 15))
                return ins
            op("tensor", mmu, s_ups, reads=[wub, xncb], writes=["pb1"])
            sgp = it % 2
            op("scalar", lambda e, it=it, sgp=sgp: e.activation(
                out=p3_sg[:, sgp * 512:(sgp + 1) * 512], in_=pb[0][:, :],
                func=AF.Silu, bias=0.0, scale=g_gsc[:, it:it + 1]),
                s_sg, reads=["pb0", "scales"], writes=[f"sg{sgp}"])
            op("vector", lambda e, it=it, sgp=sgp: e.scalar_tensor_tensor(
                out=big[:, it * 512:(it + 1) * 512], in0=pb[1][:, :],
                scalar=g_usc[:, it:it + 1],
                in1=p3_sg[:, sgp * 512:(sgp + 1) * 512],
                op0=ALU.mult, op1=ALU.mult),
                s_h, reads=["pb1", f"sg{sgp}", "scales"], writes=["hres"])
        for hb in range(NKT):
            hp2 = hb % 2
            wob = f"wo{hp2}"
            dma("sync", p3_wo[:, hp2 * 2048:(hp2 + 1) * 2048],
                sp_wo[hb], s_wold, writes=[wob])

            def mmat(e, hp2=hp2):
                for ft in range(16):
                    ins = e.matmul(pb[2 + hp2][:, :],
                                   p3_wo[:, hp2 * 2048 + ft * 128: hp2 * 2048 + ft * 128 + 128],
                                   p3_gw[:, ft * 512:(ft + 1) * 512],
                                   start=(ft == 0), stop=(ft == 15))
                return ins
            op("tensor", mmat, s_at, reads=[wob, "gw"], writes=[f"pb{2 + hp2}"])
            for q in range(4):
                qp = (hb * 4 + q) % 4
                wdb = f"wd{qp}"
                dma("scalar", p3_wd[:, qp * 2048:(qp + 1) * 2048],
                    sp_wd[hb][:, q * 2048:(q + 1) * 2048],
                    s_wdld, writes=[wdb])

                def mmdn(e, hp2=hp2, q=q, qp=qp):
                    for j in range(16):
                        it = q * 16 + j
                        ins = e.matmul(pb[4 + hp2][:, :],
                                       p3_wd[:, qp * 2048 + j * 128: qp * 2048 + j * 128 + 128],
                                       big[:, it * 512:(it + 1) * 512],
                                       start=(it == 0), stop=(it == 63))
                    return ins
                op("tensor", mmdn, s_dn, reads=[wdb, "hres"], writes=[f"pb{4 + hp2}"])
            if hb > 0:
                emit_p3_tail(mc, hb - 1)
        emit_p3_tail(mc, NKT - 1)

    em.barrier(s_bar)

    sb.finalize(es)
    em.replay()
    return nc, es


_CONSTS = None


def _consts():
    global _CONSTS
    if _CONSTS is None:
        m = np.arange(256)
        k = np.arange(256)
        diffp = np.maximum(m[None, :] - k[:, None], 0).astype(np.float32)
        maskp = (m[None, :] >= k[:, None]).astype(np.float32)
        _CONSTS = {
            "c_ident": np.eye(128, dtype=np.float32),
            "c_ones_bf": np.ones((128, 1), dtype=np.float32),
            "c_ones1": np.ones((1, 128), dtype=np.float32),
            "c_qdidx": np.broadcast_to((m + 1).astype(np.float32), (128, 256)).copy(),
            "c_diffp": diffp,
            "c_maskp": maskp,
            "c_kdidx": (255 - k).astype(np.float32).reshape(256, 1),
            "c_misc": np.broadcast_to(
                np.array([1e-5, 1e-6, -128.0, 0.0], dtype=np.float32), (128, 4)
            ).copy(),
        }
    return _CONSTS


def kernel(L0_attn_hidden_states, L0_attn_slope_rate, L0_attn_w_qkv,
           L0_attn_w_gate, L0_attn_w_out, L0_attn_norm_w,
           L0_ffn_gate_w, L0_ffn_gate_scale, L0_ffn_up_w, L0_ffn_up_scale,
           L0_ffn_down_w, L0_ffn_down_scale):
    x = np.ascontiguousarray(np.asarray(L0_attn_hidden_states, dtype=np.float32))
    shared = {
        "slope": np.asarray(L0_attn_slope_rate, dtype=np.float32),
        "w_qkv": np.asarray(L0_attn_w_qkv, dtype=np.float32),
        "w_gate": np.asarray(L0_attn_w_gate, dtype=np.float32),
        "w_out": np.asarray(L0_attn_w_out, dtype=np.float32),
        "norm_w": np.asarray(L0_attn_norm_w, dtype=np.float32),
        "gate_w": np.asarray(L0_ffn_gate_w, dtype=np.int32),
        "gate_sc": np.asarray(L0_ffn_gate_scale, dtype=np.float32),
        "up_w": np.asarray(L0_ffn_up_w, dtype=np.int32),
        "up_sc": np.asarray(L0_ffn_up_scale, dtype=np.float32),
        "down_w": np.asarray(L0_ffn_down_w, dtype=np.int32),
        "down_sc": np.asarray(L0_ffn_down_scale, dtype=np.float32),
    }
    shared.update(_consts())
    nc, es = build_program()
    in_maps = [dict(shared, x=x[c]) for c in range(8)]
    try:
        res = run_bass_kernel_spmd(nc, in_maps, core_ids=list(range(8)))
    finally:
        es.close()
    return np.stack([np.asarray(r["out"]) for r in res.results], axis=0)
